# version 38
# baseline (speedup 1.0000x reference)
import numpy as np
import ml_dtypes
import concourse.bass as bass
import concourse.mybir as mybir
from concourse.bass_utils import run_bass_kernel_spmd

BF16 = mybir.dt.bfloat16
F32 = mybir.dt.float32
I32 = mybir.dt.int32
F8 = mybir.dt.float8e4
AF = mybir.ActivationFunctionType
ALU = mybir.AluOpType
AX = mybir.AxisListType

MAGIC = 12582912.0
MAGIC64 = 6755399441055744.0
S, D, H, HD, INNER = 2048, 1024, 16, 64, 4096
NT = S // 128
ND = D // 128
NI = INNER // 128
EPS_LN = 1e-5
EPS_RMS = 1e-6

ENGINES = ("sync", "gpsimd", "tensor", "vector", "scalar")


class Ch:
    def __init__(self, sem):
        self.sem = sem
        self.n = 0


class Prog:
    def __init__(self):
        self.ops = {e: [] for e in ENGINES}

    def op(self, eng, meth, *args, inc=None, inck=1, **kw):
        if inc is not None:
            inc.n += inck
            sem, val = inc.sem, inck
        else:
            sem, val = None, 0
        self.ops[eng].append(("op", meth, args, kw, sem, val))

    def wait(self, eng, ch, val=None):
        v = ch.n if val is None else val
        if v > 0:
            self.ops[eng].append(("wait", ch.sem, v))

    def drain(self, eng):
        self.ops[eng].append(("drain",))

    def emit(self, eng_name, eng):
        for entry in self.ops[eng_name]:
            if entry[0] == "op":
                _, meth, args, kw, sem, val = entry
                r = getattr(eng, meth)(*args, **kw)
                if sem is not None:
                    r.then_inc(sem, val)
            elif entry[0] == "wait":
                eng.wait_ge(entry[1], entry[2])
            else:
                eng.drain()


DEBUG_TAPS = False


def build_program():
    nc = bass.Bass("TRN2", target_bir_lowering=False,
                   detect_race_conditions=False)

    dx = nc.dram_tensor("x", [S, D], F32, kind="ExternalInput").ap()
    dwq = nc.dram_tensor("wqt", [D, D], F32, kind="ExternalInput").ap()
    dwk = nc.dram_tensor("wkt", [D, D], F32, kind="ExternalInput").ap()
    dwv = nc.dram_tensor("wvt", [D, D], F32, kind="ExternalInput").ap()
    dwo = nc.dram_tensor("wot", [D, D], F32, kind="ExternalInput").ap()
    dbq = nc.dram_tensor("bq8", [128, ND], F32, kind="ExternalInput").ap()
    dbk = nc.dram_tensor("bk8", [128, ND], F32, kind="ExternalInput").ap()
    dbv = nc.dram_tensor("bvrow", [1, D], F32, kind="ExternalInput").ap()
    dbo = nc.dram_tensor("borow", [1, D], F32, kind="ExternalInput").ap()
    dnq8 = nc.dram_tensor("nqw8", [128, ND], F32, kind="ExternalInput").ap()
    dnk8 = nc.dram_tensor("nkw8", [128, ND], F32, kind="ExternalInput").ap()
    dnqh = nc.dram_tensor("nqwhd", [H, HD], F32, kind="ExternalInput").ap()
    dnkh = nc.dram_tensor("nkwhd", [H, HD], F32, kind="ExternalInput").ap()
    dwg = nc.dram_tensor("wgT", [D, INNER], F32, kind="ExternalInput").ap()
    dwu = nc.dram_tensor("wuT", [D, INNER], F32, kind="ExternalInput").ap()
    dwd = nc.dram_tensor("wdT", [INNER, D], F32, kind="ExternalInput").ap()
    didf = nc.dram_tensor("identf", [128, 128], F32, kind="ExternalInput").ap()
    dout = nc.dram_tensor("out", [S, D], F32, kind="ExternalOutput").ap()
    dscr_g = nc.dram_tensor("scr_g", [128, ND * INNER], F8, kind="Internal").ap()
    dscr_u = nc.dram_tensor("scr_u", [128, ND * INNER], F8, kind="Internal").ap()
    dscr_d = nc.dram_tensor("scr_d", [128, NI * D], F8, kind="Internal").ap()
    dtaps = {}
    if DEBUG_TAPS:
        for nm, shape, dt in (
                ("t_xn1T", [128, ND * S], F32), ("t_qsT", [128, ND * S], F32),
                ("t_ksT", [128, ND * S], F32), ("t_v65", [128, NT * H * 65], F32),
                ("t_negmb", [128, H], F32), ("t_oT", [128, ND * S], F32),
                ("t_xmid", [128, NT * D], F32), ("t_w8g", [128, ND * INNER], F32),
                ("t_w8d", [128, NI * D], F32),
                ("t_scl", [128, 16], F32), ("t_na1", [128, D], F32),
                ("t_hbuf", [128, INNER], F32), ("t_a2T", [128, INNER], F32),
                ("t_l2", [128, 320], F32), ("t_qh2s", [16, S], F32),
                ("t_a1", [128, D], F32), ("t_tm", [128, D], F32),
                ("t_n1", [128, D], I32)):
            dtaps[nm] = nc.dram_tensor(nm, shape, dt, kind="ExternalOutput").ap()

    P = Prog()
    semlist = []

    def mksem(name):
        cm = nc.semaphore(name)
        semlist.append(cm)
        return Ch(cm.__enter__())

    cL = mksem("cL")
    cG = mksem("cG")
    cT = mksem("cT")
    cV = mksem("cV")
    cS = mksem("cS")
    cO = mksem("cO")
    cB = mksem("cB")

    def barrier():
        base = cB.n
        cB.n += len(ENGINES)
        for e in ENGINES:
            P.ops[e].append(("op", "sem_inc", (cB.sem, 1), {}, None, 0))
            P.ops[e].append(("wait", cB.sem, base + len(ENGINES)))

    def tap(nm, src_ap, ch=None):
        if not DEBUG_TAPS:
            return
        if ch is not None:
            P.wait("gpsimd", ch)
        P.op("gpsimd", "dma_start", out=dtaps[nm], in_=src_ap, inc=cG,
             inck=16)

    def emm(out, lhsT, rhs, start, stop, inc=None):
        N = rhs.shape[-1]
        if N <= 512:
            P.op("tensor", "matmul", out, lhsT, rhs, start=start, stop=stop,
                 inc=inc)
            return
        for a in range(0, N, 512):
            b = min(a + 512, N)
            P.op("tensor", "matmul", out[:, a:b], lhsT, rhs[:, a:b],
                 start=start, stop=stop, inc=(inc if b == N else None))

    live = {}

    def alloc(name, shape, dtype, side):
        cm = nc.sbuf_tensor(name, shape, dtype, side=side)
        h = cm.__enter__()
        live[name] = cm
        return h

    def free(*names):
        for name in names:
            live.pop(name).__exit__(None, None, None)

    identf = alloc("c_identf", [128, 128], F32, "left")
    identb = alloc("c_identb", [128, 128], BF16, "left")
    bq8 = alloc("c_bq8", [128, ND], F32, "left")
    bk8 = alloc("c_bk8", [128, ND], F32, "left")
    nqw8 = alloc("c_nqw8", [128, ND], F32, "left")
    nkw8 = alloc("c_nkw8", [128, ND], F32, "left")
    borow = alloc("c_borow", [1, D], BF16, "left")
    magic = alloc("c_magic", [128, 1], F32, "left")
    onesb = alloc("c_onesb", [1, 128], BF16, "left")
    onesf = alloc("c_onesf", [1, 128], F32, "left")
    masks = alloc("c_masks", [128, 16 * ND], BF16, "left")
    onesfc = alloc("c_onesfc", [16, 1], F32, "left")
    negmb = alloc("c_negmb", [128, H], F32, "left")
    wmaxq = alloc("c_wmaxq", [H, 1], F32, "left")
    wmaxk = alloc("c_wmaxk", [H, 1], F32, "left")
    nmrow = alloc("c_nmrow", [1, H], F32, "left")
    sm16a = alloc("c_sm16a", [16, 1], F32, "left")
    sm16b = alloc("c_sm16b", [16, 1], F32, "left")
    sm16c = alloc("c_sm16c", [16, 1], F32, "left")
    wacc = alloc("c_wacc", [128, 96], F32, "left")
    swg_b = alloc("c_swg_b", [128, 1], F32, "left")
    mclg_b = alloc("c_mclg_b", [128, 1], F32, "left")
    swu_b = alloc("c_swu_b", [128, 1], F32, "left")
    mclu_b = alloc("c_mclu_b", [128, 1], F32, "left")
    swd_b = alloc("c_swd_b", [128, 1], F32, "left")
    mcld_b = alloc("c_mcld_b", [128, 1], F32, "left")
    mclud_b = alloc("c_mclud_b", [128, 1], F32, "left")
    sw1 = alloc("c_sw1", [1, 128], F32, "left")
    sw2 = alloc("c_sw2", [1, 8], F32, "left")
    colsum2 = alloc("c_colsum2", [128, 4], F32, "left")

    pcm = []

    def palloc(name, shape, dtype):
        cm = nc.psum_tensor(name, shape, dtype)
        pcm.append(cm)
        return cm.__enter__()

    plive = {}

    def pzalloc(name, shape, dtype):
        cm = nc.psum_tensor(name, shape, dtype)
        plive[name] = cm
        return cm.__enter__()

    def pzfree(*names):
        for name in names:
            plive.pop(name).__exit__(None, None, None)

    psA = palloc("psA", [128, 1024], F32)
    psB = palloc("psB", [128, 1024], F32)
    psC = palloc("psC", [128, 1024], F32)
    psT = pzalloc("psT", [128, 1024], BF16)
    psS = pzalloc("psS", [128, 512], F32)

    P.op("sync", "dma_start", out=identf[:], in_=didf, inc=cL, inck=16)
    P.op("sync", "dma_start", out=bq8[:], in_=dbq, inc=cL, inck=16)
    P.op("sync", "dma_start", out=bk8[:], in_=dbk, inc=cL, inck=16)
    P.op("sync", "dma_start", out=nqw8[:], in_=dnq8, inc=cL, inck=16)
    P.op("sync", "dma_start", out=nkw8[:], in_=dnk8, inc=cL, inck=16)
    P.op("gpsimd", "dma_start", out=identb[:], in_=didf, inc=cG, inck=16)
    P.op("gpsimd", "dma_start", out=borow[:], in_=dbo, inc=cG, inck=16)

    P.op("vector", "memset", magic[:], MAGIC)
    P.op("vector", "memset", onesb[:], 1.0)
    P.op("vector", "memset", onesf[:], 1.0)
    P.op("vector", "memset", masks[:], 0.0)
    P.drain("vector")
    P.op("vector", "memset", onesfc[:], 1.0)
    P.drain("vector")
    for c in range(ND):
        P.op("vector", "memset", masks[0:64, 16 * c + 2 * c: 16 * c + 2 * c + 1], 1.0)
        P.op("vector", "memset", masks[64:128, 16 * c + 2 * c + 1: 16 * c + 2 * c + 2], 1.0)
    P.drain("vector")
    P.op("vector", "sem_inc", cV.sem, 1)
    cV.n += 1
    memset_V = cV.n

    qsT = alloc("qsT", [128, ND * S], BF16, "right")
    ksT = alloc("ksT", [128, ND * S], BF16, "right")
    v65 = alloc("v65", [128, NT * H * 65], BF16, "right")
    WTa = alloc("WTa", [128, ND * D], BF16, "right")
    xn1T = alloc("xn1T", [128, ND * S], BF16, "right")
    xring = alloc("xring", [128, 2 * D], F32, "right")
    xnring = alloc("xnring", [128, 2 * D], BF16, "right")
    q2buf = alloc("q2buf", [128, S], BF16, "right")
    r_bc = alloc("r_bc", [128, S], F32, "right")
    qh2s = alloc("qh2s", [16, S], F32, "right")
    lns = alloc("lns", [128, 16 * NT], F32, "right")
    qhn = alloc("qhn", [H, S], F32, "right")
    nqwhd = alloc("s_nqwhd", [H, HD], F32, "right")
    nkwhd = alloc("s_nkwhd", [H, HD], F32, "right")
    bvrow = alloc("s_bvrow", [1, D], BF16, "right")
    wstage = alloc("wstage", [128, 2 * 1024], F32, "right")

    def lncol(t, j):
        return lns[:, 16 * t + j: 16 * t + j + 1]

    P.op("sync", "dma_start", out=nqwhd[:], in_=dnqh, inc=cL, inck=16)
    P.op("sync", "dma_start", out=nkwhd[:], in_=dnkh, inc=cL, inck=16)
    P.op("gpsimd", "dma_start", out=bvrow[:], in_=dbv, inc=cG, inck=16)
    consts_L = cL.n
    consts_G = cG.n

    P.wait("vector", cL, consts_L)
    P.op("vector", "reduce_max", wmaxq[:], nqwhd[:], axis=AX.X,
         apply_absolute_value=True)
    P.op("vector", "reduce_max", wmaxk[:], nkwhd[:], axis=AX.X,
         apply_absolute_value=True)
    P.drain("vector")
    P.op("vector", "sem_inc", cV.sem, 1)
    cV.n += 1

    pass1_srcs = []
    for tag, dr, rows, cols in (("g", dwg, D, INNER), ("u", dwu, D, INNER),
                                ("d", dwd, INNER, D)):
        for i in range(rows // 128):
            for j in range(cols // 1024):
                pass1_srcs.append(dr[128 * i:128 * (i + 1),
                                     1024 * j:1024 * (j + 1)])
    assert len(pass1_srcs) == 96
    p1_idx = [0]
    p1_marks = [0, 0]

    def emit_pass1(nchunks):
        for _ in range(nchunks):
            i = p1_idx[0]
            if i >= 96:
                return
            sl = i % 2
            if i >= 2:
                P.wait("sync", cV, p1_marks[sl])
            P.op("sync", "dma_start",
                 out=wstage[:, 1024 * sl: 1024 * (sl + 1)],
                 in_=pass1_srcs[i], inc=cL, inck=16)
            P.wait("vector", cL)
            P.op("vector", "reduce_sum", wacc[:, i: i + 1],
                 wstage[:, 1024 * sl: 1024 * (sl + 1)], axis=AX.X,
                 apply_absolute_value=True, inc=cV)
            p1_marks[sl] = cV.n
            p1_idx[0] += 1

    P.wait("gpsimd", cG, consts_G)
    for ic in range(ND):
        P.op("gpsimd", "dma_start", out=WTa[:, D * ic: D * (ic + 1)],
             in_=dwq[128 * ic: 128 * (ic + 1), :], inc=cG, inck=16)
    wq_loaded = cG.n
    for ic in range(ND):
        P.op("gpsimd", "dma_start", out=v65[:, D * ic: D * (ic + 1)],
             in_=dwk[128 * ic: 128 * (ic + 1), :], inc=cG, inck=16)
    wk_loaded = cG.n

    norm_marks = [0, 0]
    trans_marks = [0, 0]
    evacT_mark = 0
    for t in range(NT):
        sl = t % 2
        if t >= 2:
            P.wait("sync", cS, norm_marks[sl])
        P.op("sync", "dma_start", out=xring[:, D * sl: D * (sl + 1)],
             in_=dx[128 * t: 128 * (t + 1), :], inc=cL, inck=16)
        x_t = xring[:, D * sl: D * (sl + 1)]
        P.wait("vector", cL)
        P.op("vector", "reduce_sum", lncol(t, 0), x_t, axis=AX.X)
        P.op("vector", "scalar_tensor_tensor", r_bc[:, 0:D], x_t, 1.0, x_t,
             ALU.mult, ALU.mult, accum_out=lncol(t, 1))
        P.drain("vector")
        P.op("vector", "tensor_scalar_mul", lncol(t, 2), lncol(t, 0), 1.0 / D)
        P.drain("vector")
        P.op("vector", "tensor_mul", lncol(t, 3), lncol(t, 2), lncol(t, 2))
        P.drain("vector")
        P.op("vector", "scalar_tensor_tensor", lncol(t, 4), lncol(t, 1),
             1.0 / D, lncol(t, 3), ALU.mult, ALU.subtract)
        P.drain("vector")
        P.op("vector", "tensor_scalar_add", lncol(t, 5), lncol(t, 4), EPS_LN)
        P.drain("vector")
        P.op("vector", "reciprocal", lncol(t, 6), lncol(t, 5))
        P.drain("vector")
        P.op("vector", "sem_inc", cV.sem, 1)
        cV.n += 1
        P.wait("scalar", cV)
        P.op("scalar", "activation", lncol(t, 7), lncol(t, 6), AF.Sqrt)
        P.drain("scalar")
        P.op("scalar", "sem_inc", cS.sem, 1)
        cS.n += 1
        P.wait("vector", cS)
        P.op("vector", "scalar_tensor_tensor", lncol(t, 8), lncol(t, 2), -1.0,
             lncol(t, 7), ALU.mult, ALU.mult)
        P.drain("vector")
        P.op("vector", "sem_inc", cV.sem, 1)
        cV.n += 1
        P.wait("scalar", cV)
        if t >= 2:
            P.wait("scalar", cT, trans_marks[sl])
        P.op("scalar", "activation", xnring[:, D * sl: D * (sl + 1)], x_t,
             AF.Identity, scale=lncol(t, 7), bias=lncol(t, 8), inc=cS)
        norm_marks[sl] = cS.n
        P.wait("tensor", cS)
        if t >= 1:
            P.wait("tensor", cV, evacT_mark)
        for j in range(ND):
            P.op("tensor", "transpose", psT[:, 128 * j: 128 * (j + 1)],
                 xnring[:, D * sl + 128 * j: D * sl + 128 * (j + 1)],
                 identb[:], inc=(cT if j == ND - 1 else None))
        trans_marks[sl] = cT.n
        P.wait("vector", cT)
        src = psT[:, 0:1024].rearrange("p (c f) -> p c f", f=128)
        dst = xn1T[:, :].rearrange("p (c f) -> p c f", f=S)[:, :, 128 * t: 128 * (t + 1)]
        P.op("vector", "tensor_copy", dst, src, inc=cV)
        evacT_mark = cV.n
        emit_pass1(4)

    def qk_proj(which, w_loaded_mark, wt):
        tgt = qsT if which == "q" else ksT
        b8 = bq8 if which == "q" else bk8
        nw8 = nqw8 if which == "q" else nkw8
        wmax = wmaxq if which == "q" else wmaxk
        bound = sm16a if which == "q" else sm16b

        wt_free = [0]
        P.wait("tensor", cV)
        evac_marks = {}
        qh_marks = {0: 0, 1: 0}
        copy_mark = [0]

        def emit_qh2(oc2, tc2):
            P.wait("tensor", cS, evac_marks[(oc2, tc2)])
            P.wait("tensor", cV, copy_mark[0])
            emm(psC[0:16, :],
                masks[:, 16 * oc2: 16 * oc2 + 16],
                q2buf[:, 1024 * tc2: 1024 * (tc2 + 1)],
                start=True, stop=True, inc=cT)
            qh_marks[tc2] = cT.n
            P.wait("vector", cT, qh_marks[tc2])
            dstq = qh2s[:, 1024 * tc2: 1024 * (tc2 + 1)]
            if oc2 == 0:
                P.op("vector", "tensor_copy", dstq, psC[0:16, :], inc=cV)
            else:
                P.op("vector", "scalar_tensor_tensor", dstq, dstq, 1.0,
                     psC[0:16, :], ALU.mult, ALU.add, inc=cV)
            P.drain("vector")
            copy_mark[0] = cV.n

        prev = None
        for oc in range(ND):
            for tc in range(2):
                pp = psA if tc == 0 else psB
                P.wait("tensor", cG, w_loaded_mark)
                if (oc - 1, tc) in evac_marks:
                    P.wait("tensor", cS, evac_marks[(oc - 1, tc)])
                for ic in range(ND):
                    emm(pp[:, :],
                        wt[:, D * ic + 128 * oc: D * ic + 128 * (oc + 1)],
                        xn1T[:, S * ic + 1024 * tc: S * ic + 1024 * (tc + 1)],
                        start=(ic == 0), stop=(ic == ND - 1),
                        inc=(cT if ic == ND - 1 else None))
                if (oc, tc) == (ND - 1, 1):
                    wt_free[0] = cT.n
                P.wait("scalar", cT)
                if oc > 0:
                    P.wait("scalar", cT, qh_marks[tc])
                P.op("scalar", "activation",
                     tgt[:, S * oc + 1024 * tc: S * oc + 1024 * (tc + 1)],
                     pp[:, :], AF.Identity, bias=b8[:, oc: oc + 1])
                P.op("scalar", "activation",
                     q2buf[:, 1024 * tc: 1024 * (tc + 1)],
                     pp[:, :], AF.Square, bias=b8[:, oc: oc + 1], inc=cS)
                evac_marks[(oc, tc)] = cS.n
                if prev is not None:
                    emit_qh2(*prev)
                prev = (oc, tc)
        emit_qh2(*prev)
        qh2_done = copy_mark[0]
        P.wait("tensor", cV, qh2_done)
        P.wait("tensor", cS)
        for j in range(4):
            pq = psA if j < 2 else psB
            P.op("tensor", "matmul", pq[0:1, 512 * (j % 2): 512 * (j % 2) + 512],
                 onesfc[:], qh2s[:, 512 * j: 512 * (j + 1)],
                 start=True, stop=True, inc=(cT if j == 3 else None))
        P.wait("vector", cT)
        P.op("vector", "tensor_scalar_mul", qhn[0:1, 0:1024], psA[0:1, :], 1.0 / D)
        P.op("vector", "tensor_scalar_mul", qhn[0:1, 1024:2048], psB[0:1, :], 1.0 / D)
        P.drain("vector")
        P.op("vector", "tensor_scalar_add", r_bc[0:1, :], qhn[0:1, :], EPS_RMS)
        P.drain("vector")
        P.op("vector", "reciprocal", qhn[0:1, :], r_bc[0:1, :])
        P.drain("vector")
        P.op("vector", "sem_inc", cV.sem, 1)
        cV.n += 1
        P.wait("scalar", cV)
        P.op("scalar", "activation", qhn[0:1, :], qhn[0:1, :], AF.Sqrt,
             scale=(0.015625 if which == "q" else 1.0))
        P.drain("scalar")
        P.op("scalar", "sem_inc", cS.sem, 1)
        cS.n += 1
        P.wait("tensor", cS)
        for j in range(4):
            pq = psA if j < 2 else psB
            P.op("tensor", "matmul", pq[:, 512 * (j % 2): 512 * (j % 2) + 512],
                 onesf[:, 0:128], qhn[0:1, 512 * j: 512 * (j + 1)],
                 start=True, stop=True, inc=(cT if j == 3 else None))
        P.wait("vector", cT)
        P.op("vector", "tensor_copy", r_bc[:, 0:1024], psA[:, :])
        P.op("vector", "tensor_copy", r_bc[:, 1024:2048], psB[:, :], inc=cV)
        rbc_done = cV.n
        for oc in range(ND):
            P.op("vector", "scalar_tensor_tensor",
                 tgt[:, S * oc: S * (oc + 1)],
                 tgt[:, S * oc: S * (oc + 1)],
                 nw8[:, oc: oc + 1], r_bc[:, :], ALU.mult, ALU.mult,
                 inc=(cV if oc == ND - 1 else None))
        scale_done = cV.n
        P.wait("scalar", cV, qh2_done)
        P.wait("scalar", cT)
        P.op("scalar", "activation", qhn[:, :], qh2s[:, :], AF.Sqrt, inc=cS)
        P.wait("vector", cS)
        P.op("vector", "tensor_mul", qhn[:, :], qhn[:, :], r_bc[0:16, :])
        P.drain("vector")
        P.op("vector", "reduce_max", bound[:], qhn[:, :], axis=AX.X)
        P.drain("vector")
        P.op("vector", "tensor_mul", bound[:], bound[:], wmax[:])
        P.drain("vector")
        P.op("vector", "sem_inc", cV.sem, 1)
        cV.n += 1
        return scale_done, wt_free[0]

    P.wait("tensor", cV, evacT_mark)
    q_scale_done, wta_free = qk_proj("q", wq_loaded, WTa)
    emit_pass1(16)
    P.wait("gpsimd", cT, wta_free)
    for ic in range(ND):
        P.op("gpsimd", "dma_start", out=WTa[:, D * ic: D * (ic + 1)],
             in_=dwv[128 * ic: 128 * (ic + 1), :], inc=cG, inck=16)
    wv_loaded = cG.n
    k_scale_done, v65_free = qk_proj("k", wk_loaded, v65)
    emit_pass1(16)

    P.op("vector", "tensor_mul", sm16c[:], sm16a[:], sm16b[:])
    P.drain("vector")
    P.op("vector", "tensor_scalar_mul", sm16c[:], sm16c[:], -1.0)
    P.drain("vector")
    P.op("vector", "sem_inc", cV.sem, 1)
    cV.n += 1
    P.wait("tensor", cV)
    P.op("tensor", "transpose", psS[0:1, 0:16], sm16c[:],
         identf[0:16, 0:16], inc=cT)
    P.wait("vector", cT)
    P.op("vector", "tensor_copy", nmrow[:], psS[0:1, 0:16], inc=cV)
    P.wait("tensor", cV)
    P.op("tensor", "matmul", psS[:, 16:32], onesf[:, 0:128], nmrow[0:1, :],
         start=True, stop=True, inc=cT)
    P.wait("vector", cT)
    P.op("vector", "tensor_copy", negmb[:], psS[:, 16:32], inc=cV)

    P.wait("vector", cV, memset_V)
    P.wait("vector", cT, v65_free)
    v65r = v65[:, :].rearrange("p (t h j) -> p t h j", h=H, j=65)
    P.op("vector", "memset", v65r[:, :, :, 64:65], 1.0)
    P.drain("vector")
    P.op("vector", "sem_inc", cV.sem, 1)
    cV.n += 1
    vones_done = cV.n
    v_evac_marks = [0, 0]
    P.wait("tensor", cV)
    for t in range(NT):
        ps = psA if t % 2 == 0 else psB
        P.wait("tensor", cG, wv_loaded)
        if t >= 2:
            P.wait("tensor", cS, v_evac_marks[t % 2])
        for ic in range(ND):
            emm(ps[:, :],
                xn1T[:, S * ic + 128 * t: S * ic + 128 * (t + 1)],
                WTa[:, D * ic: D * (ic + 1)],
                start=(ic == 0), stop=False)
        emm(ps[:, :], onesb[:, 0:128], bvrow[:],
            start=False, stop=True, inc=cT)
        P.wait("scalar", cT)
        if t == 0:
            P.wait("scalar", cV, vones_done)
        dst = v65[:, 1040 * t: 1040 * (t + 1)].rearrange(
            "p (h j) -> p h j", j=65)[:, :, 0:64]
        src = ps[:, :].rearrange("p (h j) -> p h j", j=64)
        P.op("scalar", "activation", dst, src, AF.Identity, inc=cS)
        v_evac_marks[t % 2] = cS.n
        emit_pass1(2)
    emit_pass1(96)

    tap("t_xn1T", xn1T[:, :], cV)
    tap("t_qsT", qsT[:, :], cV)
    tap("t_ksT", ksT[:, :], cV)
    tap("t_v65", v65[:, :], cS)
    tap("t_negmb", negmb[:, :], cV)
    tap("t_qh2s", qh2s[:, :], cV)
    barrier()
    free("wstage", "s_bvrow", "s_nkwhd", "s_nqwhd", "qhn", "lns", "qh2s", "r_bc",
         "q2buf", "xnring", "xring", "xn1T", "WTa")
    pzfree("psS", "psT")
    psD = pzalloc("psD", [128, 1024], F32)
    oT_evac_done = [0, 0]

    for wi, (tag, c0, c1, nelem, mcl_b, sw_b) in enumerate((
            ("g", 0, 32, D * INNER, mclg_b, swg_b),
            ("u", 32, 64, D * INNER, mclu_b, swu_b),
            ("d", 64, 96, INNER * D, mcld_b, swd_b))):
        P.op("vector", "reduce_sum", colsum2[:, wi: wi + 1],
             wacc[:, c0: c1], axis=AX.X)
        P.drain("vector")
        P.op("vector", "sem_inc", cV.sem, 1)
        cV.n += 1
        P.wait("tensor", cV)
        P.op("tensor", "transpose", psD[0:1, 32:160],
             colsum2[:, wi: wi + 1], identf[:], inc=cT)
        P.wait("vector", cT)
        P.op("vector", "tensor_copy", sw1[:], psD[0:1, 32:160], inc=cV)
        P.drain("vector")
        P.op("vector", "reduce_sum", sw2[0:1, wi: wi + 1], sw1[:], axis=AX.X)
        P.drain("vector")
        P.op("vector", "tensor_scalar_mul", sw2[0:1, wi + 4: wi + 5],
             sw2[0:1, wi: wi + 1], 1.0 / nelem)
        P.drain("vector")
        P.op("vector", "tensor_scalar_max", sw2[0:1, wi: wi + 1],
             sw2[0:1, wi + 4: wi + 5], 1e-5)
        P.drain("vector")
        P.op("vector", "sem_inc", cV.sem, 1)
        cV.n += 1
        P.wait("tensor", cV)
        P.op("tensor", "matmul", psD[:, 160:161], onesf[:, 0:128],
             sw2[0:1, wi: wi + 1], start=True, stop=True, inc=cT)
        P.wait("vector", cT)
        P.op("vector", "tensor_copy", mcl_b[:], psD[:, 160:161])
        P.drain("vector")
        P.op("vector", "reciprocal", sw_b[:], mcl_b[:])
        P.drain("vector")
        P.op("vector", "sem_inc", cV.sem, 1)
        cV.n += 1
    sw_done = cV.n
    P.op("vector", "tensor_mul", mclud_b[:], mclu_b[:], mcld_b[:])
    P.drain("vector")
    P.op("vector", "tensor_scalar_mul", mclud_b[:], mclud_b[:], 1.0 / 127.0)
    P.drain("vector")
    P.op("vector", "sem_inc", cV.sem, 1)
    cV.n += 1
    oT_evac_done[1] = cV.n

    p2list = []
    for tag, dr, dst, rows, cols, cw in (("g", dwg, dscr_g, D, INNER, 2048),
                                         ("u", dwu, dscr_u, D, INNER, 2048),
                                         ("d", dwd, dscr_d, INNER, D, 1024)):
        sw_b = {"g": swg_b, "u": swu_b, "d": swd_b}[tag]
        for i2 in range(rows // 128):
            for j2 in range(cols // cw):
                p2list.append((dr[128 * i2: 128 * (i2 + 1), cw * j2: cw * (j2 + 1)],
                               dst[:, cols * i2 + cw * j2: cols * i2 + cw * (j2 + 1)],
                               sw_b, cw))
    assert len(p2list) == 64
    p2_state = {"i": 0, "ld": [0, 0], "st": 0}

    def emit_p2(n):
        for _ in range(n):
            i2 = p2_state["i"]
            if i2 >= len(p2list):
                return
            srcp, dstp, sw_b, cw = p2list[i2]
            sl = i2 % 2
            if i2 >= 2:
                P.wait("sync", cV, p2_state["ld"][sl])
            P.op("sync", "dma_start",
                 out=wstg[:, 2048 * sl: 2048 * sl + cw], in_=srcp,
                 inc=cL, inck=16)
            P.wait("vector", cL)
            if i2 == 0:
                P.wait("vector", cV, sw_done)
            if i2 >= 1:
                P.wait("vector", cG, p2_state["st"])
            ws = wstg[:, 2048 * sl: 2048 * sl + cw]
            P.op("vector", "tensor_scalar", pbufg[:, 0:cw], ws, sw_b[:, 0:1],
                 0.5, ALU.mult, ALU.is_ge)
            P.op("vector", "tensor_scalar", mbufg[:, 0:cw], ws, sw_b[:, 0:1],
                 -0.5, ALU.mult, ALU.is_le)
            P.op("vector", "tensor_sub", pw8[:, 0:cw], pbufg[:, 0:cw],
                 mbufg[:, 0:cw], inc=cV)
            p2_state["ld"][sl] = cV.n
            P.wait("gpsimd", cV)
            P.op("gpsimd", "dma_start", out=dstp, in_=pw8[:, 0:cw],
                 inc=cG, inck=16)
            p2_state["st"] = cG.n
            p2_state["i"] += 1


    oT = alloc("oT", [128, ND * S], BF16, "left")
    pTr = alloc("pTr", [128, 2 * 1024], BF16, "right")
    wstg = alloc("wstg", [128, 2 * 2048], F32, "right")
    pbufg = alloc("pbufg", [128, 2048], BF16, "right")
    mbufg = alloc("mbufg", [128, 2048], BF16, "right")
    pw8 = alloc("pw8", [128, 2048], F8, "right")
    rdrow = alloc("rdrow", [1, 1024], F32, "right")
    rdrowb = alloc("rdrowb", [1, 1024], BF16, "right")
    rdb = alloc("rdb", [128, 1024], F32, "right")
    otmp = alloc("otmp", [64, 1024], BF16, "right")

    exp_free = [0, 0]
    score_free = [0, 0]
    pairs = [(h, qc) for h in range(H) for qc in range(2)]
    tail_prev = None

    def tail_recip(prev):
        h, qc, slot, av_mark = prev
        P.wait("vector", cT, av_mark)
        P.op("vector", "reciprocal", rdrow[:], oT_ps[slot][64:65, :])
        P.drain("vector")
        P.op("vector", "tensor_copy", rdrowb[:], rdrow[:])
        P.drain("vector")
        P.op("vector", "sem_inc", cV.sem, 1)
        cV.n += 1
        return cV.n

    def tail_bcast(prev, rd_ready):
        P.wait("tensor", cV, rd_ready)
        for j in range(2):
            P.op("tensor", "matmul", psB[:, 512 * j: 512 * (j + 1)],
                 onesb[:, 0:128], rdrowb[0:1, 512 * j: 512 * (j + 1)],
                 start=True, stop=True, inc=(cT if j == 1 else None))

    def tail_evac(prev):
        h, qc, slot, av_mark = prev
        hp = h % 2
        c = h // 2
        P.wait("vector", cT)
        P.op("vector", "tensor_copy", rdb[:, :], psB[:, :], inc=cV)
        rdb_done = cV.n
        ocols = slice(S * c + 1024 * qc, S * c + 1024 * (qc + 1))
        if hp == 0:
            P.op("vector", "tensor_mul", oT[0:64, ocols],
                 oT_ps[slot][0:64, :], rdb[0:64, :], inc=cV)
        else:
            P.op("vector", "tensor_mul", otmp[:, :],
                 oT_ps[slot][0:64, :], rdb[0:64, :], inc=cV)
            P.wait("gpsimd", cV)
            P.op("gpsimd", "dma_start", out=oT[64:128, ocols],
                 in_=otmp[:, :], inc=cG, inck=16)
        oT_evac_done[slot] = cV.n
        return rdb_done, cV.n

    oT_ps = [psC, psD]
    sc_mark = [0, 0]
    for i, (h, qc) in enumerate(pairs):
        hp = h % 2
        c = h // 2
        slot = i % 2
        rd_ready = tail_recip(tail_prev) if tail_prev is not None else None
        rdb_done = None

        def emit_av(kkm):
            sl2 = kkm % 2
            P.wait("tensor", cS, score_free[sl2])
            if kkm == 0:
                P.wait("tensor", cV, oT_evac_done[slot])
            emm(oT_ps[slot][0:65, :],
                v65[:, 1040 * kkm + 65 * h: 1040 * kkm + 65 * (h + 1)],
                pTr[:, 1024 * sl2: 1024 * sl2 + 1024],
                start=(kkm == 0), stop=(kkm == NT - 1), inc=cT)
            exp_free[sl2] = cT.n

        for kk in range(NT):
            sl = kk % 2
            ps = psA if sl == 0 else psB
            P.wait("tensor", cS, score_free[sl])
            if kk == 7 and rdb_done is not None:
                P.wait("tensor", cV, rdb_done)
            emm(ps[:, :],
                ksT[64 * hp:64 * hp + 64,
                    S * c + 128 * kk: S * c + 128 * (kk + 1)],
                qsT[64 * hp:64 * hp + 64,
                    S * c + 1024 * qc: S * c + 1024 * (qc + 1)],
                start=True, stop=True, inc=cT)
            sc_mark[sl] = cT.n
            P.wait("scalar", cT, sc_mark[sl])
            P.wait("scalar", cT, exp_free[sl])
            P.op("scalar", "activation",
                 pTr[:, 1024 * sl: 1024 * sl + 1024],
                 ps[:, :], AF.Exp, bias=negmb[:, h: h + 1], inc=cS)
            score_free[sl] = cS.n
            if kk >= 1:
                emit_av(kk - 1)
            if kk == 5 and tail_prev is not None:
                P.wait("tensor", cS, score_free[1])
                tail_bcast(tail_prev, rd_ready)
                rdb_done, _ = tail_evac(tail_prev)
        emit_av(NT - 1)
        av_mark = cT.n
        tail_prev = (h, qc, slot, av_mark)
        emit_p2(2)
    emit_p2(64)
    rd_ready = tail_recip(tail_prev)
    tail_bcast(tail_prev, rd_ready)
    tail_evac(tail_prev)
    P.wait("gpsimd", cG)
    tap("t_oT", oT[:, :], cV)
    barrier()
    pzfree("psD")
    psS = pzalloc("psS2", [128, 512], F32)
    psT = pzalloc("psT2", [128, 1024], BF16)
    free("otmp", "rdb", "rdrowb", "rdrow", "pw8", "mbufg", "pbufg", "wstg", "pTr")
    free("v65", "ksT", "qsT")

    x_mid = alloc("x_mid", [128, NT * D], F32, "right")
    WTo = alloc("WTo", [128, ND * D], BF16, "right")
    xr2 = alloc("xr2", [128, 2 * D], F32, "right")
    for ic in range(ND):
        P.op("gpsimd", "dma_start", out=WTo[:, D * ic: D * (ic + 1)],
             in_=dwo[128 * ic: 128 * (ic + 1), :], inc=cG, inck=16)
    wo_loaded = cG.n

    wo_evac = [0, 0]
    for t in range(NT):
        sl = t % 2
        if t >= 2:
            P.wait("sync", cV, wo_evac[sl])
        P.op("sync", "dma_start", out=xr2[:, D * sl: D * (sl + 1)],
             in_=dx[128 * t: 128 * (t + 1), :], inc=cL, inck=16)
        ps = psA if sl == 0 else psB
        P.wait("tensor", cG, wo_loaded)
        if t >= 2:
            P.wait("tensor", cV, wo_evac[sl])
        for ic in range(ND):
            emm(ps[:, :],
                oT[:, S * ic + 128 * t: S * ic + 128 * (t + 1)],
                WTo[:, D * ic: D * (ic + 1)],
                start=(ic == 0), stop=False)
        emm(ps[:, :], onesb[:, 0:128], borow[:],
            start=False, stop=True, inc=cT)
        P.wait("vector", cT)
        P.wait("vector", cL)
        P.op("vector", "scalar_tensor_tensor", x_mid[:, D * t: D * (t + 1)],
             ps[:, :], 1.0, xr2[:, D * sl: D * (sl + 1)], ALU.mult, ALU.add,
             inc=cV)
        wo_evac[sl] = cV.n

    tap("t_xmid", x_mid[:, :], cV)
    barrier()
    free("xr2", "WTo")
    free("oT")

    w8g = alloc("w8g", [128, ND * INNER], F8, "left")
    w8u = alloc("w8u", [128, ND * INNER], F8, "left")
    w8d = alloc("w8d", [128, NI * D], F8, "left")

    P.op("sync", "dma_start", out=w8g[:, :], in_=dscr_g, inc=cL, inck=16)
    P.op("gpsimd", "dma_start", out=w8u[:, :], in_=dscr_u, inc=cG, inck=16)
    P.op("sync", "dma_start", out=w8d[:, :], in_=dscr_d, inc=cL, inck=16)
    w8_loads_L = cL.n
    w8_loads_G = cG.n
    P.wait("vector", cL, w8_loads_L)
    P.wait("vector", cG, w8_loads_G)
    P.op("vector", "sem_inc", cV.sem, 1)
    cV.n += 1
    w8_done = cV.n

    tap("t_w8g", w8g[:, :], cV)
    tap("t_w8d", w8d[:, :], cV)
    barrier()

    tmbuf = alloc("tmbuf", [128, D], F32, "right")
    n1buf = alloc("n1buf", [128, D], I32, "right")
    r1buf = alloc("r1buf", [128, D], F32, "right")
    nabuf = alloc("nabuf", [128, D], BF16, "right")
    a1T = alloc("a1T", [128, D], BF16, "right")
    gsbuf = alloc("gsbuf", [128, 1024], F32, "right")
    hbuf = alloc("hbuf", [128, INNER], F32, "right")
    a2T = alloc("a2T", [128, 1024], BF16, "right")
    l2 = alloc("l2", [128, 20 * 16], F32, "right")
    hm4 = alloc("hm4", [128, 16], F32, "right")

    def l2c(j):
        return l2[:, 16 * j: 16 * (j + 1)]

    for t in range(NT):
        x_t = x_mid[:, D * t: D * (t + 1)]
        P.op("vector", "reduce_sum", l2c(0)[:, t: t + 1], x_t, axis=AX.X)
        P.op("vector", "scalar_tensor_tensor", tmbuf[:], x_t, 1.0, x_t,
             ALU.mult, ALU.mult, accum_out=l2c(1)[:, t: t + 1])
        P.op("vector", "reduce_max", l2c(2)[:, t: t + 1], x_t, axis=AX.X)
        P.op("vector", "tensor_reduce", l2c(3)[:, t: t + 1], x_t, axis=AX.X,
             op=ALU.min)
    P.drain("vector")
    P.op("vector", "tensor_scalar_mul", l2c(4), l2c(0), 1.0 / D)
    P.drain("vector")
    P.op("vector", "tensor_mul", l2c(5), l2c(4), l2c(4))
    P.drain("vector")
    P.op("vector", "scalar_tensor_tensor", l2c(6), l2c(1), 1.0 / D, l2c(5),
         ALU.mult, ALU.subtract)
    P.drain("vector")
    P.op("vector", "tensor_scalar_add", l2c(7), l2c(6), EPS_LN)
    P.drain("vector")
    P.op("vector", "reciprocal", l2c(8), l2c(7))
    P.drain("vector")
    P.op("vector", "sem_inc", cV.sem, 1)
    cV.n += 1
    P.wait("scalar", cV)
    P.op("scalar", "activation", l2c(9), l2c(8), AF.Sqrt)
    P.drain("scalar")
    P.op("scalar", "sem_inc", cS.sem, 1)
    cS.n += 1
    P.wait("vector", cS)
    P.op("vector", "scalar_tensor_tensor", l2c(10), l2c(4), -1.0, l2c(9),
         ALU.mult, ALU.mult)
    P.drain("vector")
    P.op("vector", "tensor_sub", l2c(11), l2c(2), l2c(4))
    P.op("vector", "tensor_sub", l2c(12), l2c(4), l2c(3))
    P.drain("vector")
    P.op("vector", "tensor_max", l2c(13), l2c(11), l2c(12))
    P.drain("vector")
    P.op("vector", "tensor_mul", l2c(14), l2c(9), l2c(13))
    P.drain("vector")
    P.op("vector", "tensor_scalar_max", l2c(15), l2c(14), 1e-5)
    P.drain("vector")
    P.op("vector", "reciprocal", l2c(16), l2c(15))
    P.drain("vector")
    P.op("vector", "tensor_scalar_mul", l2c(17), l2c(16), 127.0)
    P.drain("vector")
    P.op("vector", "tensor_mul", l2c(18), l2c(9), l2c(17))
    P.drain("vector")
    P.op("vector", "tensor_mul", l2c(19), l2c(10), l2c(17))
    P.drain("vector")
    P.op("vector", "reciprocal", l2c(11), l2c(17))
    P.drain("vector")
    P.op("vector", "tensor_scalar", l2c(12), l2c(11), mclg_b[:, 0:1], -1.0,
         ALU.mult, ALU.mult)
    P.drain("vector")
    P.op("vector", "sem_inc", cV.sem, 1)
    cV.n += 1
    ln2_done = cV.n

    a1T_free = [0, 0]
    gate_free = 0
    up_free = 0
    down_free = 0
    out_marks = [0, 0]
    na1_trans_mark = 0
    na2_trans_mark = 0
    scalar_tm_mark = 0
    for t in range(NT):
        sl = t % 2
        P.wait("scalar", cV, ln2_done)
        if t >= 1:
            P.wait("scalar", cV, scalar_tm_mark)
        P.op("scalar", "activation", tmbuf[:], x_mid[:, D * t: D * (t + 1)],
             AF.Identity, scale=l2c(18)[:, t: t + 1],
             bias=l2c(19)[:, t: t + 1], inc=cS)
        P.wait("vector", cS)
        P.op("vector", "tensor_scalar", n1buf[:], tmbuf[:], MAGIC, MAGIC,
             ALU.add, ALU.subtract)
        P.op("vector", "scalar_tensor_tensor", r1buf[:], tmbuf[:], 0.0,
             n1buf[:], ALU.add, ALU.subtract, inc=cV)
        scalar_tm_mark = cV.n
        P.op("vector", "tensor_scalar", r1buf[:], r1buf[:], MAGIC64, MAGIC64,
             ALU.add, ALU.subtract)
        if t >= 1:
            P.wait("vector", cT, na1_trans_mark)
        P.op("vector", "scalar_tensor_tensor", nabuf[:], r1buf[:], -1.0,
             n1buf[:], ALU.mult, ALU.subtract, inc=cV)
        if DEBUG_TAPS and t == 0:
            P.wait("gpsimd", cV)
            P.op("gpsimd", "dma_start", out=dtaps["t_a1"], in_=nabuf[:],
                 inc=cG, inck=16)
            P.op("gpsimd", "dma_start", out=dtaps["t_tm"], in_=tmbuf[:],
                 inc=cG, inck=16)
            P.op("gpsimd", "dma_start", out=dtaps["t_n1"], in_=n1buf[:],
                 inc=cG, inck=16)
            dbg_mark = cG.n
            P.wait("vector", cG, dbg_mark)
            P.wait("scalar", cG, dbg_mark)
            P.wait("tensor", cG, dbg_mark)
        P.wait("tensor", cV)
        if t >= 1:
            P.wait("tensor", cT, a1T_free[0])
            P.wait("tensor", cS, na2_trans_mark)
        for j in range(ND):
            P.op("tensor", "transpose", psT[:, 128 * j: 128 * (j + 1)],
                 nabuf[:, 128 * j: 128 * (j + 1)], identb[:],
                 inc=(cT if j == ND - 1 else None))
        na1_trans_mark = cT.n
        P.wait("scalar", cT, na1_trans_mark)
        P.op("scalar", "activation", a1T[:, :], psT[:, :], AF.Identity,
             inc=cS)
        a1t_ready = cS.n
        for ob in range(4):
            P.wait("tensor", cS, a1t_ready)
            if t == 0 and ob == 0:
                P.wait("tensor", cV, w8_done)
            P.wait("tensor", cS, gate_free)
            for ic in range(ND):
                emm(psA[:, :],
                    a1T[:, 128 * ic: 128 * (ic + 1)],
                    w8g[:, INNER * ic + 1024 * ob: INNER * ic + 1024 * (ob + 1)],
                    start=(ic == 0), stop=(ic == ND - 1),
                    inc=(cT if ic == ND - 1 else None))
            gate_mm = cT.n
            P.wait("tensor", cV, up_free)
            for ic in range(ND):
                emm(psB[:, :],
                    a1T[:, 128 * ic: 128 * (ic + 1)],
                    w8u[:, INNER * ic + 1024 * ob: INNER * ic + 1024 * (ob + 1)],
                    start=(ic == 0), stop=(ic == ND - 1),
                    inc=(cT if ic == ND - 1 else None))
            up_mm = cT.n
            P.wait("scalar", cT, gate_mm)
            P.op("scalar", "activation", gsbuf[:], psA[:, :], AF.Silu,
                 scale=l2c(12)[:, t: t + 1], inc=cS)
            gate_free = cS.n
            P.wait("vector", cS)
            P.wait("vector", cT, up_mm)
            P.op("vector", "tensor_mul", hbuf[:, 1024 * ob: 1024 * (ob + 1)],
                 gsbuf[:], psB[:, :])
            P.op("vector", "reduce_max", hm4[:, 4 * sl + ob: 4 * sl + ob + 1],
                 hbuf[:, 1024 * ob: 1024 * (ob + 1)], axis=AX.X,
                 apply_absolute_value=True, inc=cV)
            up_free = cV.n
        P.drain("vector")
        P.op("vector", "reduce_max", l2c(13)[:, t: t + 1],
             hm4[:, 4 * sl: 4 * sl + 4], axis=AX.X)
        P.drain("vector")
        P.op("vector", "tensor_scalar_max", l2c(14)[:, t: t + 1],
             l2c(13)[:, t: t + 1], 1e-5)
        P.drain("vector")
        P.op("vector", "reciprocal", l2c(15)[:, t: t + 1],
             l2c(14)[:, t: t + 1])
        P.drain("vector")
        P.op("vector", "tensor_scalar_mul", l2c(16)[:, t: t + 1],
             l2c(15)[:, t: t + 1], 127.0)
        P.drain("vector")
        P.op("vector", "tensor_mul", l2c(5)[:, t: t + 1],
             l2c(11)[:, t: t + 1], l2c(14)[:, t: t + 1])
        P.drain("vector")
        P.op("vector", "tensor_single_scalar", l2c(6)[:, t: t + 1],
             l2c(5)[:, t: t + 1], mclud_b[:, 0:1], ALU.mult)
        P.drain("vector")
        P.op("vector", "sem_inc", cV.sem, 1)
        cV.n += 1
        s2_done = cV.n
        for qb in range(4):
            P.wait("scalar", cV, s2_done)
            P.wait("scalar", cV, scalar_tm_mark)
            P.op("scalar", "activation", tmbuf[:],
                 hbuf[:, 1024 * qb: 1024 * (qb + 1)], AF.Identity,
                 scale=l2c(16)[:, t: t + 1], bias=magic[:, 0:1], inc=cS)
            P.wait("vector", cS)
            P.op("vector", "tensor_scalar_sub", n1buf[:], tmbuf[:], MAGIC)
            P.op("vector", "scalar_tensor_tensor", r1buf[:], tmbuf[:], -MAGIC,
                 n1buf[:], ALU.add, ALU.subtract, inc=cV)
            scalar_tm_mark = cV.n
            P.op("vector", "tensor_scalar", r1buf[:], r1buf[:], MAGIC64,
                 MAGIC64, ALU.add, ALU.subtract)
            P.wait("vector", cT, na1_trans_mark)
            P.op("vector", "scalar_tensor_tensor", nabuf[:], r1buf[:], -1.0,
                 n1buf[:], ALU.mult, ALU.subtract, inc=cV)
            na2_ready = cV.n
            P.wait("tensor", cV, na2_ready)
            P.wait("tensor", cS, na2_trans_mark)
            for j in range(ND):
                P.op("tensor", "transpose", psT[:, 128 * j: 128 * (j + 1)],
                     nabuf[:, 128 * j: 128 * (j + 1)],
                     identb[:], inc=(cT if j == ND - 1 else None))
            na2_trans_tmark = cT.n
            qbase = 0
            P.wait("scalar", cT, na2_trans_tmark)
            P.op("scalar", "activation", a2T[:, qbase: qbase + 1024],
                 psT[:, :], AF.Identity, inc=cS)
            na2_trans_mark = cS.n
            if DEBUG_TAPS and t == 0:
                P.wait("gpsimd", cS, na2_trans_mark)
                P.op("gpsimd", "dma_start",
                     out=dtaps["t_a2T"][:, 1024 * qb: 1024 * (qb + 1)],
                     in_=a2T[:, 0:1024], inc=cG, inck=16)
                P.wait("vector", cG)
            na1_trans_mark = na2_trans_tmark
            P.wait("tensor", cS, na2_trans_mark)
            if qb == 0:
                P.wait("tensor", cV, down_free)
            for j in range(ND):
                ic = 8 * qb + j
                emm(psC[:, :],
                    a2T[:, qbase + 128 * j: qbase + 128 * (j + 1)],
                    w8d[:, D * ic: D * (ic + 1)],
                    start=(ic == 0), stop=(ic == NI - 1),
                    inc=(cT if ic == NI - 1 else None))
        if t == 0:
            tap("t_na1", nabuf[:, :], cV)
            tap("t_hbuf", hbuf[:, :], cV)
            tap("t_l2", l2[:, :], cV)
        a1T_free[0] = cT.n
        P.wait("vector", cT)
        P.op("vector", "scalar_tensor_tensor",
             x_mid[:, D * t: D * (t + 1)], psC[:, :],
             l2c(6)[:, t: t + 1], x_mid[:, D * t: D * (t + 1)],
             ALU.mult, ALU.add, inc=cV)
        down_free = cV.n
        P.wait("gpsimd", cV, down_free)
        P.op("gpsimd", "dma_start", out=dout[128 * t: 128 * (t + 1), :],
             in_=x_mid[:, D * t: D * (t + 1)], inc=cO, inck=16)

    P.wait("gpsimd", cO, cO.n)

    with nc.Block() as blk:
        @blk.sync
        def _(e):
            P.emit("sync", e)

        @blk.gpsimd
        def _(e):
            P.emit("gpsimd", e)

        @blk.tensor
        def _(e):
            P.emit("tensor", e)

        @blk.vector
        def _(e):
            P.emit("vector", e)

        @blk.scalar
        def _(e):
            P.emit("scalar", e)

    for name in list(live.keys())[::-1]:
        try:
            live.pop(name).__exit__(None, None, None)
        except Exception:
            pass
    for cm in pcm[::-1]:
        try:
            cm.__exit__(None, None, None)
        except Exception:
            pass
    for cm in semlist[::-1]:
        try:
            cm.__exit__(None, None, None)
        except Exception:
            pass
    return nc


_CACHED_NC = None


def kernel(L0_attn_x, L0_attn_Wq, L0_attn_bq, L0_attn_Wk, L0_attn_bk,
           L0_attn_Wv, L0_attn_bv, L0_attn_Wo, L0_attn_bo,
           L0_attn_norm_q_w, L0_attn_norm_k_w,
           L0_ffn_w_gate, L0_ffn_w_up, L0_ffn_w_down):
    global _CACHED_NC
    if _CACHED_NC is None:
        _CACHED_NC = build_program()
    nc = _CACHED_NC

    f32 = np.float32
    x = np.ascontiguousarray(np.asarray(L0_attn_x, dtype=f32))
    assert x.shape == (8, S, D)

    def c(a):
        return np.ascontiguousarray(np.asarray(a, dtype=f32))

    common = {
        "wqt": c(np.asarray(L0_attn_Wq, dtype=f32).T),
        "wkt": c(np.asarray(L0_attn_Wk, dtype=f32).T),
        "wvt": c(np.asarray(L0_attn_Wv, dtype=f32).T),
        "wot": c(np.asarray(L0_attn_Wo, dtype=f32).T),
        "bq8": c(np.asarray(L0_attn_bq, dtype=f32).reshape(ND, 128).T),
        "bk8": c(np.asarray(L0_attn_bk, dtype=f32).reshape(ND, 128).T),
        "bvrow": c(np.asarray(L0_attn_bv, dtype=f32).reshape(1, D)),
        "borow": c(np.asarray(L0_attn_bo, dtype=f32).reshape(1, D)),
        "nqw8": c(np.asarray(L0_attn_norm_q_w, dtype=f32).reshape(ND, 128).T),
        "nkw8": c(np.asarray(L0_attn_norm_k_w, dtype=f32).reshape(ND, 128).T),
        "nqwhd": c(np.asarray(L0_attn_norm_q_w, dtype=f32).reshape(H, HD)),
        "nkwhd": c(np.asarray(L0_attn_norm_k_w, dtype=f32).reshape(H, HD)),
        "wgT": c(np.asarray(L0_ffn_w_gate, dtype=f32).T),
        "wuT": c(np.asarray(L0_ffn_w_up, dtype=f32).T),
        "wdT": c(np.asarray(L0_ffn_w_down, dtype=f32).T),
        "identf": np.eye(128, dtype=f32),
    }
    in_maps = []
    for core in range(8):
        m = {"x": np.ascontiguousarray(x[core])}
        m.update(common)
        in_maps.append(m)

    res = run_bass_kernel_spmd(nc, in_maps, core_ids=list(range(8)))
    out = np.stack([np.asarray(r["out"], dtype=np.float32)
                    for r in res.results], axis=0)
    return out


# version 41
# speedup vs baseline: 1.0349x; 1.0349x over previous
import numpy as np
import ml_dtypes
import concourse.bass as bass
import concourse.mybir as mybir
from concourse.bass_utils import run_bass_kernel_spmd

BF16 = mybir.dt.bfloat16
F32 = mybir.dt.float32
I32 = mybir.dt.int32
F8 = mybir.dt.float8e4
AF = mybir.ActivationFunctionType
ALU = mybir.AluOpType
AX = mybir.AxisListType

MAGIC = 12582912.0
MAGIC64 = 6755399441055744.0
S, D, H, HD, INNER = 2048, 1024, 16, 64, 4096
NT = S // 128
ND = D // 128
NI = INNER // 128
EPS_LN = 1e-5
EPS_RMS = 1e-6

ENGINES = ("sync", "gpsimd", "tensor", "vector", "scalar")


class Ch:
    def __init__(self, sem):
        self.sem = sem
        self.n = 0


class Prog:
    def __init__(self):
        self.ops = {e: [] for e in ENGINES}

    def op(self, eng, meth, *args, inc=None, inck=1, **kw):
        if inc is not None:
            inc.n += inck
            sem, val = inc.sem, inck
        else:
            sem, val = None, 0
        self.ops[eng].append(("op", meth, args, kw, sem, val))

    def wait(self, eng, ch, val=None):
        v = ch.n if val is None else val
        if v > 0:
            self.ops[eng].append(("wait", ch.sem, v))

    def drain(self, eng):
        self.ops[eng].append(("drain",))

    def emit(self, eng_name, eng):
        for entry in self.ops[eng_name]:
            if entry[0] == "op":
                _, meth, args, kw, sem, val = entry
                r = getattr(eng, meth)(*args, **kw)
                if sem is not None:
                    r.then_inc(sem, val)
            elif entry[0] == "wait":
                eng.wait_ge(entry[1], entry[2])
            else:
                eng.drain()


DEBUG_TAPS = False


def build_program():
    nc = bass.Bass("TRN2", target_bir_lowering=False,
                   detect_race_conditions=False)

    dx = nc.dram_tensor("x", [S, D], F32, kind="ExternalInput").ap()
    dwq = nc.dram_tensor("wqt", [D, D], F32, kind="ExternalInput").ap()
    dwk = nc.dram_tensor("wkt", [D, D], F32, kind="ExternalInput").ap()
    dwv = nc.dram_tensor("wvt", [D, D], F32, kind="ExternalInput").ap()
    dwo = nc.dram_tensor("wot", [D, D], F32, kind="ExternalInput").ap()
    dbq = nc.dram_tensor("bq8", [128, ND], F32, kind="ExternalInput").ap()
    dbk = nc.dram_tensor("bk8", [128, ND], F32, kind="ExternalInput").ap()
    dbv = nc.dram_tensor("bvrow", [1, D], F32, kind="ExternalInput").ap()
    dbo = nc.dram_tensor("borow", [1, D], F32, kind="ExternalInput").ap()
    dnq8 = nc.dram_tensor("nqw8", [128, ND], F32, kind="ExternalInput").ap()
    dnk8 = nc.dram_tensor("nkw8", [128, ND], F32, kind="ExternalInput").ap()
    dnqh = nc.dram_tensor("nqwhd", [H, HD], F32, kind="ExternalInput").ap()
    dnkh = nc.dram_tensor("nkwhd", [H, HD], F32, kind="ExternalInput").ap()
    dwg = nc.dram_tensor("wgT", [D, INNER], F32, kind="ExternalInput").ap()
    dwu = nc.dram_tensor("wuT", [D, INNER], F32, kind="ExternalInput").ap()
    dwd = nc.dram_tensor("wdT", [INNER, D], F32, kind="ExternalInput").ap()
    didf = nc.dram_tensor("identf", [128, 128], F32, kind="ExternalInput").ap()
    dout = nc.dram_tensor("out", [S, D], F32, kind="ExternalOutput").ap()
    dscr_g = nc.dram_tensor("scr_g", [128, ND * INNER], F8, kind="Internal").ap()
    dscr_u = nc.dram_tensor("scr_u", [128, ND * INNER], F8, kind="Internal").ap()
    dscr_d = nc.dram_tensor("scr_d", [128, NI * D], F8, kind="Internal").ap()
    dtaps = {}
    if DEBUG_TAPS:
        for nm, shape, dt in (
                ("t_xn1T", [128, ND * S], F32), ("t_qsT", [128, ND * S], F32),
                ("t_ksT", [128, ND * S], F32), ("t_v65", [128, NT * H * 65], F32),
                ("t_negmb", [128, H], F32), ("t_oT", [128, ND * S], F32),
                ("t_xmid", [128, NT * D], F32), ("t_w8g", [128, ND * INNER], F32),
                ("t_w8d", [128, NI * D], F32),
                ("t_scl", [128, 16], F32), ("t_na1", [128, D], F32),
                ("t_hbuf", [128, INNER], F32), ("t_a2T", [128, INNER], F32),
                ("t_l2", [128, 320], F32), ("t_qh2s", [16, S], F32),
                ("t_a1", [128, D], F32), ("t_tm", [128, D], F32),
                ("t_n1", [128, D], I32)):
            dtaps[nm] = nc.dram_tensor(nm, shape, dt, kind="ExternalOutput").ap()

    P = Prog()
    semlist = []

    def mksem(name):
        cm = nc.semaphore(name)
        semlist.append(cm)
        return Ch(cm.__enter__())

    cL = mksem("cL")
    cG = mksem("cG")
    cT = mksem("cT")
    cV = mksem("cV")
    cS = mksem("cS")
    cO = mksem("cO")
    cB = mksem("cB")

    def barrier():
        base = cB.n
        cB.n += len(ENGINES)
        for e in ENGINES:
            P.ops[e].append(("op", "sem_inc", (cB.sem, 1), {}, None, 0))
            P.ops[e].append(("wait", cB.sem, base + len(ENGINES)))

    def tap(nm, src_ap, ch=None):
        if not DEBUG_TAPS:
            return
        if ch is not None:
            P.wait("gpsimd", ch)
        P.op("gpsimd", "dma_start", out=dtaps[nm], in_=src_ap, inc=cG,
             inck=16)

    def emm(out, lhsT, rhs, start, stop, inc=None):
        N = rhs.shape[-1]
        if N <= 512:
            P.op("tensor", "matmul", out, lhsT, rhs, start=start, stop=stop,
                 inc=inc)
            return
        for a in range(0, N, 512):
            b = min(a + 512, N)
            P.op("tensor", "matmul", out[:, a:b], lhsT, rhs[:, a:b],
                 start=start, stop=stop, inc=(inc if b == N else None))

    live = {}

    def alloc(name, shape, dtype, side):
        cm = nc.sbuf_tensor(name, shape, dtype, side=side)
        h = cm.__enter__()
        live[name] = cm
        return h

    def free(*names):
        for name in names:
            live.pop(name).__exit__(None, None, None)

    identf = alloc("c_identf", [128, 128], F32, "left")
    identb = alloc("c_identb", [128, 128], BF16, "left")
    bq8 = alloc("c_bq8", [128, ND], F32, "left")
    bk8 = alloc("c_bk8", [128, ND], F32, "left")
    nqw8 = alloc("c_nqw8", [128, ND], F32, "left")
    nkw8 = alloc("c_nkw8", [128, ND], F32, "left")
    borow = alloc("c_borow", [1, D], BF16, "left")
    magic = alloc("c_magic", [128, 1], F32, "left")
    onesb = alloc("c_onesb", [1, 128], BF16, "left")
    onesf = alloc("c_onesf", [1, 128], F32, "left")
    masks = alloc("c_masks", [128, 16 * ND], BF16, "left")
    onesfc = alloc("c_onesfc", [16, 1], F32, "left")
    negmb = alloc("c_negmb", [128, H], F32, "left")
    wmaxq = alloc("c_wmaxq", [H, 1], F32, "left")
    wmaxk = alloc("c_wmaxk", [H, 1], F32, "left")
    nmrow = alloc("c_nmrow", [1, H], F32, "left")
    sm16a = alloc("c_sm16a", [16, 1], F32, "left")
    sm16b = alloc("c_sm16b", [16, 1], F32, "left")
    sm16c = alloc("c_sm16c", [16, 1], F32, "left")
    wacc = alloc("c_wacc", [128, 96], F32, "left")
    swg_b = alloc("c_swg_b", [128, 1], F32, "left")
    mclg_b = alloc("c_mclg_b", [128, 1], F32, "left")
    swu_b = alloc("c_swu_b", [128, 1], F32, "left")
    mclu_b = alloc("c_mclu_b", [128, 1], F32, "left")
    swd_b = alloc("c_swd_b", [128, 1], F32, "left")
    mcld_b = alloc("c_mcld_b", [128, 1], F32, "left")
    mclud_b = alloc("c_mclud_b", [128, 1], F32, "left")
    sw1 = alloc("c_sw1", [1, 128], F32, "left")
    sw2 = alloc("c_sw2", [1, 8], F32, "left")
    colsum2 = alloc("c_colsum2", [128, 4], F32, "left")

    pcm = []

    def palloc(name, shape, dtype):
        cm = nc.psum_tensor(name, shape, dtype)
        pcm.append(cm)
        return cm.__enter__()

    plive = {}

    def pzalloc(name, shape, dtype):
        cm = nc.psum_tensor(name, shape, dtype)
        plive[name] = cm
        return cm.__enter__()

    def pzfree(*names):
        for name in names:
            plive.pop(name).__exit__(None, None, None)

    psA = palloc("psA", [128, 1024], F32)
    psB = palloc("psB", [128, 1024], F32)
    psC = palloc("psC", [128, 1024], F32)
    psT = pzalloc("psT", [128, 1024], BF16)
    psS = pzalloc("psS", [128, 512], F32)

    P.op("sync", "dma_start", out=identf[:], in_=didf, inc=cL, inck=16)
    P.op("sync", "dma_start", out=bq8[:], in_=dbq, inc=cL, inck=16)
    P.op("sync", "dma_start", out=bk8[:], in_=dbk, inc=cL, inck=16)
    P.op("sync", "dma_start", out=nqw8[:], in_=dnq8, inc=cL, inck=16)
    P.op("sync", "dma_start", out=nkw8[:], in_=dnk8, inc=cL, inck=16)
    P.op("gpsimd", "dma_start", out=identb[:], in_=didf, inc=cG, inck=16)
    P.op("gpsimd", "dma_start", out=borow[:], in_=dbo, inc=cG, inck=16)

    P.op("vector", "memset", magic[:], MAGIC)
    P.op("vector", "memset", onesb[:], 1.0)
    P.op("vector", "memset", onesf[:], 1.0)
    P.op("vector", "memset", masks[:], 0.0)
    P.drain("vector")
    P.op("vector", "memset", onesfc[:], 1.0)
    P.drain("vector")
    for c in range(ND):
        P.op("vector", "memset", masks[0:64, 16 * c + 2 * c: 16 * c + 2 * c + 1], 1.0)
        P.op("vector", "memset", masks[64:128, 16 * c + 2 * c + 1: 16 * c + 2 * c + 2], 1.0)
    P.drain("vector")
    P.op("vector", "sem_inc", cV.sem, 1)
    cV.n += 1
    memset_V = cV.n

    qsT = alloc("qsT", [128, ND * S], BF16, "right")
    ksT = alloc("ksT", [128, ND * S], BF16, "right")
    v65 = alloc("v65", [128, NT * H * 65], BF16, "right")
    WTa = alloc("WTa", [128, ND * D], BF16, "right")
    xn1T = alloc("xn1T", [128, ND * S], BF16, "right")
    xring = alloc("xring", [128, 2 * D], F32, "right")
    xnring = alloc("xnring", [128, 2 * D], BF16, "right")
    q2buf = alloc("q2buf", [128, S], BF16, "right")
    r_bc = alloc("r_bc", [128, S], F32, "right")
    qh2s = alloc("qh2s", [16, S], F32, "right")
    lns = alloc("lns", [128, 16 * NT], F32, "right")
    qhn = alloc("qhn", [H, S], F32, "right")
    nqwhd = alloc("s_nqwhd", [H, HD], F32, "right")
    nkwhd = alloc("s_nkwhd", [H, HD], F32, "right")
    bvrow = alloc("s_bvrow", [1, D], BF16, "right")
    wstage = alloc("wstage", [128, 2 * 1024], F32, "right")

    def lncol(t, j):
        return lns[:, 16 * t + j: 16 * t + j + 1]

    P.op("sync", "dma_start", out=nqwhd[:], in_=dnqh, inc=cL, inck=16)
    P.op("sync", "dma_start", out=nkwhd[:], in_=dnkh, inc=cL, inck=16)
    P.op("gpsimd", "dma_start", out=bvrow[:], in_=dbv, inc=cG, inck=16)
    consts_L = cL.n
    consts_G = cG.n

    P.wait("vector", cL, consts_L)
    P.op("vector", "reduce_max", wmaxq[:], nqwhd[:], axis=AX.X,
         apply_absolute_value=True)
    P.op("vector", "reduce_max", wmaxk[:], nkwhd[:], axis=AX.X,
         apply_absolute_value=True)
    P.drain("vector")
    P.op("vector", "sem_inc", cV.sem, 1)
    cV.n += 1

    pass1_srcs = []
    for tag, dr, rows, cols in (("g", dwg, D, INNER), ("u", dwu, D, INNER),
                                ("d", dwd, INNER, D)):
        for i in range(rows // 128):
            for j in range(cols // 1024):
                pass1_srcs.append(dr[128 * i:128 * (i + 1),
                                     1024 * j:1024 * (j + 1)])
    assert len(pass1_srcs) == 96
    p1_idx = [0]
    p1_marks = [0, 0]

    def emit_pass1(nchunks):
        for _ in range(nchunks):
            i = p1_idx[0]
            if i >= 96:
                return
            sl = i % 2
            if i >= 2:
                P.wait("sync", cV, p1_marks[sl])
            P.op("sync", "dma_start",
                 out=wstage[:, 1024 * sl: 1024 * (sl + 1)],
                 in_=pass1_srcs[i], inc=cL, inck=16)
            P.wait("vector", cL)
            P.op("vector", "reduce_sum", wacc[:, i: i + 1],
                 wstage[:, 1024 * sl: 1024 * (sl + 1)], axis=AX.X,
                 apply_absolute_value=True, inc=cV)
            p1_marks[sl] = cV.n
            p1_idx[0] += 1

    P.wait("gpsimd", cG, consts_G)
    for ic in range(ND):
        P.op("gpsimd", "dma_start", out=WTa[:, D * ic: D * (ic + 1)],
             in_=dwq[128 * ic: 128 * (ic + 1), :], inc=cG, inck=16)
    wq_loaded = cG.n
    for ic in range(ND):
        P.op("gpsimd", "dma_start", out=v65[:, D * ic: D * (ic + 1)],
             in_=dwk[128 * ic: 128 * (ic + 1), :], inc=cG, inck=16)
    wk_loaded = cG.n

    norm_marks = [0, 0]
    trans_marks = [0, 0]
    evacT_m = [0]

    def emit_ln_tile(t):
        sl = t % 2
        if t >= 2:
            P.wait("sync", cS, norm_marks[sl])
        P.op("sync", "dma_start", out=xring[:, D * sl: D * (sl + 1)],
             in_=dx[128 * t: 128 * (t + 1), :], inc=cL, inck=16)
        x_t = xring[:, D * sl: D * (sl + 1)]
        P.wait("vector", cL)
        P.op("vector", "reduce_sum", lncol(t, 0), x_t, axis=AX.X)
        P.op("vector", "scalar_tensor_tensor", r_bc[:, 0:D], x_t, 1.0, x_t,
             ALU.mult, ALU.mult, accum_out=lncol(t, 1))
        P.drain("vector")
        P.op("vector", "tensor_scalar_mul", lncol(t, 2), lncol(t, 0), 1.0 / D)
        P.drain("vector")
        P.op("vector", "tensor_mul", lncol(t, 3), lncol(t, 2), lncol(t, 2))
        P.drain("vector")
        P.op("vector", "scalar_tensor_tensor", lncol(t, 4), lncol(t, 1),
             1.0 / D, lncol(t, 3), ALU.mult, ALU.subtract)
        P.drain("vector")
        P.op("vector", "tensor_scalar_add", lncol(t, 5), lncol(t, 4), EPS_LN)
        P.drain("vector")
        P.op("vector", "reciprocal", lncol(t, 6), lncol(t, 5))
        P.drain("vector")
        P.op("vector", "sem_inc", cV.sem, 1)
        cV.n += 1
        P.wait("scalar", cV)
        P.op("scalar", "activation", lncol(t, 7), lncol(t, 6), AF.Sqrt)
        P.drain("scalar")
        P.op("scalar", "sem_inc", cS.sem, 1)
        cS.n += 1
        P.wait("vector", cS)
        P.op("vector", "scalar_tensor_tensor", lncol(t, 8), lncol(t, 2), -1.0,
             lncol(t, 7), ALU.mult, ALU.mult)
        P.drain("vector")
        P.op("vector", "sem_inc", cV.sem, 1)
        cV.n += 1
        P.wait("scalar", cV)
        if t >= 2:
            P.wait("scalar", cT, trans_marks[sl])
        P.op("scalar", "activation", xnring[:, D * sl: D * (sl + 1)], x_t,
             AF.Identity, scale=lncol(t, 7), bias=lncol(t, 8), inc=cS)
        norm_marks[sl] = cS.n
        P.wait("tensor", cS)
        if t >= 1:
            P.wait("tensor", cV, evacT_m[0])
        for j in range(ND):
            P.op("tensor", "transpose", psT[:, 128 * j: 128 * (j + 1)],
                 xnring[:, D * sl + 128 * j: D * sl + 128 * (j + 1)],
                 identb[:], inc=(cT if j == ND - 1 else None))
        trans_marks[sl] = cT.n
        P.wait("vector", cT)
        tsrc = psT[:, 0:1024].rearrange("p (c f) -> p c f", f=128)
        tdst = xn1T[:, :].rearrange("p (c f) -> p c f", f=S)[:, :, 128 * t: 128 * (t + 1)]
        P.op("vector", "tensor_copy", tdst, tsrc, inc=cV)
        evacT_m[0] = cV.n
        emit_pass1(4)

    def qk_state(which, w_loaded_mark, wt):
        return {"which": which, "w": w_loaded_mark, "wt": wt, "wt_free": 0,
                "evac": {}, "ev_idx": {}, "qh": {0: 0, 1: 0}, "cm": [0],
                "prev": None, "idx": 0,
                "tgt": qsT if which == "q" else ksT,
                "b8": bq8 if which == "q" else bk8,
                "nw8": nqw8 if which == "q" else nkw8,
                "wmax": wmaxq if which == "q" else wmaxk,
                "bound": sm16a if which == "q" else sm16b}

    def emit_qh2(st, oc2, tc2):
        P.wait("tensor", cS, st["evac"][(oc2, tc2)])
        P.wait("tensor", cV, st["cm"][0])
        emm(psC[0:16, :],
            masks[:, 16 * oc2: 16 * oc2 + 16],
            q2buf[:, 1024 * tc2: 1024 * (tc2 + 1)],
            start=True, stop=True, inc=cT)
        st["qh"][tc2] = cT.n
        P.wait("vector", cT, st["qh"][tc2])
        dstq = qh2s[:, 1024 * tc2: 1024 * (tc2 + 1)]
        if oc2 == 0:
            P.op("vector", "tensor_copy", dstq, psC[0:16, :], inc=cV)
        else:
            P.op("vector", "scalar_tensor_tensor", dstq, dstq, 1.0,
                 psC[0:16, :], ALU.mult, ALU.add, inc=cV)
        P.drain("vector")
        st["cm"][0] = cV.n

    def emit_qk_group(st, oc, tc, wait_cv=None):
        pp = psA if st["idx"] % 2 == 0 else psB
        if wait_cv is not None:
            P.wait("tensor", cV, wait_cv)
        P.wait("tensor", cG, st["w"])
        if st["idx"] - 2 in st["ev_idx"]:
            P.wait("tensor", cS, st["ev_idx"][st["idx"] - 2])
        for ic in range(ND):
            emm(pp[:, :],
                st["wt"][:, D * ic + 128 * oc: D * ic + 128 * (oc + 1)],
                xn1T[:, S * ic + 1024 * tc: S * ic + 1024 * (tc + 1)],
                start=(ic == 0), stop=(ic == ND - 1),
                inc=(cT if ic == ND - 1 else None))
        if (oc, tc) == (ND - 1, 1):
            st["wt_free"] = cT.n
        if st["prev"] is not None:
            emit_qh2(st, *st["prev"])
        P.wait("scalar", cT)
        P.op("scalar", "activation",
             st["tgt"][:, S * oc + 1024 * tc: S * oc + 1024 * (tc + 1)],
             pp[:, :], AF.Identity, bias=st["b8"][:, oc: oc + 1])
        P.op("scalar", "activation",
             q2buf[:, 1024 * tc: 1024 * (tc + 1)],
             pp[:, :], AF.Square, bias=st["b8"][:, oc: oc + 1], inc=cS)
        st["evac"][(oc, tc)] = st["ev_idx"][st["idx"]] = cS.n
        st["prev"] = (oc, tc)
        st["idx"] += 1

    def emit_qk_tail(st):
        tgt, nw8 = st["tgt"], st["nw8"]
        emit_qh2(st, *st["prev"])
        qh2_done = st["cm"][0]
        P.wait("tensor", cV, qh2_done)
        P.wait("tensor", cS)
        for j in range(4):
            pq = psA if j < 2 else psB
            P.op("tensor", "matmul", pq[0:1, 512 * (j % 2): 512 * (j % 2) + 512],
                 onesfc[:], qh2s[:, 512 * j: 512 * (j + 1)],
                 start=True, stop=True, inc=(cT if j == 3 else None))
        P.wait("vector", cT)
        P.op("vector", "tensor_scalar_mul", qhn[0:1, 0:1024], psA[0:1, :], 1.0 / D)
        P.op("vector", "tensor_scalar_mul", qhn[0:1, 1024:2048], psB[0:1, :], 1.0 / D)
        P.drain("vector")
        P.op("vector", "tensor_scalar_add", r_bc[0:1, :], qhn[0:1, :], EPS_RMS)
        P.drain("vector")
        P.op("vector", "reciprocal", qhn[0:1, :], r_bc[0:1, :])
        P.drain("vector")
        P.op("vector", "sem_inc", cV.sem, 1)
        cV.n += 1
        P.wait("scalar", cV)
        P.op("scalar", "activation", qhn[0:1, :], qhn[0:1, :], AF.Sqrt,
             scale=(0.015625 if st["which"] == "q" else 1.0))
        P.drain("scalar")
        P.op("scalar", "sem_inc", cS.sem, 1)
        cS.n += 1
        P.wait("tensor", cS)
        for j in range(4):
            pq = psA if j < 2 else psB
            P.op("tensor", "matmul", pq[:, 512 * (j % 2): 512 * (j % 2) + 512],
                 onesf[:, 0:128], qhn[0:1, 512 * j: 512 * (j + 1)],
                 start=True, stop=True, inc=(cT if j == 3 else None))
        P.wait("vector", cT)
        P.op("vector", "tensor_copy", r_bc[:, 0:1024], psA[:, :])
        P.op("vector", "tensor_copy", r_bc[:, 1024:2048], psB[:, :], inc=cV)
        for oc in range(ND):
            P.op("vector", "scalar_tensor_tensor",
                 tgt[:, S * oc: S * (oc + 1)],
                 tgt[:, S * oc: S * (oc + 1)],
                 nw8[:, oc: oc + 1], r_bc[:, :], ALU.mult, ALU.mult,
                 inc=(cV if oc == ND - 1 else None))
        scale_done = cV.n
        P.wait("scalar", cV, qh2_done)
        P.wait("scalar", cT)
        P.op("scalar", "activation", qhn[:, :], qh2s[:, :], AF.Sqrt, inc=cS)
        P.wait("vector", cS)
        P.op("vector", "tensor_mul", qhn[:, :], qhn[:, :], r_bc[0:16, :])
        P.drain("vector")
        P.op("vector", "reduce_max", st["bound"][:], qhn[:, :], axis=AX.X)
        P.drain("vector")
        P.op("vector", "tensor_mul", st["bound"][:], st["bound"][:],
             st["wmax"][:])
        P.drain("vector")
        P.op("vector", "sem_inc", cV.sem, 1)
        cV.n += 1
        return scale_done, st["wt_free"]

    for t in range(8):
        emit_ln_tile(t)
    evac_t7 = evacT_m[0]
    stq = qk_state("q", wq_loaded, WTa)
    for g in range(ND):
        emit_qk_group(stq, g, 0, wait_cv=(evac_t7 if g == 0 else None))
        emit_ln_tile(8 + g)
    for g in range(ND):
        emit_qk_group(stq, g, 1, wait_cv=(evacT_m[0] if g == 0 else None))
    q_scale_done, wta_free = emit_qk_tail(stq)
    emit_pass1(16)
    P.wait("gpsimd", cT, wta_free)
    for ic in range(ND):
        P.op("gpsimd", "dma_start", out=WTa[:, D * ic: D * (ic + 1)],
             in_=dwv[128 * ic: 128 * (ic + 1), :], inc=cG, inck=16)
    wv_loaded = cG.n
    stk = qk_state("k", wk_loaded, v65)
    kv_wait = [cV.n]
    for tc in range(2):
        for g in range(ND):
            emit_qk_group(stk, g, tc,
                          wait_cv=(kv_wait[0] if (tc, g) == (0, 0) else None))
    k_scale_done, v65_free = emit_qk_tail(stk)
    emit_pass1(16)

    P.op("vector", "tensor_mul", sm16c[:], sm16a[:], sm16b[:])
    P.drain("vector")
    P.op("vector", "tensor_scalar_mul", sm16c[:], sm16c[:], -1.0)
    P.drain("vector")
    P.op("vector", "sem_inc", cV.sem, 1)
    cV.n += 1
    P.wait("tensor", cV)
    P.op("tensor", "transpose", psS[0:1, 0:16], sm16c[:],
         identf[0:16, 0:16], inc=cT)
    P.wait("vector", cT)
    P.op("vector", "tensor_copy", nmrow[:], psS[0:1, 0:16], inc=cV)
    P.wait("tensor", cV)
    P.op("tensor", "matmul", psS[:, 16:32], onesf[:, 0:128], nmrow[0:1, :],
         start=True, stop=True, inc=cT)
    P.wait("vector", cT)
    P.op("vector", "tensor_copy", negmb[:], psS[:, 16:32], inc=cV)

    P.wait("vector", cV, memset_V)
    P.wait("vector", cT, v65_free)
    v65r = v65[:, :].rearrange("p (t h j) -> p t h j", h=H, j=65)
    P.op("vector", "memset", v65r[:, :, :, 64:65], 1.0)
    P.drain("vector")
    P.op("vector", "sem_inc", cV.sem, 1)
    cV.n += 1
    vones_done = cV.n
    v_evac_marks = [0, 0]
    P.wait("tensor", cV)
    for t in range(NT):
        ps = psA if t % 2 == 0 else psB
        P.wait("tensor", cG, wv_loaded)
        if t >= 2:
            P.wait("tensor", cS, v_evac_marks[t % 2])
        for ic in range(ND):
            emm(ps[:, :],
                xn1T[:, S * ic + 128 * t: S * ic + 128 * (t + 1)],
                WTa[:, D * ic: D * (ic + 1)],
                start=(ic == 0), stop=False)
        emm(ps[:, :], onesb[:, 0:128], bvrow[:],
            start=False, stop=True, inc=cT)
        P.wait("scalar", cT)
        if t == 0:
            P.wait("scalar", cV, vones_done)
        dst = v65[:, 1040 * t: 1040 * (t + 1)].rearrange(
            "p (h j) -> p h j", j=65)[:, :, 0:64]
        src = ps[:, :].rearrange("p (h j) -> p h j", j=64)
        P.op("scalar", "activation", dst, src, AF.Identity, inc=cS)
        v_evac_marks[t % 2] = cS.n
        emit_pass1(2)
    emit_pass1(96)

    tap("t_xn1T", xn1T[:, :], cV)
    tap("t_qsT", qsT[:, :], cV)
    tap("t_ksT", ksT[:, :], cV)
    tap("t_v65", v65[:, :], cS)
    tap("t_negmb", negmb[:, :], cV)
    tap("t_qh2s", qh2s[:, :], cV)
    barrier()
    free("wstage", "s_bvrow", "s_nkwhd", "s_nqwhd", "qhn", "lns", "qh2s", "r_bc",
         "q2buf", "xnring", "xring", "xn1T", "WTa")
    pzfree("psS", "psT")
    psD = pzalloc("psD", [128, 1024], F32)
    oT_evac_done = [0, 0]

    for wi, (tag, c0, c1, nelem, mcl_b, sw_b) in enumerate((
            ("g", 0, 32, D * INNER, mclg_b, swg_b),
            ("u", 32, 64, D * INNER, mclu_b, swu_b),
            ("d", 64, 96, INNER * D, mcld_b, swd_b))):
        P.op("vector", "reduce_sum", colsum2[:, wi: wi + 1],
             wacc[:, c0: c1], axis=AX.X)
        P.drain("vector")
        P.op("vector", "sem_inc", cV.sem, 1)
        cV.n += 1
        P.wait("tensor", cV)
        P.op("tensor", "transpose", psD[0:1, 32:160],
             colsum2[:, wi: wi + 1], identf[:], inc=cT)
        P.wait("vector", cT)
        P.op("vector", "tensor_copy", sw1[:], psD[0:1, 32:160], inc=cV)
        P.drain("vector")
        P.op("vector", "reduce_sum", sw2[0:1, wi: wi + 1], sw1[:], axis=AX.X)
        P.drain("vector")
        P.op("vector", "tensor_scalar_mul", sw2[0:1, wi + 4: wi + 5],
             sw2[0:1, wi: wi + 1], 1.0 / nelem)
        P.drain("vector")
        P.op("vector", "tensor_scalar_max", sw2[0:1, wi: wi + 1],
             sw2[0:1, wi + 4: wi + 5], 1e-5)
        P.drain("vector")
        P.op("vector", "sem_inc", cV.sem, 1)
        cV.n += 1
        P.wait("tensor", cV)
        P.op("tensor", "matmul", psD[:, 160:161], onesf[:, 0:128],
             sw2[0:1, wi: wi + 1], start=True, stop=True, inc=cT)
        P.wait("vector", cT)
        P.op("vector", "tensor_copy", mcl_b[:], psD[:, 160:161])
        P.drain("vector")
        P.op("vector", "reciprocal", sw_b[:], mcl_b[:])
        P.drain("vector")
        P.op("vector", "sem_inc", cV.sem, 1)
        cV.n += 1
    sw_done = cV.n
    P.op("vector", "tensor_mul", mclud_b[:], mclu_b[:], mcld_b[:])
    P.drain("vector")
    P.op("vector", "tensor_scalar_mul", mclud_b[:], mclud_b[:], 1.0 / 127.0)
    P.drain("vector")
    P.op("vector", "sem_inc", cV.sem, 1)
    cV.n += 1
    oT_evac_done[1] = cV.n

    p2list = []
    for tag, dr, dst, rows, cols, cw in (("g", dwg, dscr_g, D, INNER, 2048),
                                         ("u", dwu, dscr_u, D, INNER, 2048),
                                         ("d", dwd, dscr_d, INNER, D, 1024)):
        sw_b = {"g": swg_b, "u": swu_b, "d": swd_b}[tag]
        for i2 in range(rows // 128):
            for j2 in range(cols // cw):
                p2list.append((dr[128 * i2: 128 * (i2 + 1), cw * j2: cw * (j2 + 1)],
                               dst[:, cols * i2 + cw * j2: cols * i2 + cw * (j2 + 1)],
                               sw_b, cw))
    assert len(p2list) == 64
    p2_state = {"i": 0, "ld": [0, 0], "st": 0}

    def emit_p2(n):
        for _ in range(n):
            i2 = p2_state["i"]
            if i2 >= len(p2list):
                return
            srcp, dstp, sw_b, cw = p2list[i2]
            sl = i2 % 2
            if i2 >= 2:
                P.wait("sync", cV, p2_state["ld"][sl])
            P.op("sync", "dma_start",
                 out=wstg[:, 2048 * sl: 2048 * sl + cw], in_=srcp,
                 inc=cL, inck=16)
            P.wait("vector", cL)
            if i2 == 0:
                P.wait("vector", cV, sw_done)
            if i2 >= 1:
                P.wait("vector", cG, p2_state["st"])
            ws = wstg[:, 2048 * sl: 2048 * sl + cw]
            P.op("vector", "tensor_scalar", pbufg[:, 0:cw], ws, sw_b[:, 0:1],
                 0.5, ALU.mult, ALU.is_ge)
            P.op("vector", "tensor_scalar", mbufg[:, 0:cw], ws, sw_b[:, 0:1],
                 -0.5, ALU.mult, ALU.is_le)
            P.op("vector", "tensor_sub", pw8[:, 0:cw], pbufg[:, 0:cw],
                 mbufg[:, 0:cw], inc=cV)
            p2_state["ld"][sl] = cV.n
            P.wait("gpsimd", cV)
            P.op("gpsimd", "dma_start", out=dstp, in_=pw8[:, 0:cw],
                 inc=cG, inck=16)
            p2_state["st"] = cG.n
            p2_state["i"] += 1


    oT = alloc("oT", [128, ND * S], BF16, "left")
    pTr = alloc("pTr", [128, 2 * 1024], BF16, "right")
    wstg = alloc("wstg", [128, 2 * 2048], F32, "right")
    pbufg = alloc("pbufg", [128, 2048], BF16, "right")
    mbufg = alloc("mbufg", [128, 2048], BF16, "right")
    pw8 = alloc("pw8", [128, 2048], F8, "right")
    rdrow = alloc("rdrow", [1, 1024], F32, "right")
    rdrowb = alloc("rdrowb", [1, 1024], BF16, "right")
    rdb = alloc("rdb", [128, 1024], F32, "right")
    otmp = alloc("otmp", [64, 1024], BF16, "right")

    exp_free = [0, 0]
    score_free = [0, 0]
    pairs = [(h, qc) for h in range(H) for qc in range(2)]
    tail_prev = None

    def tail_recip(prev):
        h, qc, slot, av_mark = prev
        P.wait("vector", cT, av_mark)
        P.op("vector", "reciprocal", rdrow[:], oT_ps[slot][64:65, :])
        P.drain("vector")
        P.op("vector", "tensor_copy", rdrowb[:], rdrow[:])
        P.drain("vector")
        P.op("vector", "sem_inc", cV.sem, 1)
        cV.n += 1
        return cV.n

    def tail_bcast(prev, rd_ready):
        P.wait("tensor", cV, rd_ready)
        for j in range(2):
            P.op("tensor", "matmul", psB[:, 512 * j: 512 * (j + 1)],
                 onesb[:, 0:128], rdrowb[0:1, 512 * j: 512 * (j + 1)],
                 start=True, stop=True, inc=(cT if j == 1 else None))

    def tail_evac(prev):
        h, qc, slot, av_mark = prev
        hp = h % 2
        c = h // 2
        P.wait("vector", cT)
        P.op("vector", "tensor_copy", rdb[:, :], psB[:, :], inc=cV)
        rdb_done = cV.n
        ocols = slice(S * c + 1024 * qc, S * c + 1024 * (qc + 1))
        if hp == 0:
            P.op("vector", "tensor_mul", oT[0:64, ocols],
                 oT_ps[slot][0:64, :], rdb[0:64, :], inc=cV)
        else:
            P.op("vector", "tensor_mul", otmp[:, :],
                 oT_ps[slot][0:64, :], rdb[0:64, :], inc=cV)
            P.wait("gpsimd", cV)
            P.op("gpsimd", "dma_start", out=oT[64:128, ocols],
                 in_=otmp[:, :], inc=cG, inck=16)
        oT_evac_done[slot] = cV.n
        return rdb_done, cV.n

    oT_ps = [psC, psD]
    sc_mark = [0, 0]
    for i, (h, qc) in enumerate(pairs):
        hp = h % 2
        c = h // 2
        slot = i % 2
        rd_ready = tail_recip(tail_prev) if tail_prev is not None else None
        rdb_done = None

        def emit_av(kkm):
            sl2 = kkm % 2
            P.wait("tensor", cS, score_free[sl2])
            if kkm == 0:
                P.wait("tensor", cV, oT_evac_done[slot])
            emm(oT_ps[slot][0:65, :],
                v65[:, 1040 * kkm + 65 * h: 1040 * kkm + 65 * (h + 1)],
                pTr[:, 1024 * sl2: 1024 * sl2 + 1024],
                start=(kkm == 0), stop=(kkm == NT - 1), inc=cT)
            exp_free[sl2] = cT.n

        for kk in range(NT):
            sl = kk % 2
            ps = psA if sl == 0 else psB
            P.wait("tensor", cS, score_free[sl])
            if kk == 7 and rdb_done is not None:
                P.wait("tensor", cV, rdb_done)
            emm(ps[:, :],
                ksT[64 * hp:64 * hp + 64,
                    S * c + 128 * kk: S * c + 128 * (kk + 1)],
                qsT[64 * hp:64 * hp + 64,
                    S * c + 1024 * qc: S * c + 1024 * (qc + 1)],
                start=True, stop=True, inc=cT)
            sc_mark[sl] = cT.n
            P.wait("scalar", cT, sc_mark[sl])
            P.wait("scalar", cT, exp_free[sl])
            P.op("scalar", "activation",
                 pTr[:, 1024 * sl: 1024 * sl + 1024],
                 ps[:, :], AF.Exp, bias=negmb[:, h: h + 1], inc=cS)
            score_free[sl] = cS.n
            if kk >= 1:
                emit_av(kk - 1)
            if kk == 5 and tail_prev is not None:
                P.wait("tensor", cS, score_free[1])
                tail_bcast(tail_prev, rd_ready)
                rdb_done, _ = tail_evac(tail_prev)
        emit_av(NT - 1)
        av_mark = cT.n
        tail_prev = (h, qc, slot, av_mark)
        emit_p2(2)
    emit_p2(64)
    rd_ready = tail_recip(tail_prev)
    tail_bcast(tail_prev, rd_ready)
    tail_evac(tail_prev)
    P.wait("gpsimd", cG)
    tap("t_oT", oT[:, :], cV)
    barrier()
    pzfree("psD")
    psS = pzalloc("psS2", [128, 512], F32)
    psT = pzalloc("psT2", [128, 1024], BF16)
    free("otmp", "rdb", "rdrowb", "rdrow", "pw8", "mbufg", "pbufg", "wstg", "pTr")
    free("v65", "ksT", "qsT")

    x_mid = alloc("x_mid", [128, NT * D], F32, "right")
    WTo = alloc("WTo", [128, ND * D], BF16, "right")
    xr2 = alloc("xr2", [128, 2 * D], F32, "right")
    for ic in range(ND):
        P.op("gpsimd", "dma_start", out=WTo[:, D * ic: D * (ic + 1)],
             in_=dwo[128 * ic: 128 * (ic + 1), :], inc=cG, inck=16)
    wo_loaded = cG.n

    wo_evac = [0, 0]
    for t in range(NT):
        sl = t % 2
        if t >= 2:
            P.wait("sync", cV, wo_evac[sl])
        P.op("sync", "dma_start", out=xr2[:, D * sl: D * (sl + 1)],
             in_=dx[128 * t: 128 * (t + 1), :], inc=cL, inck=16)
        ps = psA if sl == 0 else psB
        P.wait("tensor", cG, wo_loaded)
        if t >= 2:
            P.wait("tensor", cV, wo_evac[sl])
        for ic in range(ND):
            emm(ps[:, :],
                oT[:, S * ic + 128 * t: S * ic + 128 * (t + 1)],
                WTo[:, D * ic: D * (ic + 1)],
                start=(ic == 0), stop=False)
        emm(ps[:, :], onesb[:, 0:128], borow[:],
            start=False, stop=True, inc=cT)
        P.wait("vector", cT)
        P.wait("vector", cL)
        P.op("vector", "scalar_tensor_tensor", x_mid[:, D * t: D * (t + 1)],
             ps[:, :], 1.0, xr2[:, D * sl: D * (sl + 1)], ALU.mult, ALU.add,
             inc=cV)
        wo_evac[sl] = cV.n

    tap("t_xmid", x_mid[:, :], cV)
    barrier()
    free("xr2", "WTo")
    free("oT")

    w8g = alloc("w8g", [128, ND * INNER], F8, "left")
    w8u = alloc("w8u", [128, ND * INNER], F8, "left")
    w8d = alloc("w8d", [128, NI * D], F8, "left")

    P.op("sync", "dma_start", out=w8g[:, :], in_=dscr_g, inc=cL, inck=16)
    P.op("gpsimd", "dma_start", out=w8u[:, :], in_=dscr_u, inc=cG, inck=16)
    P.op("sync", "dma_start", out=w8d[:, :], in_=dscr_d, inc=cL, inck=16)
    w8_loads_L = cL.n
    w8_loads_G = cG.n
    P.wait("vector", cL, w8_loads_L)
    P.wait("vector", cG, w8_loads_G)
    P.op("vector", "sem_inc", cV.sem, 1)
    cV.n += 1
    w8_done = cV.n

    tap("t_w8g", w8g[:, :], cV)
    tap("t_w8d", w8d[:, :], cV)
    barrier()

    tmbuf = alloc("tmbuf", [128, D], F32, "right")
    n1buf = alloc("n1buf", [128, D], I32, "right")
    r1buf = alloc("r1buf", [128, D], F32, "right")
    nabuf = alloc("nabuf", [128, D], BF16, "right")
    a1T = alloc("a1T", [128, D], BF16, "right")
    gsbuf = alloc("gsbuf", [128, 1024], F32, "right")
    hbuf = alloc("hbuf", [128, INNER], F32, "right")
    a2T = alloc("a2T", [128, 1024], BF16, "right")
    l2 = alloc("l2", [128, 20 * 16], F32, "right")
    hm4 = alloc("hm4", [128, 16], F32, "right")

    def l2c(j):
        return l2[:, 16 * j: 16 * (j + 1)]

    for t in range(NT):
        x_t = x_mid[:, D * t: D * (t + 1)]
        P.op("vector", "reduce_sum", l2c(0)[:, t: t + 1], x_t, axis=AX.X)
        P.op("vector", "scalar_tensor_tensor", tmbuf[:], x_t, 1.0, x_t,
             ALU.mult, ALU.mult, accum_out=l2c(1)[:, t: t + 1])
        P.op("vector", "reduce_max", l2c(2)[:, t: t + 1], x_t, axis=AX.X)
        P.op("vector", "tensor_reduce", l2c(3)[:, t: t + 1], x_t, axis=AX.X,
             op=ALU.min)
    P.drain("vector")
    P.op("vector", "tensor_scalar_mul", l2c(4), l2c(0), 1.0 / D)
    P.drain("vector")
    P.op("vector", "tensor_mul", l2c(5), l2c(4), l2c(4))
    P.drain("vector")
    P.op("vector", "scalar_tensor_tensor", l2c(6), l2c(1), 1.0 / D, l2c(5),
         ALU.mult, ALU.subtract)
    P.drain("vector")
    P.op("vector", "tensor_scalar_add", l2c(7), l2c(6), EPS_LN)
    P.drain("vector")
    P.op("vector", "reciprocal", l2c(8), l2c(7))
    P.drain("vector")
    P.op("vector", "sem_inc", cV.sem, 1)
    cV.n += 1
    P.wait("scalar", cV)
    P.op("scalar", "activation", l2c(9), l2c(8), AF.Sqrt)
    P.drain("scalar")
    P.op("scalar", "sem_inc", cS.sem, 1)
    cS.n += 1
    P.wait("vector", cS)
    P.op("vector", "scalar_tensor_tensor", l2c(10), l2c(4), -1.0, l2c(9),
         ALU.mult, ALU.mult)
    P.drain("vector")
    P.op("vector", "tensor_sub", l2c(11), l2c(2), l2c(4))
    P.op("vector", "tensor_sub", l2c(12), l2c(4), l2c(3))
    P.drain("vector")
    P.op("vector", "tensor_max", l2c(13), l2c(11), l2c(12))
    P.drain("vector")
    P.op("vector", "tensor_mul", l2c(14), l2c(9), l2c(13))
    P.drain("vector")
    P.op("vector", "tensor_scalar_max", l2c(15), l2c(14), 1e-5)
    P.drain("vector")
    P.op("vector", "reciprocal", l2c(16), l2c(15))
    P.drain("vector")
    P.op("vector", "tensor_scalar_mul", l2c(17), l2c(16), 127.0)
    P.drain("vector")
    P.op("vector", "tensor_mul", l2c(18), l2c(9), l2c(17))
    P.drain("vector")
    P.op("vector", "tensor_mul", l2c(19), l2c(10), l2c(17))
    P.drain("vector")
    P.op("vector", "reciprocal", l2c(11), l2c(17))
    P.drain("vector")
    P.op("vector", "tensor_scalar", l2c(12), l2c(11), mclg_b[:, 0:1], -1.0,
         ALU.mult, ALU.mult)
    P.drain("vector")
    P.op("vector", "sem_inc", cV.sem, 1)
    cV.n += 1
    ln2_done = cV.n

    a1T_free = [0, 0]
    gate_free = 0
    up_free = 0
    down_free = 0
    out_marks = [0, 0]
    na1_trans_mark = 0
    na2_trans_mark = 0
    scalar_tm_mark = 0
    for t in range(NT):
        sl = t % 2
        P.wait("scalar", cV, ln2_done)
        if t >= 1:
            P.wait("scalar", cV, scalar_tm_mark)
        P.op("scalar", "activation", tmbuf[:], x_mid[:, D * t: D * (t + 1)],
             AF.Identity, scale=l2c(18)[:, t: t + 1],
             bias=l2c(19)[:, t: t + 1], inc=cS)
        P.wait("vector", cS)
        P.op("vector", "tensor_scalar", n1buf[:], tmbuf[:], MAGIC, MAGIC,
             ALU.add, ALU.subtract)
        P.op("vector", "scalar_tensor_tensor", r1buf[:], tmbuf[:], 0.0,
             n1buf[:], ALU.add, ALU.subtract, inc=cV)
        scalar_tm_mark = cV.n
        P.op("vector", "tensor_scalar", r1buf[:], r1buf[:], MAGIC64, MAGIC64,
             ALU.add, ALU.subtract)
        if t >= 1:
            P.wait("vector", cT, na1_trans_mark)
        P.op("vector", "scalar_tensor_tensor", nabuf[:], r1buf[:], -1.0,
             n1buf[:], ALU.mult, ALU.subtract, inc=cV)
        if DEBUG_TAPS and t == 0:
            P.wait("gpsimd", cV)
            P.op("gpsimd", "dma_start", out=dtaps["t_a1"], in_=nabuf[:],
                 inc=cG, inck=16)
            P.op("gpsimd", "dma_start", out=dtaps["t_tm"], in_=tmbuf[:],
                 inc=cG, inck=16)
            P.op("gpsimd", "dma_start", out=dtaps["t_n1"], in_=n1buf[:],
                 inc=cG, inck=16)
            dbg_mark = cG.n
            P.wait("vector", cG, dbg_mark)
            P.wait("scalar", cG, dbg_mark)
            P.wait("tensor", cG, dbg_mark)
        P.wait("tensor", cV)
        if t >= 1:
            P.wait("tensor", cT, a1T_free[0])
            P.wait("tensor", cV, na2_trans_mark)
        for j in range(ND):
            P.op("tensor", "transpose", psT[:, 128 * j: 128 * (j + 1)],
                 nabuf[:, 128 * j: 128 * (j + 1)], identb[:],
                 inc=(cT if j == ND - 1 else None))
        na1_trans_mark = cT.n
        P.wait("vector", cT)
        P.op("vector", "tensor_copy", a1T[:, :], psT[:, :], inc=cV)
        a1t_ready = cV.n
        for ob in range(4):
            P.wait("tensor", cV, a1t_ready)
            if t == 0 and ob == 0:
                P.wait("tensor", cV, w8_done)
            P.wait("tensor", cS, gate_free)
            for ic in range(ND):
                emm(psA[:, :],
                    a1T[:, 128 * ic: 128 * (ic + 1)],
                    w8g[:, INNER * ic + 1024 * ob: INNER * ic + 1024 * (ob + 1)],
                    start=(ic == 0), stop=(ic == ND - 1),
                    inc=(cT if ic == ND - 1 else None))
            gate_mm = cT.n
            P.wait("tensor", cV, up_free)
            for ic in range(ND):
                emm(psB[:, :],
                    a1T[:, 128 * ic: 128 * (ic + 1)],
                    w8u[:, INNER * ic + 1024 * ob: INNER * ic + 1024 * (ob + 1)],
                    start=(ic == 0), stop=(ic == ND - 1),
                    inc=(cT if ic == ND - 1 else None))
            up_mm = cT.n
            P.wait("scalar", cT, gate_mm)
            P.op("scalar", "activation", gsbuf[:], psA[:, :], AF.Silu,
                 scale=l2c(12)[:, t: t + 1], inc=cS)
            gate_free = cS.n
            P.wait("vector", cS)
            P.wait("vector", cT, up_mm)
            P.op("vector", "tensor_mul", hbuf[:, 1024 * ob: 1024 * (ob + 1)],
                 gsbuf[:], psB[:, :])
            P.op("vector", "reduce_max", hm4[:, 4 * sl + ob: 4 * sl + ob + 1],
                 hbuf[:, 1024 * ob: 1024 * (ob + 1)], axis=AX.X,
                 apply_absolute_value=True, inc=cV)
            up_free = cV.n
        P.drain("vector")
        P.op("vector", "reduce_max", l2c(13)[:, t: t + 1],
             hm4[:, 4 * sl: 4 * sl + 4], axis=AX.X)
        P.drain("vector")
        P.op("vector", "tensor_scalar_max", l2c(14)[:, t: t + 1],
             l2c(13)[:, t: t + 1], 1e-5)
        P.drain("vector")
        P.op("vector", "reciprocal", l2c(15)[:, t: t + 1],
             l2c(14)[:, t: t + 1])
        P.drain("vector")
        P.op("vector", "tensor_scalar_mul", l2c(16)[:, t: t + 1],
             l2c(15)[:, t: t + 1], 127.0)
        P.drain("vector")
        P.op("vector", "tensor_mul", l2c(5)[:, t: t + 1],
             l2c(11)[:, t: t + 1], l2c(14)[:, t: t + 1])
        P.drain("vector")
        P.op("vector", "tensor_single_scalar", l2c(6)[:, t: t + 1],
             l2c(5)[:, t: t + 1], mclud_b[:, 0:1], ALU.mult)
        P.drain("vector")
        P.op("vector", "sem_inc", cV.sem, 1)
        cV.n += 1
        s2_done = cV.n
        for qb in range(4):
            P.wait("scalar", cV, s2_done)
            P.wait("scalar", cV, scalar_tm_mark)
            P.op("scalar", "activation", tmbuf[:],
                 hbuf[:, 1024 * qb: 1024 * (qb + 1)], AF.Identity,
                 scale=l2c(16)[:, t: t + 1], bias=magic[:, 0:1], inc=cS)
            P.wait("vector", cS)
            P.op("vector", "tensor_scalar_sub", n1buf[:], tmbuf[:], MAGIC)
            P.op("vector", "scalar_tensor_tensor", r1buf[:], tmbuf[:], -MAGIC,
                 n1buf[:], ALU.add, ALU.subtract, inc=cV)
            scalar_tm_mark = cV.n
            P.op("vector", "tensor_scalar", r1buf[:], r1buf[:], MAGIC64,
                 MAGIC64, ALU.add, ALU.subtract)
            P.wait("vector", cT, na1_trans_mark)
            P.op("vector", "scalar_tensor_tensor", nabuf[:], r1buf[:], -1.0,
                 n1buf[:], ALU.mult, ALU.subtract, inc=cV)
            na2_ready = cV.n
            P.wait("tensor", cV, na2_ready)
            P.wait("tensor", cV, na2_trans_mark)
            for j in range(ND):
                P.op("tensor", "transpose", psT[:, 128 * j: 128 * (j + 1)],
                     nabuf[:, 128 * j: 128 * (j + 1)],
                     identb[:], inc=(cT if j == ND - 1 else None))
            P.wait("vector", cT)
            na2_trans_tmark = cT.n
            qbase = 0
            P.op("vector", "tensor_copy",
                 a2T[:, qbase: qbase + 1024], psT[:, :], inc=cV)
            na2_trans_mark = cV.n
            if DEBUG_TAPS and t == 0:
                P.wait("gpsimd", cV, na2_trans_mark)
                P.op("gpsimd", "dma_start",
                     out=dtaps["t_a2T"][:, 1024 * qb: 1024 * (qb + 1)],
                     in_=a2T[:, 0:1024], inc=cG, inck=16)
                P.wait("vector", cG)
            na1_trans_mark = na2_trans_tmark
            P.wait("tensor", cV, na2_trans_mark)
            if qb == 0:
                P.wait("tensor", cV, down_free)
            for j in range(ND):
                ic = 8 * qb + j
                emm(psC[:, :],
                    a2T[:, qbase + 128 * j: qbase + 128 * (j + 1)],
                    w8d[:, D * ic: D * (ic + 1)],
                    start=(ic == 0), stop=(ic == NI - 1),
                    inc=(cT if ic == NI - 1 else None))
        if t == 0:
            tap("t_na1", nabuf[:, :], cV)
            tap("t_hbuf", hbuf[:, :], cV)
            tap("t_l2", l2[:, :], cV)
        a1T_free[0] = cT.n
        P.wait("vector", cT)
        P.op("vector", "scalar_tensor_tensor",
             x_mid[:, D * t: D * (t + 1)], psC[:, :],
             l2c(6)[:, t: t + 1], x_mid[:, D * t: D * (t + 1)],
             ALU.mult, ALU.add, inc=cV)
        down_free = cV.n
        P.wait("gpsimd", cV, down_free)
        P.op("gpsimd", "dma_start", out=dout[128 * t: 128 * (t + 1), :],
             in_=x_mid[:, D * t: D * (t + 1)], inc=cO, inck=16)

    P.wait("gpsimd", cO, cO.n)

    with nc.Block() as blk:
        @blk.sync
        def _(e):
            P.emit("sync", e)

        @blk.gpsimd
        def _(e):
            P.emit("gpsimd", e)

        @blk.tensor
        def _(e):
            P.emit("tensor", e)

        @blk.vector
        def _(e):
            P.emit("vector", e)

        @blk.scalar
        def _(e):
            P.emit("scalar", e)

    for name in list(live.keys())[::-1]:
        try:
            live.pop(name).__exit__(None, None, None)
        except Exception:
            pass
    for cm in pcm[::-1]:
        try:
            cm.__exit__(None, None, None)
        except Exception:
            pass
    for cm in semlist[::-1]:
        try:
            cm.__exit__(None, None, None)
        except Exception:
            pass
    return nc


_CACHED_NC = None


def kernel(L0_attn_x, L0_attn_Wq, L0_attn_bq, L0_attn_Wk, L0_attn_bk,
           L0_attn_Wv, L0_attn_bv, L0_attn_Wo, L0_attn_bo,
           L0_attn_norm_q_w, L0_attn_norm_k_w,
           L0_ffn_w_gate, L0_ffn_w_up, L0_ffn_w_down):
    global _CACHED_NC
    if _CACHED_NC is None:
        _CACHED_NC = build_program()
    nc = _CACHED_NC

    f32 = np.float32
    x = np.ascontiguousarray(np.asarray(L0_attn_x, dtype=f32))
    assert x.shape == (8, S, D)

    def c(a):
        return np.ascontiguousarray(np.asarray(a, dtype=f32))

    common = {
        "wqt": c(np.asarray(L0_attn_Wq, dtype=f32).T),
        "wkt": c(np.asarray(L0_attn_Wk, dtype=f32).T),
        "wvt": c(np.asarray(L0_attn_Wv, dtype=f32).T),
        "wot": c(np.asarray(L0_attn_Wo, dtype=f32).T),
        "bq8": c(np.asarray(L0_attn_bq, dtype=f32).reshape(ND, 128).T),
        "bk8": c(np.asarray(L0_attn_bk, dtype=f32).reshape(ND, 128).T),
        "bvrow": c(np.asarray(L0_attn_bv, dtype=f32).reshape(1, D)),
        "borow": c(np.asarray(L0_attn_bo, dtype=f32).reshape(1, D)),
        "nqw8": c(np.asarray(L0_attn_norm_q_w, dtype=f32).reshape(ND, 128).T),
        "nkw8": c(np.asarray(L0_attn_norm_k_w, dtype=f32).reshape(ND, 128).T),
        "nqwhd": c(np.asarray(L0_attn_norm_q_w, dtype=f32).reshape(H, HD)),
        "nkwhd": c(np.asarray(L0_attn_norm_k_w, dtype=f32).reshape(H, HD)),
        "wgT": c(np.asarray(L0_ffn_w_gate, dtype=f32).T),
        "wuT": c(np.asarray(L0_ffn_w_up, dtype=f32).T),
        "wdT": c(np.asarray(L0_ffn_w_down, dtype=f32).T),
        "identf": np.eye(128, dtype=f32),
    }
    in_maps = []
    for core in range(8):
        m = {"x": np.ascontiguousarray(x[core])}
        m.update(common)
        in_maps.append(m)

    res = run_bass_kernel_spmd(nc, in_maps, core_ids=list(range(8)))
    out = np.stack([np.asarray(r["out"], dtype=np.float32)
                    for r in res.results], axis=0)
    return out


# version 42
# speedup vs baseline: 1.0537x; 1.0182x over previous
import numpy as np
import ml_dtypes
import concourse.bass as bass
import concourse.mybir as mybir
from concourse.bass_utils import run_bass_kernel_spmd

BF16 = mybir.dt.bfloat16
F32 = mybir.dt.float32
I32 = mybir.dt.int32
F8 = mybir.dt.float8e4
AF = mybir.ActivationFunctionType
ALU = mybir.AluOpType
AX = mybir.AxisListType

MAGIC = 12582912.0
MAGIC64 = 6755399441055744.0
S, D, H, HD, INNER = 2048, 1024, 16, 64, 4096
NT = S // 128
ND = D // 128
NI = INNER // 128
EPS_LN = 1e-5
EPS_RMS = 1e-6

ENGINES = ("sync", "gpsimd", "tensor", "vector", "scalar")


class Ch:
    def __init__(self, sem):
        self.sem = sem
        self.n = 0


class Prog:
    def __init__(self):
        self.ops = {e: [] for e in ENGINES}

    def op(self, eng, meth, *args, inc=None, inck=1, **kw):
        if inc is not None:
            inc.n += inck
            sem, val = inc.sem, inck
        else:
            sem, val = None, 0
        self.ops[eng].append(("op", meth, args, kw, sem, val))

    def wait(self, eng, ch, val=None):
        v = ch.n if val is None else val
        if v > 0:
            self.ops[eng].append(("wait", ch.sem, v))

    def drain(self, eng):
        self.ops[eng].append(("drain",))

    def emit(self, eng_name, eng):
        for entry in self.ops[eng_name]:
            if entry[0] == "op":
                _, meth, args, kw, sem, val = entry
                r = getattr(eng, meth)(*args, **kw)
                if sem is not None:
                    r.then_inc(sem, val)
            elif entry[0] == "wait":
                eng.wait_ge(entry[1], entry[2])
            else:
                eng.drain()


DEBUG_TAPS = False


def build_program():
    nc = bass.Bass("TRN2", target_bir_lowering=False,
                   detect_race_conditions=False)

    dx = nc.dram_tensor("x", [S, D], F32, kind="ExternalInput").ap()
    dwq = nc.dram_tensor("wqt", [D, D], F32, kind="ExternalInput").ap()
    dwk = nc.dram_tensor("wkt", [D, D], F32, kind="ExternalInput").ap()
    dwv = nc.dram_tensor("wvt", [D, D], F32, kind="ExternalInput").ap()
    dwo = nc.dram_tensor("wot", [D, D], F32, kind="ExternalInput").ap()
    dbq = nc.dram_tensor("bq8", [128, ND], F32, kind="ExternalInput").ap()
    dbk = nc.dram_tensor("bk8", [128, ND], F32, kind="ExternalInput").ap()
    dbv = nc.dram_tensor("bvrow", [1, D], F32, kind="ExternalInput").ap()
    dbo = nc.dram_tensor("borow", [1, D], F32, kind="ExternalInput").ap()
    dnq8 = nc.dram_tensor("nqw8", [128, ND], F32, kind="ExternalInput").ap()
    dnk8 = nc.dram_tensor("nkw8", [128, ND], F32, kind="ExternalInput").ap()
    dnqh = nc.dram_tensor("nqwhd", [H, HD], F32, kind="ExternalInput").ap()
    dnkh = nc.dram_tensor("nkwhd", [H, HD], F32, kind="ExternalInput").ap()
    dwg = nc.dram_tensor("wgT", [D, INNER], F32, kind="ExternalInput").ap()
    dwu = nc.dram_tensor("wuT", [D, INNER], F32, kind="ExternalInput").ap()
    dwd = nc.dram_tensor("wdT", [INNER, D], F32, kind="ExternalInput").ap()
    didf = nc.dram_tensor("identf", [128, 128], F32, kind="ExternalInput").ap()
    dout = nc.dram_tensor("out", [S, D], F32, kind="ExternalOutput").ap()
    dscr_g = nc.dram_tensor("scr_g", [128, ND * INNER], F8, kind="Internal").ap()
    dscr_u = nc.dram_tensor("scr_u", [128, ND * INNER], F8, kind="Internal").ap()
    dscr_d = nc.dram_tensor("scr_d", [128, NI * D], F8, kind="Internal").ap()
    dtaps = {}
    if DEBUG_TAPS:
        for nm, shape, dt in (
                ("t_xn1T", [128, ND * S], F32), ("t_qsT", [128, ND * S], F32),
                ("t_ksT", [128, ND * S], F32), ("t_v65", [128, NT * H * 65], F32),
                ("t_negmb", [128, H], F32), ("t_oT", [128, ND * S], F32),
                ("t_xmid", [128, NT * D], F32), ("t_w8g", [128, ND * INNER], F32),
                ("t_w8d", [128, NI * D], F32),
                ("t_scl", [128, 16], F32), ("t_na1", [128, D], F32),
                ("t_hbuf", [128, INNER], F32), ("t_a2T", [128, INNER], F32),
                ("t_l2", [128, 320], F32), ("t_qh2s", [16, S], F32),
                ("t_a1", [128, D], F32), ("t_tm", [128, D], F32),
                ("t_n1", [128, D], I32)):
            dtaps[nm] = nc.dram_tensor(nm, shape, dt, kind="ExternalOutput").ap()

    P = Prog()
    semlist = []

    def mksem(name):
        cm = nc.semaphore(name)
        semlist.append(cm)
        return Ch(cm.__enter__())

    cL = mksem("cL")
    cG = mksem("cG")
    cT = mksem("cT")
    cV = mksem("cV")
    cS = mksem("cS")
    cO = mksem("cO")
    cB = mksem("cB")

    def barrier():
        base = cB.n
        cB.n += len(ENGINES)
        for e in ENGINES:
            P.ops[e].append(("op", "sem_inc", (cB.sem, 1), {}, None, 0))
            P.ops[e].append(("wait", cB.sem, base + len(ENGINES)))

    def tap(nm, src_ap, ch=None):
        if not DEBUG_TAPS:
            return
        if ch is not None:
            P.wait("gpsimd", ch)
        P.op("gpsimd", "dma_start", out=dtaps[nm], in_=src_ap, inc=cG,
             inck=16)

    def emm(out, lhsT, rhs, start, stop, inc=None):
        N = rhs.shape[-1]
        if N <= 512:
            P.op("tensor", "matmul", out, lhsT, rhs, start=start, stop=stop,
                 inc=inc)
            return
        for a in range(0, N, 512):
            b = min(a + 512, N)
            P.op("tensor", "matmul", out[:, a:b], lhsT, rhs[:, a:b],
                 start=start, stop=stop, inc=(inc if b == N else None))

    live = {}

    def alloc(name, shape, dtype, side):
        cm = nc.sbuf_tensor(name, shape, dtype, side=side)
        h = cm.__enter__()
        live[name] = cm
        return h

    def free(*names):
        for name in names:
            live.pop(name).__exit__(None, None, None)

    identf = alloc("c_identf", [128, 128], F32, "left")
    identb = alloc("c_identb", [128, 128], BF16, "left")
    bq8 = alloc("c_bq8", [128, ND], F32, "left")
    bk8 = alloc("c_bk8", [128, ND], F32, "left")
    nqw8 = alloc("c_nqw8", [128, ND], F32, "left")
    nkw8 = alloc("c_nkw8", [128, ND], F32, "left")
    borow = alloc("c_borow", [1, D], BF16, "left")
    magic = alloc("c_magic", [128, 1], F32, "left")
    onesb = alloc("c_onesb", [1, 128], BF16, "left")
    onesf = alloc("c_onesf", [1, 128], F32, "left")
    masks = alloc("c_masks", [128, 16 * ND], BF16, "left")
    onesfc = alloc("c_onesfc", [16, 1], F32, "left")
    negmb = alloc("c_negmb", [128, H], F32, "left")
    wmaxq = alloc("c_wmaxq", [H, 1], F32, "left")
    wmaxk = alloc("c_wmaxk", [H, 1], F32, "left")
    nmrow = alloc("c_nmrow", [1, H], F32, "left")
    sm16a = alloc("c_sm16a", [16, 1], F32, "left")
    sm16b = alloc("c_sm16b", [16, 1], F32, "left")
    sm16c = alloc("c_sm16c", [16, 1], F32, "left")
    wacc = alloc("c_wacc", [128, 96], F32, "left")
    swg_b = alloc("c_swg_b", [128, 1], F32, "left")
    mclg_b = alloc("c_mclg_b", [128, 1], F32, "left")
    swu_b = alloc("c_swu_b", [128, 1], F32, "left")
    mclu_b = alloc("c_mclu_b", [128, 1], F32, "left")
    swd_b = alloc("c_swd_b", [128, 1], F32, "left")
    mcld_b = alloc("c_mcld_b", [128, 1], F32, "left")
    mclud_b = alloc("c_mclud_b", [128, 1], F32, "left")
    sw1 = alloc("c_sw1", [1, 128], F32, "left")
    sw2 = alloc("c_sw2", [1, 8], F32, "left")
    colsum2 = alloc("c_colsum2", [128, 4], F32, "left")

    pcm = []

    def palloc(name, shape, dtype):
        cm = nc.psum_tensor(name, shape, dtype)
        pcm.append(cm)
        return cm.__enter__()

    plive = {}

    def pzalloc(name, shape, dtype):
        cm = nc.psum_tensor(name, shape, dtype)
        plive[name] = cm
        return cm.__enter__()

    def pzfree(*names):
        for name in names:
            plive.pop(name).__exit__(None, None, None)

    psA = palloc("psA", [128, 1024], F32)
    psB = palloc("psB", [128, 1024], F32)
    psC = palloc("psC", [128, 1024], F32)
    psT = pzalloc("psT", [128, 1024], BF16)
    psS = pzalloc("psS", [128, 512], F32)

    P.op("sync", "dma_start", out=identf[:], in_=didf, inc=cL, inck=16)
    P.op("sync", "dma_start", out=bq8[:], in_=dbq, inc=cL, inck=16)
    P.op("sync", "dma_start", out=bk8[:], in_=dbk, inc=cL, inck=16)
    P.op("sync", "dma_start", out=nqw8[:], in_=dnq8, inc=cL, inck=16)
    P.op("sync", "dma_start", out=nkw8[:], in_=dnk8, inc=cL, inck=16)
    P.op("gpsimd", "dma_start", out=identb[:], in_=didf, inc=cG, inck=16)
    P.op("gpsimd", "dma_start", out=borow[:], in_=dbo, inc=cG, inck=16)

    P.op("vector", "memset", magic[:], MAGIC)
    P.op("vector", "memset", onesb[:], 1.0)
    P.op("vector", "memset", onesf[:], 1.0)
    P.op("vector", "memset", masks[:], 0.0)
    P.drain("vector")
    P.op("vector", "memset", onesfc[:], 1.0)
    P.drain("vector")
    for c in range(ND):
        P.op("vector", "memset", masks[0:64, 16 * c + 2 * c: 16 * c + 2 * c + 1], 1.0)
        P.op("vector", "memset", masks[64:128, 16 * c + 2 * c + 1: 16 * c + 2 * c + 2], 1.0)
    P.drain("vector")
    P.op("vector", "sem_inc", cV.sem, 1)
    cV.n += 1
    memset_V = cV.n

    qsT = alloc("qsT", [128, ND * S], BF16, "right")
    ksT = alloc("ksT", [128, ND * S], BF16, "right")
    v65 = alloc("v65", [128, NT * H * 65], BF16, "right")
    WTa = alloc("WTa", [128, ND * D], BF16, "right")
    xn1T = alloc("xn1T", [128, ND * S], BF16, "right")
    xring = alloc("xring", [128, 2 * D], F32, "right")
    xnring = alloc("xnring", [128, 2 * D], BF16, "right")
    q2buf = alloc("q2buf", [128, S], BF16, "right")
    r_bc = alloc("r_bc", [128, S], F32, "right")
    qh2s = alloc("qh2s", [16, S], F32, "right")
    lns = alloc("lns", [128, 16 * NT], F32, "right")
    qhn = alloc("qhn", [H, S], F32, "right")
    nqwhd = alloc("s_nqwhd", [H, HD], F32, "right")
    nkwhd = alloc("s_nkwhd", [H, HD], F32, "right")
    bvrow = alloc("s_bvrow", [1, D], BF16, "right")
    wstage = alloc("wstage", [128, 3 * 1024], F32, "right")

    def lncol(t, j):
        return lns[:, 16 * t + j: 16 * t + j + 1]

    P.op("sync", "dma_start", out=nqwhd[:], in_=dnqh, inc=cL, inck=16)
    P.op("sync", "dma_start", out=nkwhd[:], in_=dnkh, inc=cL, inck=16)
    P.op("gpsimd", "dma_start", out=bvrow[:], in_=dbv, inc=cG, inck=16)
    consts_L = cL.n
    consts_G = cG.n

    P.wait("vector", cL, consts_L)
    P.op("vector", "reduce_max", wmaxq[:], nqwhd[:], axis=AX.X,
         apply_absolute_value=True)
    P.op("vector", "reduce_max", wmaxk[:], nkwhd[:], axis=AX.X,
         apply_absolute_value=True)
    P.drain("vector")
    P.op("vector", "sem_inc", cV.sem, 1)
    cV.n += 1

    pass1_srcs = []
    for tag, dr, rows, cols in (("g", dwg, D, INNER), ("u", dwu, D, INNER),
                                ("d", dwd, INNER, D)):
        for i in range(rows // 128):
            for j in range(cols // 1024):
                pass1_srcs.append(dr[128 * i:128 * (i + 1),
                                     1024 * j:1024 * (j + 1)])
    assert len(pass1_srcs) == 96
    p1_idx = [0]
    p1_marks = [0, 0, 0]

    def emit_pass1(nchunks):
        for _ in range(nchunks):
            i = p1_idx[0]
            if i >= 96:
                return
            sl = i % 3
            if i >= 3:
                P.wait("sync", cV, p1_marks[sl])
            P.op("sync", "dma_start",
                 out=wstage[:, 1024 * sl: 1024 * (sl + 1)],
                 in_=pass1_srcs[i], inc=cL, inck=16)
            P.wait("vector", cL)
            P.op("vector", "reduce_sum", wacc[:, i: i + 1],
                 wstage[:, 1024 * sl: 1024 * (sl + 1)], axis=AX.X,
                 apply_absolute_value=True, inc=cV)
            p1_marks[sl] = cV.n
            p1_idx[0] += 1

    P.wait("gpsimd", cG, consts_G)
    for ic in range(ND):
        P.op("gpsimd", "dma_start", out=WTa[:, D * ic: D * (ic + 1)],
             in_=dwq[128 * ic: 128 * (ic + 1), :], inc=cG, inck=16)
    wq_loaded = cG.n
    for ic in range(ND):
        P.op("gpsimd", "dma_start", out=v65[:, D * ic: D * (ic + 1)],
             in_=dwk[128 * ic: 128 * (ic + 1), :], inc=cG, inck=16)
    wk_loaded = cG.n

    norm_marks = [0, 0]
    trans_marks = [0, 0]
    evacT_m = [0]

    def emit_ln_tile(t):
        sl = t % 2
        if t >= 2:
            P.wait("sync", cS, norm_marks[sl])
        P.op("sync", "dma_start", out=xring[:, D * sl: D * (sl + 1)],
             in_=dx[128 * t: 128 * (t + 1), :], inc=cL, inck=16)
        x_t = xring[:, D * sl: D * (sl + 1)]
        P.wait("vector", cL)
        P.op("vector", "reduce_sum", lncol(t, 0), x_t, axis=AX.X)
        P.op("vector", "scalar_tensor_tensor", r_bc[:, 0:D], x_t, 1.0, x_t,
             ALU.mult, ALU.mult, accum_out=lncol(t, 1))
        P.drain("vector")
        P.op("vector", "tensor_scalar_mul", lncol(t, 2), lncol(t, 0), 1.0 / D)
        P.drain("vector")
        P.op("vector", "tensor_mul", lncol(t, 3), lncol(t, 2), lncol(t, 2))
        P.drain("vector")
        P.op("vector", "scalar_tensor_tensor", lncol(t, 4), lncol(t, 1),
             1.0 / D, lncol(t, 3), ALU.mult, ALU.subtract)
        P.drain("vector")
        P.op("vector", "tensor_scalar_add", lncol(t, 5), lncol(t, 4), EPS_LN)
        P.drain("vector")
        P.op("vector", "reciprocal", lncol(t, 6), lncol(t, 5))
        P.drain("vector")
        P.op("vector", "sem_inc", cV.sem, 1)
        cV.n += 1
        P.wait("scalar", cV)
        P.op("scalar", "activation", lncol(t, 7), lncol(t, 6), AF.Sqrt)
        P.drain("scalar")
        P.op("scalar", "sem_inc", cS.sem, 1)
        cS.n += 1
        P.wait("vector", cS)
        P.op("vector", "scalar_tensor_tensor", lncol(t, 8), lncol(t, 2), -1.0,
             lncol(t, 7), ALU.mult, ALU.mult)
        P.drain("vector")
        P.op("vector", "sem_inc", cV.sem, 1)
        cV.n += 1
        P.wait("scalar", cV)
        if t >= 2:
            P.wait("scalar", cT, trans_marks[sl])
        P.op("scalar", "activation", xnring[:, D * sl: D * (sl + 1)], x_t,
             AF.Identity, scale=lncol(t, 7), bias=lncol(t, 8), inc=cS)
        norm_marks[sl] = cS.n
        P.wait("tensor", cS)
        if t >= 1:
            P.wait("tensor", cV, evacT_m[0])
        for j in range(ND):
            P.op("tensor", "transpose", psT[:, 128 * j: 128 * (j + 1)],
                 xnring[:, D * sl + 128 * j: D * sl + 128 * (j + 1)],
                 identb[:], inc=(cT if j == ND - 1 else None))
        trans_marks[sl] = cT.n
        P.wait("vector", cT)
        tsrc = psT[:, 0:1024].rearrange("p (c f) -> p c f", f=128)
        tdst = xn1T[:, :].rearrange("p (c f) -> p c f", f=S)[:, :, 128 * t: 128 * (t + 1)]
        P.op("vector", "tensor_copy", tdst, tsrc, inc=cV)
        evacT_m[0] = cV.n
        emit_pass1(4)

    def qk_state(which, w_loaded_mark, wt):
        return {"which": which, "w": w_loaded_mark, "wt": wt, "wt_free": 0,
                "evac": {}, "ev_idx": {}, "qh": {0: 0, 1: 0}, "cm": [0],
                "prev": None, "idx": 0,
                "tgt": qsT if which == "q" else ksT,
                "b8": bq8 if which == "q" else bk8,
                "nw8": nqw8 if which == "q" else nkw8,
                "wmax": wmaxq if which == "q" else wmaxk,
                "bound": sm16a if which == "q" else sm16b}

    def emit_qh2(st, oc2, tc2):
        P.wait("tensor", cS, st["evac"][(oc2, tc2)])
        P.wait("tensor", cV, st["cm"][0])
        emm(psC[0:16, :],
            masks[:, 16 * oc2: 16 * oc2 + 16],
            q2buf[:, 1024 * tc2: 1024 * (tc2 + 1)],
            start=True, stop=True, inc=cT)
        st["qh"][tc2] = cT.n
        P.wait("vector", cT, st["qh"][tc2])
        dstq = qh2s[:, 1024 * tc2: 1024 * (tc2 + 1)]
        if oc2 == 0:
            P.op("vector", "tensor_copy", dstq, psC[0:16, :], inc=cV)
        else:
            P.op("vector", "scalar_tensor_tensor", dstq, dstq, 1.0,
                 psC[0:16, :], ALU.mult, ALU.add, inc=cV)
        P.drain("vector")
        st["cm"][0] = cV.n

    def emit_qk_group(st, oc, tc, wait_cv=None):
        pp = psA if st["idx"] % 2 == 0 else psB
        if wait_cv is not None:
            P.wait("tensor", cV, wait_cv)
        P.wait("tensor", cG, st["w"])
        if st["idx"] - 2 in st["ev_idx"]:
            P.wait("tensor", cS, st["ev_idx"][st["idx"] - 2])
        for ic in range(ND):
            emm(pp[:, :],
                st["wt"][:, D * ic + 128 * oc: D * ic + 128 * (oc + 1)],
                xn1T[:, S * ic + 1024 * tc: S * ic + 1024 * (tc + 1)],
                start=(ic == 0), stop=(ic == ND - 1),
                inc=(cT if ic == ND - 1 else None))
        if (oc, tc) == (ND - 1, 1):
            st["wt_free"] = cT.n
        if st["prev"] is not None:
            emit_qh2(st, *st["prev"])
        P.wait("scalar", cT)
        P.op("scalar", "activation",
             st["tgt"][:, S * oc + 1024 * tc: S * oc + 1024 * (tc + 1)],
             pp[:, :], AF.Identity, bias=st["b8"][:, oc: oc + 1])
        P.op("scalar", "activation",
             q2buf[:, 1024 * tc: 1024 * (tc + 1)],
             pp[:, :], AF.Square, bias=st["b8"][:, oc: oc + 1], inc=cS)
        st["evac"][(oc, tc)] = st["ev_idx"][st["idx"]] = cS.n
        st["prev"] = (oc, tc)
        st["idx"] += 1

    def emit_qk_tail(st):
        tgt, nw8 = st["tgt"], st["nw8"]
        emit_qh2(st, *st["prev"])
        qh2_done = st["cm"][0]
        P.wait("tensor", cV, qh2_done)
        P.wait("tensor", cS)
        for j in range(4):
            pq = psA if j < 2 else psB
            P.op("tensor", "matmul", pq[0:1, 512 * (j % 2): 512 * (j % 2) + 512],
                 onesfc[:], qh2s[:, 512 * j: 512 * (j + 1)],
                 start=True, stop=True, inc=(cT if j == 3 else None))
        P.wait("vector", cT)
        P.op("vector", "tensor_scalar_mul", qhn[0:1, 0:1024], psA[0:1, :], 1.0 / D)
        P.op("vector", "tensor_scalar_mul", qhn[0:1, 1024:2048], psB[0:1, :], 1.0 / D)
        P.drain("vector")
        P.op("vector", "tensor_scalar_add", r_bc[0:1, :], qhn[0:1, :], EPS_RMS)
        P.drain("vector")
        P.op("vector", "reciprocal", qhn[0:1, :], r_bc[0:1, :])
        P.drain("vector")
        P.op("vector", "sem_inc", cV.sem, 1)
        cV.n += 1
        P.wait("scalar", cV)
        P.op("scalar", "activation", qhn[0:1, :], qhn[0:1, :], AF.Sqrt,
             scale=(0.015625 if st["which"] == "q" else 1.0))
        P.drain("scalar")
        P.op("scalar", "sem_inc", cS.sem, 1)
        cS.n += 1
        P.wait("tensor", cS)
        for j in range(4):
            pq = psA if j < 2 else psB
            P.op("tensor", "matmul", pq[:, 512 * (j % 2): 512 * (j % 2) + 512],
                 onesf[:, 0:128], qhn[0:1, 512 * j: 512 * (j + 1)],
                 start=True, stop=True, inc=(cT if j == 3 else None))
        P.wait("vector", cT)
        P.op("vector", "tensor_copy", r_bc[:, 0:1024], psA[:, :])
        P.op("vector", "tensor_copy", r_bc[:, 1024:2048], psB[:, :], inc=cV)
        for oc in range(ND):
            P.op("vector", "scalar_tensor_tensor",
                 tgt[:, S * oc: S * (oc + 1)],
                 tgt[:, S * oc: S * (oc + 1)],
                 nw8[:, oc: oc + 1], r_bc[:, :], ALU.mult, ALU.mult,
                 inc=(cV if oc == ND - 1 else None))
        scale_done = cV.n
        P.wait("scalar", cV, qh2_done)
        P.wait("scalar", cT)
        P.op("scalar", "activation", qhn[:, :], qh2s[:, :], AF.Sqrt, inc=cS)
        P.wait("vector", cS)
        P.op("vector", "tensor_mul", qhn[:, :], qhn[:, :], r_bc[0:16, :])
        P.drain("vector")
        P.op("vector", "reduce_max", st["bound"][:], qhn[:, :], axis=AX.X)
        P.drain("vector")
        P.op("vector", "tensor_mul", st["bound"][:], st["bound"][:],
             st["wmax"][:])
        P.drain("vector")
        P.op("vector", "sem_inc", cV.sem, 1)
        cV.n += 1
        return scale_done, st["wt_free"]

    for t in range(8):
        emit_ln_tile(t)
    evac_t7 = evacT_m[0]
    stq = qk_state("q", wq_loaded, WTa)
    for g in range(ND):
        emit_qk_group(stq, g, 0, wait_cv=(evac_t7 if g == 0 else None))
        emit_ln_tile(8 + g)
    for g in range(ND):
        emit_qk_group(stq, g, 1, wait_cv=(evacT_m[0] if g == 0 else None))
    q_scale_done, wta_free = emit_qk_tail(stq)
    emit_pass1(16)
    P.wait("gpsimd", cT, wta_free)
    for ic in range(ND):
        P.op("gpsimd", "dma_start", out=WTa[:, D * ic: D * (ic + 1)],
             in_=dwv[128 * ic: 128 * (ic + 1), :], inc=cG, inck=16)
    wv_loaded = cG.n
    stk = qk_state("k", wk_loaded, v65)
    kv_wait = [cV.n]
    for tc in range(2):
        for g in range(ND):
            emit_qk_group(stk, g, tc,
                          wait_cv=(kv_wait[0] if (tc, g) == (0, 0) else None))
    k_scale_done, v65_free = emit_qk_tail(stk)
    emit_pass1(16)

    P.op("vector", "tensor_mul", sm16c[:], sm16a[:], sm16b[:])
    P.drain("vector")
    P.op("vector", "tensor_scalar_mul", sm16c[:], sm16c[:], -1.0)
    P.drain("vector")
    P.op("vector", "sem_inc", cV.sem, 1)
    cV.n += 1
    P.wait("tensor", cV)
    P.op("tensor", "transpose", psS[0:1, 0:16], sm16c[:],
         identf[0:16, 0:16], inc=cT)
    P.wait("vector", cT)
    P.op("vector", "tensor_copy", nmrow[:], psS[0:1, 0:16], inc=cV)
    P.wait("tensor", cV)
    P.op("tensor", "matmul", psS[:, 16:32], onesf[:, 0:128], nmrow[0:1, :],
         start=True, stop=True, inc=cT)
    P.wait("vector", cT)
    P.op("vector", "tensor_copy", negmb[:], psS[:, 16:32], inc=cV)

    P.wait("vector", cV, memset_V)
    P.wait("vector", cT, v65_free)
    v65r = v65[:, :].rearrange("p (t h j) -> p t h j", h=H, j=65)
    P.op("vector", "memset", v65r[:, :, :, 64:65], 1.0)
    P.drain("vector")
    P.op("vector", "sem_inc", cV.sem, 1)
    cV.n += 1
    vones_done = cV.n
    v_evac_marks = [0, 0]
    P.wait("tensor", cV)
    for t in range(NT):
        ps = psA if t % 2 == 0 else psB
        P.wait("tensor", cG, wv_loaded)
        if t >= 2:
            P.wait("tensor", cS, v_evac_marks[t % 2])
        for ic in range(ND):
            emm(ps[:, :],
                xn1T[:, S * ic + 128 * t: S * ic + 128 * (t + 1)],
                WTa[:, D * ic: D * (ic + 1)],
                start=(ic == 0), stop=False)
        emm(ps[:, :], onesb[:, 0:128], bvrow[:],
            start=False, stop=True, inc=cT)
        P.wait("scalar", cT)
        if t == 0:
            P.wait("scalar", cV, vones_done)
        dst = v65[:, 1040 * t: 1040 * (t + 1)].rearrange(
            "p (h j) -> p h j", j=65)[:, :, 0:64]
        src = ps[:, :].rearrange("p (h j) -> p h j", j=64)
        P.op("scalar", "activation", dst, src, AF.Identity, inc=cS)
        v_evac_marks[t % 2] = cS.n
        emit_pass1(2)
    emit_pass1(96)

    tap("t_xn1T", xn1T[:, :], cV)
    tap("t_qsT", qsT[:, :], cV)
    tap("t_ksT", ksT[:, :], cV)
    tap("t_v65", v65[:, :], cS)
    tap("t_negmb", negmb[:, :], cV)
    tap("t_qh2s", qh2s[:, :], cV)
    barrier()
    free("wstage", "s_bvrow", "s_nkwhd", "s_nqwhd", "qhn", "lns", "qh2s", "r_bc",
         "q2buf", "xnring", "xring", "xn1T", "WTa")
    pzfree("psS", "psT")
    psD = pzalloc("psD", [128, 1024], F32)
    oT_evac_done = [0, 0]

    for wi, (tag, c0, c1, nelem, mcl_b, sw_b) in enumerate((
            ("g", 0, 32, D * INNER, mclg_b, swg_b),
            ("u", 32, 64, D * INNER, mclu_b, swu_b),
            ("d", 64, 96, INNER * D, mcld_b, swd_b))):
        P.op("vector", "reduce_sum", colsum2[:, wi: wi + 1],
             wacc[:, c0: c1], axis=AX.X)
        P.drain("vector")
        P.op("vector", "sem_inc", cV.sem, 1)
        cV.n += 1
        P.wait("tensor", cV)
        P.op("tensor", "transpose", psD[0:1, 32:160],
             colsum2[:, wi: wi + 1], identf[:], inc=cT)
        P.wait("vector", cT)
        P.op("vector", "tensor_copy", sw1[:], psD[0:1, 32:160], inc=cV)
        P.drain("vector")
        P.op("vector", "reduce_sum", sw2[0:1, wi: wi + 1], sw1[:], axis=AX.X)
        P.drain("vector")
        P.op("vector", "tensor_scalar_mul", sw2[0:1, wi + 4: wi + 5],
             sw2[0:1, wi: wi + 1], 1.0 / nelem)
        P.drain("vector")
        P.op("vector", "tensor_scalar_max", sw2[0:1, wi: wi + 1],
             sw2[0:1, wi + 4: wi + 5], 1e-5)
        P.drain("vector")
        P.op("vector", "sem_inc", cV.sem, 1)
        cV.n += 1
        P.wait("tensor", cV)
        P.op("tensor", "matmul", psD[:, 160:161], onesf[:, 0:128],
             sw2[0:1, wi: wi + 1], start=True, stop=True, inc=cT)
        P.wait("vector", cT)
        P.op("vector", "tensor_copy", mcl_b[:], psD[:, 160:161])
        P.drain("vector")
        P.op("vector", "reciprocal", sw_b[:], mcl_b[:])
        P.drain("vector")
        P.op("vector", "sem_inc", cV.sem, 1)
        cV.n += 1
    sw_done = cV.n
    P.op("vector", "tensor_mul", mclud_b[:], mclu_b[:], mcld_b[:])
    P.drain("vector")
    P.op("vector", "tensor_scalar_mul", mclud_b[:], mclud_b[:], 1.0 / 127.0)
    P.drain("vector")
    P.op("vector", "sem_inc", cV.sem, 1)
    cV.n += 1
    oT_evac_done[1] = cV.n

    p2list = []
    for tag, dr, dst, rows, cols, cw in (("g", dwg, dscr_g, D, INNER, 2048),
                                         ("u", dwu, dscr_u, D, INNER, 2048),
                                         ("d", dwd, dscr_d, INNER, D, 1024)):
        sw_b = {"g": swg_b, "u": swu_b, "d": swd_b}[tag]
        for i2 in range(rows // 128):
            for j2 in range(cols // cw):
                p2list.append((dr[128 * i2: 128 * (i2 + 1), cw * j2: cw * (j2 + 1)],
                               dst[:, cols * i2 + cw * j2: cols * i2 + cw * (j2 + 1)],
                               sw_b, cw))
    assert len(p2list) == 64
    p2_state = {"i": 0, "ld": [0, 0], "st": 0}

    def emit_p2(n):
        for _ in range(n):
            i2 = p2_state["i"]
            if i2 >= len(p2list):
                return
            srcp, dstp, sw_b, cw = p2list[i2]
            sl = i2 % 2
            if i2 >= 2:
                P.wait("sync", cV, p2_state["ld"][sl])
            P.op("sync", "dma_start",
                 out=wstg[:, 2048 * sl: 2048 * sl + cw], in_=srcp,
                 inc=cL, inck=16)
            P.wait("vector", cL)
            if i2 == 0:
                P.wait("vector", cV, sw_done)
            if i2 >= 1:
                P.wait("vector", cG, p2_state["st"])
            ws = wstg[:, 2048 * sl: 2048 * sl + cw]
            P.op("vector", "tensor_scalar", pbufg[:, 0:cw], ws, sw_b[:, 0:1],
                 0.5, ALU.mult, ALU.is_ge)
            P.op("vector", "tensor_scalar", mbufg[:, 0:cw], ws, sw_b[:, 0:1],
                 -0.5, ALU.mult, ALU.is_le)
            P.op("vector", "tensor_sub", pw8[:, 0:cw], pbufg[:, 0:cw],
                 mbufg[:, 0:cw], inc=cV)
            p2_state["ld"][sl] = cV.n
            P.wait("gpsimd", cV)
            P.op("gpsimd", "dma_start", out=dstp, in_=pw8[:, 0:cw],
                 inc=cG, inck=16)
            p2_state["st"] = cG.n
            p2_state["i"] += 1


    oT = alloc("oT", [128, ND * S], BF16, "left")
    pTr = alloc("pTr", [128, 2 * 1024], BF16, "right")
    wstg = alloc("wstg", [128, 2 * 2048], F32, "right")
    pbufg = alloc("pbufg", [128, 2048], BF16, "right")
    mbufg = alloc("mbufg", [128, 2048], BF16, "right")
    pw8 = alloc("pw8", [128, 2048], F8, "right")
    rdrow = alloc("rdrow", [1, 1024], F32, "right")
    rdrowb = alloc("rdrowb", [1, 1024], BF16, "right")
    rdb = alloc("rdb", [128, 1024], F32, "right")
    otmp = alloc("otmp", [64, 1024], BF16, "right")

    exp_free = [0, 0]
    score_free = [0, 0]
    pairs = [(h, qc) for h in range(H) for qc in range(2)]
    tail_prev = None

    def tail_recip(prev):
        h, qc, slot, av_mark = prev
        P.wait("vector", cT, av_mark)
        P.op("vector", "reciprocal", rdrow[:], oT_ps[slot][64:65, :])
        P.drain("vector")
        P.op("vector", "tensor_copy", rdrowb[:], rdrow[:])
        P.drain("vector")
        P.op("vector", "sem_inc", cV.sem, 1)
        cV.n += 1
        return cV.n

    def tail_bcast(prev, rd_ready):
        P.wait("tensor", cV, rd_ready)
        for j in range(2):
            P.op("tensor", "matmul", psB[:, 512 * j: 512 * (j + 1)],
                 onesb[:, 0:128], rdrowb[0:1, 512 * j: 512 * (j + 1)],
                 start=True, stop=True, inc=(cT if j == 1 else None))

    def tail_evac(prev):
        h, qc, slot, av_mark = prev
        hp = h % 2
        c = h // 2
        P.wait("vector", cT)
        P.op("vector", "tensor_copy", rdb[:, :], psB[:, :], inc=cV)
        rdb_done = cV.n
        ocols = slice(S * c + 1024 * qc, S * c + 1024 * (qc + 1))
        if hp == 0:
            P.op("vector", "tensor_mul", oT[0:64, ocols],
                 oT_ps[slot][0:64, :], rdb[0:64, :], inc=cV)
        else:
            P.op("vector", "tensor_mul", otmp[:, :],
                 oT_ps[slot][0:64, :], rdb[0:64, :], inc=cV)
            P.wait("gpsimd", cV)
            P.op("gpsimd", "dma_start", out=oT[64:128, ocols],
                 in_=otmp[:, :], inc=cG, inck=16)
        oT_evac_done[slot] = cV.n
        return rdb_done, cV.n

    oT_ps = [psC, psD]
    sc_mark = [0, 0]
    for i, (h, qc) in enumerate(pairs):
        hp = h % 2
        c = h // 2
        slot = i % 2
        rd_ready = tail_recip(tail_prev) if tail_prev is not None else None
        rdb_done = None

        def emit_av(kkm):
            sl2 = kkm % 2
            P.wait("tensor", cS, score_free[sl2])
            if kkm == 0:
                P.wait("tensor", cV, oT_evac_done[slot])
            emm(oT_ps[slot][0:65, :],
                v65[:, 1040 * kkm + 65 * h: 1040 * kkm + 65 * (h + 1)],
                pTr[:, 1024 * sl2: 1024 * sl2 + 1024],
                start=(kkm == 0), stop=(kkm == NT - 1), inc=cT)
            exp_free[sl2] = cT.n

        for kk in range(NT):
            sl = kk % 2
            ps = psA if sl == 0 else psB
            P.wait("tensor", cS, score_free[sl])
            if kk == 7 and rdb_done is not None:
                P.wait("tensor", cV, rdb_done)
            emm(ps[:, :],
                ksT[64 * hp:64 * hp + 64,
                    S * c + 128 * kk: S * c + 128 * (kk + 1)],
                qsT[64 * hp:64 * hp + 64,
                    S * c + 1024 * qc: S * c + 1024 * (qc + 1)],
                start=True, stop=True, inc=cT)
            sc_mark[sl] = cT.n
            P.wait("scalar", cT, sc_mark[sl])
            P.wait("scalar", cT, exp_free[sl])
            P.op("scalar", "activation",
                 pTr[:, 1024 * sl: 1024 * sl + 1024],
                 ps[:, :], AF.Exp, bias=negmb[:, h: h + 1], inc=cS)
            score_free[sl] = cS.n
            if kk >= 1:
                emit_av(kk - 1)
            if kk == 5 and tail_prev is not None:
                P.wait("tensor", cS, score_free[1])
                tail_bcast(tail_prev, rd_ready)
                rdb_done, _ = tail_evac(tail_prev)
        emit_av(NT - 1)
        av_mark = cT.n
        tail_prev = (h, qc, slot, av_mark)
        emit_p2(2)
    emit_p2(64)
    rd_ready = tail_recip(tail_prev)
    tail_bcast(tail_prev, rd_ready)
    tail_evac(tail_prev)
    P.wait("gpsimd", cG)
    tap("t_oT", oT[:, :], cV)
    barrier()
    pzfree("psD")
    psS = pzalloc("psS2", [128, 512], F32)
    psT = pzalloc("psT2", [128, 1024], BF16)
    free("otmp", "rdb", "rdrowb", "rdrow", "pw8", "mbufg", "pbufg", "wstg", "pTr")
    free("v65", "ksT", "qsT")

    x_mid = alloc("x_mid", [128, NT * D], F32, "right")
    WTo = alloc("WTo", [128, ND * D], BF16, "right")
    xr2 = alloc("xr2", [128, 2 * D], F32, "right")
    for ic in range(ND):
        P.op("gpsimd", "dma_start", out=WTo[:, D * ic: D * (ic + 1)],
             in_=dwo[128 * ic: 128 * (ic + 1), :], inc=cG, inck=16)
    wo_loaded = cG.n

    wo_evac = [0, 0]
    for t in range(NT):
        sl = t % 2
        if t >= 2:
            P.wait("sync", cV, wo_evac[sl])
        P.op("sync", "dma_start", out=xr2[:, D * sl: D * (sl + 1)],
             in_=dx[128 * t: 128 * (t + 1), :], inc=cL, inck=16)
        ps = psA if sl == 0 else psB
        P.wait("tensor", cG, wo_loaded)
        if t >= 2:
            P.wait("tensor", cV, wo_evac[sl])
        for ic in range(ND):
            emm(ps[:, :],
                oT[:, S * ic + 128 * t: S * ic + 128 * (t + 1)],
                WTo[:, D * ic: D * (ic + 1)],
                start=(ic == 0), stop=False)
        emm(ps[:, :], onesb[:, 0:128], borow[:],
            start=False, stop=True, inc=cT)
        P.wait("vector", cT)
        P.wait("vector", cL)
        P.op("vector", "scalar_tensor_tensor", x_mid[:, D * t: D * (t + 1)],
             ps[:, :], 1.0, xr2[:, D * sl: D * (sl + 1)], ALU.mult, ALU.add,
             inc=cV)
        wo_evac[sl] = cV.n

    tap("t_xmid", x_mid[:, :], cV)
    barrier()
    free("xr2", "WTo")
    free("oT")

    w8g = alloc("w8g", [128, ND * INNER], F8, "left")
    w8u = alloc("w8u", [128, ND * INNER], F8, "left")
    w8d = alloc("w8d", [128, NI * D], F8, "left")

    P.op("sync", "dma_start", out=w8g[:, :], in_=dscr_g, inc=cL, inck=16)
    P.op("gpsimd", "dma_start", out=w8u[:, :], in_=dscr_u, inc=cG, inck=16)
    P.op("sync", "dma_start", out=w8d[:, :], in_=dscr_d, inc=cL, inck=16)
    w8_loads_L = cL.n
    w8_loads_G = cG.n
    P.wait("vector", cL, w8_loads_L)
    P.wait("vector", cG, w8_loads_G)
    P.op("vector", "sem_inc", cV.sem, 1)
    cV.n += 1
    w8_done = cV.n

    tap("t_w8g", w8g[:, :], cV)
    tap("t_w8d", w8d[:, :], cV)
    barrier()

    tmbuf = alloc("tmbuf", [128, D], F32, "right")
    n1buf = alloc("n1buf", [128, D], I32, "right")
    r1buf = alloc("r1buf", [128, D], F32, "right")
    nabuf = alloc("nabuf", [128, D], BF16, "right")
    a1T = alloc("a1T", [128, D], BF16, "right")
    gsbuf = alloc("gsbuf", [128, 1024], F32, "right")
    hbuf = alloc("hbuf", [128, INNER], F32, "right")
    a2T = alloc("a2T", [128, 1024], BF16, "right")
    l2 = alloc("l2", [128, 20 * 16], F32, "right")
    hm4 = alloc("hm4", [128, 16], F32, "right")

    def l2c(j):
        return l2[:, 16 * j: 16 * (j + 1)]

    for t in range(NT):
        x_t = x_mid[:, D * t: D * (t + 1)]
        P.op("vector", "reduce_sum", l2c(0)[:, t: t + 1], x_t, axis=AX.X)
        P.op("vector", "scalar_tensor_tensor", tmbuf[:], x_t, 1.0, x_t,
             ALU.mult, ALU.mult, accum_out=l2c(1)[:, t: t + 1])
        P.op("vector", "reduce_max", l2c(2)[:, t: t + 1], x_t, axis=AX.X)
        P.op("vector", "tensor_reduce", l2c(3)[:, t: t + 1], x_t, axis=AX.X,
             op=ALU.min)
    P.drain("vector")
    P.op("vector", "tensor_scalar_mul", l2c(4), l2c(0), 1.0 / D)
    P.drain("vector")
    P.op("vector", "tensor_mul", l2c(5), l2c(4), l2c(4))
    P.drain("vector")
    P.op("vector", "scalar_tensor_tensor", l2c(6), l2c(1), 1.0 / D, l2c(5),
         ALU.mult, ALU.subtract)
    P.drain("vector")
    P.op("vector", "tensor_scalar_add", l2c(7), l2c(6), EPS_LN)
    P.drain("vector")
    P.op("vector", "reciprocal", l2c(8), l2c(7))
    P.drain("vector")
    P.op("vector", "sem_inc", cV.sem, 1)
    cV.n += 1
    P.wait("scalar", cV)
    P.op("scalar", "activation", l2c(9), l2c(8), AF.Sqrt)
    P.drain("scalar")
    P.op("scalar", "sem_inc", cS.sem, 1)
    cS.n += 1
    P.wait("vector", cS)
    P.op("vector", "scalar_tensor_tensor", l2c(10), l2c(4), -1.0, l2c(9),
         ALU.mult, ALU.mult)
    P.drain("vector")
    P.op("vector", "tensor_sub", l2c(11), l2c(2), l2c(4))
    P.op("vector", "tensor_sub", l2c(12), l2c(4), l2c(3))
    P.drain("vector")
    P.op("vector", "tensor_max", l2c(13), l2c(11), l2c(12))
    P.drain("vector")
    P.op("vector", "tensor_mul", l2c(14), l2c(9), l2c(13))
    P.drain("vector")
    P.op("vector", "tensor_scalar_max", l2c(15), l2c(14), 1e-5)
    P.drain("vector")
    P.op("vector", "reciprocal", l2c(16), l2c(15))
    P.drain("vector")
    P.op("vector", "tensor_scalar_mul", l2c(17), l2c(16), 127.0)
    P.drain("vector")
    P.op("vector", "tensor_mul", l2c(18), l2c(9), l2c(17))
    P.drain("vector")
    P.op("vector", "tensor_mul", l2c(19), l2c(10), l2c(17))
    P.drain("vector")
    P.op("vector", "reciprocal", l2c(11), l2c(17))
    P.drain("vector")
    P.op("vector", "tensor_scalar", l2c(12), l2c(11), mclg_b[:, 0:1], -1.0,
         ALU.mult, ALU.mult)
    P.drain("vector")
    P.op("vector", "sem_inc", cV.sem, 1)
    cV.n += 1
    ln2_done = cV.n

    a1T_free = [0, 0]
    gate_free = 0
    up_free = 0
    down_free = 0
    out_marks = [0, 0]
    na1_trans_mark = 0
    na2_trans_mark = 0
    scalar_tm_mark = 0
    for t in range(NT):
        sl = t % 2
        P.wait("scalar", cV, ln2_done)
        if t >= 1:
            P.wait("scalar", cV, scalar_tm_mark)
        P.op("scalar", "activation", tmbuf[:], x_mid[:, D * t: D * (t + 1)],
             AF.Identity, scale=l2c(18)[:, t: t + 1],
             bias=l2c(19)[:, t: t + 1], inc=cS)
        P.wait("vector", cS)
        P.op("vector", "tensor_scalar", n1buf[:], tmbuf[:], MAGIC, MAGIC,
             ALU.add, ALU.subtract)
        P.op("vector", "scalar_tensor_tensor", r1buf[:], tmbuf[:], 0.0,
             n1buf[:], ALU.add, ALU.subtract, inc=cV)
        scalar_tm_mark = cV.n
        P.op("vector", "tensor_scalar", r1buf[:], r1buf[:], MAGIC64, MAGIC64,
             ALU.add, ALU.subtract)
        if t >= 1:
            P.wait("vector", cT, na1_trans_mark)
        P.op("vector", "scalar_tensor_tensor", nabuf[:], r1buf[:], -1.0,
             n1buf[:], ALU.mult, ALU.subtract, inc=cV)
        if DEBUG_TAPS and t == 0:
            P.wait("gpsimd", cV)
            P.op("gpsimd", "dma_start", out=dtaps["t_a1"], in_=nabuf[:],
                 inc=cG, inck=16)
            P.op("gpsimd", "dma_start", out=dtaps["t_tm"], in_=tmbuf[:],
                 inc=cG, inck=16)
            P.op("gpsimd", "dma_start", out=dtaps["t_n1"], in_=n1buf[:],
                 inc=cG, inck=16)
            dbg_mark = cG.n
            P.wait("vector", cG, dbg_mark)
            P.wait("scalar", cG, dbg_mark)
            P.wait("tensor", cG, dbg_mark)
        P.wait("tensor", cV)
        if t >= 1:
            P.wait("tensor", cT, a1T_free[0])
            P.wait("tensor", cV, na2_trans_mark)
        for j in range(ND):
            P.op("tensor", "transpose", psT[:, 128 * j: 128 * (j + 1)],
                 nabuf[:, 128 * j: 128 * (j + 1)], identb[:],
                 inc=(cT if j == ND - 1 else None))
        na1_trans_mark = cT.n
        P.wait("vector", cT)
        P.op("vector", "tensor_copy", a1T[:, :], psT[:, :], inc=cV)
        a1t_ready = cV.n
        for ob in range(4):
            P.wait("tensor", cV, a1t_ready)
            if t == 0 and ob == 0:
                P.wait("tensor", cV, w8_done)
            P.wait("tensor", cS, gate_free)
            for ic in range(ND):
                emm(psA[:, :],
                    a1T[:, 128 * ic: 128 * (ic + 1)],
                    w8g[:, INNER * ic + 1024 * ob: INNER * ic + 1024 * (ob + 1)],
                    start=(ic == 0), stop=(ic == ND - 1),
                    inc=(cT if ic == ND - 1 else None))
            gate_mm = cT.n
            P.wait("tensor", cV, up_free)
            for ic in range(ND):
                emm(psB[:, :],
                    a1T[:, 128 * ic: 128 * (ic + 1)],
                    w8u[:, INNER * ic + 1024 * ob: INNER * ic + 1024 * (ob + 1)],
                    start=(ic == 0), stop=(ic == ND - 1),
                    inc=(cT if ic == ND - 1 else None))
            up_mm = cT.n
            P.wait("scalar", cT, gate_mm)
            P.op("scalar", "activation", gsbuf[:], psA[:, :], AF.Silu,
                 scale=l2c(12)[:, t: t + 1], inc=cS)
            gate_free = cS.n
            P.wait("vector", cS)
            P.wait("vector", cT, up_mm)
            P.op("vector", "tensor_mul", hbuf[:, 1024 * ob: 1024 * (ob + 1)],
                 gsbuf[:], psB[:, :])
            P.op("vector", "reduce_max", hm4[:, 4 * sl + ob: 4 * sl + ob + 1],
                 hbuf[:, 1024 * ob: 1024 * (ob + 1)], axis=AX.X,
                 apply_absolute_value=True, inc=cV)
            up_free = cV.n
        P.drain("vector")
        P.op("vector", "reduce_max", l2c(13)[:, t: t + 1],
             hm4[:, 4 * sl: 4 * sl + 4], axis=AX.X)
        P.drain("vector")
        P.op("vector", "tensor_scalar_max", l2c(14)[:, t: t + 1],
             l2c(13)[:, t: t + 1], 1e-5)
        P.drain("vector")
        P.op("vector", "reciprocal", l2c(15)[:, t: t + 1],
             l2c(14)[:, t: t + 1])
        P.drain("vector")
        P.op("vector", "tensor_scalar_mul", l2c(16)[:, t: t + 1],
             l2c(15)[:, t: t + 1], 127.0)
        P.drain("vector")
        P.op("vector", "tensor_mul", l2c(5)[:, t: t + 1],
             l2c(11)[:, t: t + 1], l2c(14)[:, t: t + 1])
        P.drain("vector")
        P.op("vector", "tensor_single_scalar", l2c(6)[:, t: t + 1],
             l2c(5)[:, t: t + 1], mclud_b[:, 0:1], ALU.mult)
        P.drain("vector")
        P.op("vector", "sem_inc", cV.sem, 1)
        cV.n += 1
        s2_done = cV.n
        for qb in range(4):
            P.wait("scalar", cV, s2_done)
            P.wait("scalar", cV, scalar_tm_mark)
            P.op("scalar", "activation", tmbuf[:],
                 hbuf[:, 1024 * qb: 1024 * (qb + 1)], AF.Identity,
                 scale=l2c(16)[:, t: t + 1], bias=magic[:, 0:1], inc=cS)
            P.wait("vector", cS)
            P.op("vector", "tensor_scalar_sub", n1buf[:], tmbuf[:], MAGIC)
            P.op("vector", "scalar_tensor_tensor", r1buf[:], tmbuf[:], -MAGIC,
                 n1buf[:], ALU.add, ALU.subtract, inc=cV)
            scalar_tm_mark = cV.n
            P.op("vector", "tensor_scalar", r1buf[:], r1buf[:], MAGIC64,
                 MAGIC64, ALU.add, ALU.subtract)
            P.wait("vector", cT, na1_trans_mark)
            P.op("vector", "scalar_tensor_tensor", nabuf[:], r1buf[:], -1.0,
                 n1buf[:], ALU.mult, ALU.subtract, inc=cV)
            na2_ready = cV.n
            P.wait("tensor", cV, na2_ready)
            P.wait("tensor", cV, na2_trans_mark)
            for j in range(ND):
                P.op("tensor", "transpose", psT[:, 128 * j: 128 * (j + 1)],
                     nabuf[:, 128 * j: 128 * (j + 1)],
                     identb[:], inc=(cT if j == ND - 1 else None))
            P.wait("vector", cT)
            na2_trans_tmark = cT.n
            qbase = 0
            P.op("vector", "tensor_copy",
                 a2T[:, qbase: qbase + 1024], psT[:, :], inc=cV)
            na2_trans_mark = cV.n
            if DEBUG_TAPS and t == 0:
                P.wait("gpsimd", cV, na2_trans_mark)
                P.op("gpsimd", "dma_start",
                     out=dtaps["t_a2T"][:, 1024 * qb: 1024 * (qb + 1)],
                     in_=a2T[:, 0:1024], inc=cG, inck=16)
                P.wait("vector", cG)
            na1_trans_mark = na2_trans_tmark
            P.wait("tensor", cV, na2_trans_mark)
            if qb == 0:
                P.wait("tensor", cV, down_free)
            for j in range(ND):
                ic = 8 * qb + j
                emm(psC[:, :],
                    a2T[:, qbase + 128 * j: qbase + 128 * (j + 1)],
                    w8d[:, D * ic: D * (ic + 1)],
                    start=(ic == 0), stop=(ic == NI - 1),
                    inc=(cT if ic == NI - 1 else None))
        if t == 0:
            tap("t_na1", nabuf[:, :], cV)
            tap("t_hbuf", hbuf[:, :], cV)
            tap("t_l2", l2[:, :], cV)
        a1T_free[0] = cT.n
        P.wait("vector", cT)
        P.op("vector", "scalar_tensor_tensor",
             x_mid[:, D * t: D * (t + 1)], psC[:, :],
             l2c(6)[:, t: t + 1], x_mid[:, D * t: D * (t + 1)],
             ALU.mult, ALU.add, inc=cV)
        down_free = cV.n
        P.wait("gpsimd", cV, down_free)
        P.op("gpsimd", "dma_start", out=dout[128 * t: 128 * (t + 1), :],
             in_=x_mid[:, D * t: D * (t + 1)], inc=cO, inck=16)

    P.wait("gpsimd", cO, cO.n)

    with nc.Block() as blk:
        @blk.sync
        def _(e):
            P.emit("sync", e)

        @blk.gpsimd
        def _(e):
            P.emit("gpsimd", e)

        @blk.tensor
        def _(e):
            P.emit("tensor", e)

        @blk.vector
        def _(e):
            P.emit("vector", e)

        @blk.scalar
        def _(e):
            P.emit("scalar", e)

    for name in list(live.keys())[::-1]:
        try:
            live.pop(name).__exit__(None, None, None)
        except Exception:
            pass
    for cm in pcm[::-1]:
        try:
            cm.__exit__(None, None, None)
        except Exception:
            pass
    for cm in semlist[::-1]:
        try:
            cm.__exit__(None, None, None)
        except Exception:
            pass
    return nc


_CACHED_NC = None


def kernel(L0_attn_x, L0_attn_Wq, L0_attn_bq, L0_attn_Wk, L0_attn_bk,
           L0_attn_Wv, L0_attn_bv, L0_attn_Wo, L0_attn_bo,
           L0_attn_norm_q_w, L0_attn_norm_k_w,
           L0_ffn_w_gate, L0_ffn_w_up, L0_ffn_w_down):
    global _CACHED_NC
    if _CACHED_NC is None:
        _CACHED_NC = build_program()
    nc = _CACHED_NC

    f32 = np.float32
    x = np.ascontiguousarray(np.asarray(L0_attn_x, dtype=f32))
    assert x.shape == (8, S, D)

    def c(a):
        return np.ascontiguousarray(np.asarray(a, dtype=f32))

    common = {
        "wqt": c(np.asarray(L0_attn_Wq, dtype=f32).T),
        "wkt": c(np.asarray(L0_attn_Wk, dtype=f32).T),
        "wvt": c(np.asarray(L0_attn_Wv, dtype=f32).T),
        "wot": c(np.asarray(L0_attn_Wo, dtype=f32).T),
        "bq8": c(np.asarray(L0_attn_bq, dtype=f32).reshape(ND, 128).T),
        "bk8": c(np.asarray(L0_attn_bk, dtype=f32).reshape(ND, 128).T),
        "bvrow": c(np.asarray(L0_attn_bv, dtype=f32).reshape(1, D)),
        "borow": c(np.asarray(L0_attn_bo, dtype=f32).reshape(1, D)),
        "nqw8": c(np.asarray(L0_attn_norm_q_w, dtype=f32).reshape(ND, 128).T),
        "nkw8": c(np.asarray(L0_attn_norm_k_w, dtype=f32).reshape(ND, 128).T),
        "nqwhd": c(np.asarray(L0_attn_norm_q_w, dtype=f32).reshape(H, HD)),
        "nkwhd": c(np.asarray(L0_attn_norm_k_w, dtype=f32).reshape(H, HD)),
        "wgT": c(np.asarray(L0_ffn_w_gate, dtype=f32).T),
        "wuT": c(np.asarray(L0_ffn_w_up, dtype=f32).T),
        "wdT": c(np.asarray(L0_ffn_w_down, dtype=f32).T),
        "identf": np.eye(128, dtype=f32),
    }
    in_maps = []
    for core in range(8):
        m = {"x": np.ascontiguousarray(x[core])}
        m.update(common)
        in_maps.append(m)

    res = run_bass_kernel_spmd(nc, in_maps, core_ids=list(range(8)))
    out = np.stack([np.asarray(r["out"], dtype=np.float32)
                    for r in res.results], axis=0)
    return out


# version 43
# speedup vs baseline: 1.0763x; 1.0214x over previous
import numpy as np
import ml_dtypes
import concourse.bass as bass
import concourse.mybir as mybir
from concourse.bass_utils import run_bass_kernel_spmd

BF16 = mybir.dt.bfloat16
F32 = mybir.dt.float32
I32 = mybir.dt.int32
F8 = mybir.dt.float8e4
AF = mybir.ActivationFunctionType
ALU = mybir.AluOpType
AX = mybir.AxisListType

MAGIC = 12582912.0
MAGIC64 = 6755399441055744.0
S, D, H, HD, INNER = 2048, 1024, 16, 64, 4096
NT = S // 128
ND = D // 128
NI = INNER // 128
EPS_LN = 1e-5
EPS_RMS = 1e-6

ENGINES = ("sync", "gpsimd", "tensor", "vector", "scalar")


class Ch:
    def __init__(self, sem):
        self.sem = sem
        self.n = 0


class Prog:
    def __init__(self):
        self.ops = {e: [] for e in ENGINES}

    def op(self, eng, meth, *args, inc=None, inck=1, **kw):
        if inc is not None:
            inc.n += inck
            sem, val = inc.sem, inck
        else:
            sem, val = None, 0
        self.ops[eng].append(("op", meth, args, kw, sem, val))

    def wait(self, eng, ch, val=None):
        v = ch.n if val is None else val
        if v > 0:
            self.ops[eng].append(("wait", ch.sem, v))

    def drain(self, eng):
        self.ops[eng].append(("drain",))

    def emit(self, eng_name, eng):
        for entry in self.ops[eng_name]:
            if entry[0] == "op":
                _, meth, args, kw, sem, val = entry
                r = getattr(eng, meth)(*args, **kw)
                if sem is not None:
                    r.then_inc(sem, val)
            elif entry[0] == "wait":
                eng.wait_ge(entry[1], entry[2])
            else:
                eng.drain()


DEBUG_TAPS = False


def build_program():
    nc = bass.Bass("TRN2", target_bir_lowering=False,
                   detect_race_conditions=False)

    dx = nc.dram_tensor("x", [S, D], F32, kind="ExternalInput").ap()
    dwq = nc.dram_tensor("wqt", [D, D], F32, kind="ExternalInput").ap()
    dwk = nc.dram_tensor("wkt", [D, D], F32, kind="ExternalInput").ap()
    dwv = nc.dram_tensor("wvt", [D, D], F32, kind="ExternalInput").ap()
    dwo = nc.dram_tensor("wot", [D, D], F32, kind="ExternalInput").ap()
    dbq = nc.dram_tensor("bq8", [128, ND], F32, kind="ExternalInput").ap()
    dbk = nc.dram_tensor("bk8", [128, ND], F32, kind="ExternalInput").ap()
    dbv = nc.dram_tensor("bvrow", [1, D], F32, kind="ExternalInput").ap()
    dbo = nc.dram_tensor("borow", [1, D], F32, kind="ExternalInput").ap()
    dnq8 = nc.dram_tensor("nqw8", [128, ND], F32, kind="ExternalInput").ap()
    dnk8 = nc.dram_tensor("nkw8", [128, ND], F32, kind="ExternalInput").ap()
    dnqh = nc.dram_tensor("nqwhd", [H, HD], F32, kind="ExternalInput").ap()
    dnkh = nc.dram_tensor("nkwhd", [H, HD], F32, kind="ExternalInput").ap()
    dwg = nc.dram_tensor("wgT", [D, INNER], F32, kind="ExternalInput").ap()
    dwu = nc.dram_tensor("wuT", [D, INNER], F32, kind="ExternalInput").ap()
    dwd = nc.dram_tensor("wdT", [INNER, D], F32, kind="ExternalInput").ap()
    didf = nc.dram_tensor("identf", [128, 128], F32, kind="ExternalInput").ap()
    dout = nc.dram_tensor("out", [S, D], F32, kind="ExternalOutput").ap()
    dscr_g = nc.dram_tensor("scr_g", [128, ND * INNER], F8, kind="Internal").ap()
    dscr_u = nc.dram_tensor("scr_u", [128, ND * INNER], F8, kind="Internal").ap()
    dscr_d = nc.dram_tensor("scr_d", [128, NI * D], F8, kind="Internal").ap()
    dtaps = {}
    if DEBUG_TAPS:
        for nm, shape, dt in (
                ("t_xn1T", [128, ND * S], F32), ("t_qsT", [128, ND * S], F32),
                ("t_ksT", [128, ND * S], F32), ("t_v65", [128, NT * H * 65], F32),
                ("t_negmb", [128, H], F32), ("t_oT", [128, ND * S], F32),
                ("t_xmid", [128, NT * D], F32), ("t_w8g", [128, ND * INNER], F32),
                ("t_w8d", [128, NI * D], F32),
                ("t_scl", [128, 16], F32), ("t_na1", [128, D], F32),
                ("t_hbuf", [128, INNER], F32), ("t_a2T", [128, INNER], F32),
                ("t_l2", [128, 320], F32), ("t_qh2s", [16, S], F32),
                ("t_a1", [128, D], F32), ("t_tm", [128, D], F32),
                ("t_n1", [128, D], I32)):
            dtaps[nm] = nc.dram_tensor(nm, shape, dt, kind="ExternalOutput").ap()

    P = Prog()
    semlist = []

    def mksem(name):
        cm = nc.semaphore(name)
        semlist.append(cm)
        return Ch(cm.__enter__())

    cL = mksem("cL")
    cG = mksem("cG")
    cT = mksem("cT")
    cV = mksem("cV")
    cS = mksem("cS")
    cO = mksem("cO")
    cB = mksem("cB")

    def barrier():
        base = cB.n
        cB.n += len(ENGINES)
        for e in ENGINES:
            P.ops[e].append(("op", "sem_inc", (cB.sem, 1), {}, None, 0))
            P.ops[e].append(("wait", cB.sem, base + len(ENGINES)))

    def tap(nm, src_ap, ch=None):
        if not DEBUG_TAPS:
            return
        if ch is not None:
            P.wait("gpsimd", ch)
        P.op("gpsimd", "dma_start", out=dtaps[nm], in_=src_ap, inc=cG,
             inck=16)

    def emm(out, lhsT, rhs, start, stop, inc=None):
        N = rhs.shape[-1]
        if N <= 512:
            P.op("tensor", "matmul", out, lhsT, rhs, start=start, stop=stop,
                 inc=inc)
            return
        for a in range(0, N, 512):
            b = min(a + 512, N)
            P.op("tensor", "matmul", out[:, a:b], lhsT, rhs[:, a:b],
                 start=start, stop=stop, inc=(inc if b == N else None))

    live = {}

    def alloc(name, shape, dtype, side):
        cm = nc.sbuf_tensor(name, shape, dtype, side=side)
        h = cm.__enter__()
        live[name] = cm
        return h

    def free(*names):
        for name in names:
            live.pop(name).__exit__(None, None, None)

    identf = alloc("c_identf", [128, 128], F32, "left")
    identb = alloc("c_identb", [128, 128], BF16, "left")
    bq8 = alloc("c_bq8", [128, ND], F32, "left")
    bk8 = alloc("c_bk8", [128, ND], F32, "left")
    nqw8 = alloc("c_nqw8", [128, ND], F32, "left")
    nkw8 = alloc("c_nkw8", [128, ND], F32, "left")
    borow = alloc("c_borow", [1, D], BF16, "left")
    magic = alloc("c_magic", [128, 1], F32, "left")
    onesb = alloc("c_onesb", [1, 128], BF16, "left")
    onesf = alloc("c_onesf", [1, 128], F32, "left")
    masks = alloc("c_masks", [128, 16 * ND], BF16, "left")
    onesfc = alloc("c_onesfc", [16, 1], F32, "left")
    negmb = alloc("c_negmb", [128, H], F32, "left")
    wmaxq = alloc("c_wmaxq", [H, 1], F32, "left")
    wmaxk = alloc("c_wmaxk", [H, 1], F32, "left")
    nmrow = alloc("c_nmrow", [1, H], F32, "left")
    sm16a = alloc("c_sm16a", [16, 1], F32, "left")
    sm16b = alloc("c_sm16b", [16, 1], F32, "left")
    sm16c = alloc("c_sm16c", [16, 1], F32, "left")
    wacc = alloc("c_wacc", [128, 96], F32, "left")
    swg_b = alloc("c_swg_b", [128, 1], F32, "left")
    mclg_b = alloc("c_mclg_b", [128, 1], F32, "left")
    swu_b = alloc("c_swu_b", [128, 1], F32, "left")
    mclu_b = alloc("c_mclu_b", [128, 1], F32, "left")
    swd_b = alloc("c_swd_b", [128, 1], F32, "left")
    mcld_b = alloc("c_mcld_b", [128, 1], F32, "left")
    mclud_b = alloc("c_mclud_b", [128, 1], F32, "left")
    sw1 = alloc("c_sw1", [1, 128], F32, "left")
    sw2 = alloc("c_sw2", [1, 8], F32, "left")
    colsum2 = alloc("c_colsum2", [128, 4], F32, "left")

    pcm = []

    def palloc(name, shape, dtype):
        cm = nc.psum_tensor(name, shape, dtype)
        pcm.append(cm)
        return cm.__enter__()

    plive = {}

    def pzalloc(name, shape, dtype):
        cm = nc.psum_tensor(name, shape, dtype)
        plive[name] = cm
        return cm.__enter__()

    def pzfree(*names):
        for name in names:
            plive.pop(name).__exit__(None, None, None)

    psA = palloc("psA", [128, 1024], F32)
    psB = palloc("psB", [128, 1024], F32)
    psC = palloc("psC", [128, 1024], F32)
    psT = pzalloc("psT", [128, 1024], BF16)
    psS = pzalloc("psS", [128, 512], F32)

    P.op("sync", "dma_start", out=identf[:], in_=didf, inc=cL, inck=16)
    P.op("sync", "dma_start", out=bq8[:], in_=dbq, inc=cL, inck=16)
    P.op("sync", "dma_start", out=bk8[:], in_=dbk, inc=cL, inck=16)
    P.op("sync", "dma_start", out=nqw8[:], in_=dnq8, inc=cL, inck=16)
    P.op("sync", "dma_start", out=nkw8[:], in_=dnk8, inc=cL, inck=16)
    P.op("gpsimd", "dma_start", out=identb[:], in_=didf, inc=cG, inck=16)
    P.op("gpsimd", "dma_start", out=borow[:], in_=dbo, inc=cG, inck=16)

    P.op("vector", "memset", magic[:], MAGIC)
    P.op("vector", "memset", onesb[:], 1.0)
    P.op("vector", "memset", onesf[:], 1.0)
    P.op("vector", "memset", masks[:], 0.0)
    P.drain("vector")
    P.op("vector", "memset", onesfc[:], 1.0)
    P.drain("vector")
    for c in range(ND):
        P.op("vector", "memset", masks[0:64, 16 * c + 2 * c: 16 * c + 2 * c + 1], 1.0)
        P.op("vector", "memset", masks[64:128, 16 * c + 2 * c + 1: 16 * c + 2 * c + 2], 1.0)
    P.drain("vector")
    P.op("vector", "sem_inc", cV.sem, 1)
    cV.n += 1
    memset_V = cV.n

    qsT = alloc("qsT", [128, ND * S], BF16, "right")
    ksT = alloc("ksT", [128, ND * S], BF16, "right")
    v65 = alloc("v65", [128, NT * H * 65], BF16, "right")
    WTa = alloc("WTa", [128, ND * D], BF16, "right")
    xn1T = alloc("xn1T", [128, ND * S], BF16, "right")
    xring = alloc("xring", [128, 2 * D], F32, "right")
    xnring = alloc("xnring", [128, 2 * D], BF16, "right")
    q2buf = alloc("q2buf", [128, S], BF16, "right")
    r_bc = alloc("r_bc", [128, S], F32, "right")
    qh2s = alloc("qh2s", [16, S], F32, "right")
    lns = alloc("lns", [128, 16 * NT], F32, "right")
    qhn = alloc("qhn", [H, S], F32, "right")
    nqwhd = alloc("s_nqwhd", [H, HD], F32, "right")
    nkwhd = alloc("s_nkwhd", [H, HD], F32, "right")
    bvrow = alloc("s_bvrow", [1, D], BF16, "right")
    wstage = alloc("wstage", [128, 3 * 1024], F32, "right")

    def lncol(t, j):
        return lns[:, 16 * t + j: 16 * t + j + 1]

    P.op("sync", "dma_start", out=nqwhd[:], in_=dnqh, inc=cL, inck=16)
    P.op("sync", "dma_start", out=nkwhd[:], in_=dnkh, inc=cL, inck=16)
    P.op("gpsimd", "dma_start", out=bvrow[:], in_=dbv, inc=cG, inck=16)
    consts_L = cL.n
    consts_G = cG.n

    P.wait("vector", cL, consts_L)
    P.op("vector", "reduce_max", wmaxq[:], nqwhd[:], axis=AX.X,
         apply_absolute_value=True)
    P.op("vector", "reduce_max", wmaxk[:], nkwhd[:], axis=AX.X,
         apply_absolute_value=True)
    P.drain("vector")
    P.op("vector", "sem_inc", cV.sem, 1)
    cV.n += 1

    pass1_srcs = []
    for tag, dr, rows, cols in (("g", dwg, D, INNER), ("u", dwu, D, INNER),
                                ("d", dwd, INNER, D)):
        for i in range(rows // 128):
            for j in range(cols // 1024):
                pass1_srcs.append(dr[128 * i:128 * (i + 1),
                                     1024 * j:1024 * (j + 1)])
    assert len(pass1_srcs) == 96
    p1_idx = [0]
    p1_marks = [0, 0, 0]

    def emit_pass1(nchunks):
        for _ in range(nchunks):
            i = p1_idx[0]
            if i >= 96:
                return
            sl = i % 3
            if i >= 3:
                P.wait("sync", cV, p1_marks[sl])
            P.op("sync", "dma_start",
                 out=wstage[:, 1024 * sl: 1024 * (sl + 1)],
                 in_=pass1_srcs[i], inc=cL, inck=16)
            P.wait("vector", cL)
            P.op("vector", "reduce_sum", wacc[:, i: i + 1],
                 wstage[:, 1024 * sl: 1024 * (sl + 1)], axis=AX.X,
                 apply_absolute_value=True, inc=cV)
            p1_marks[sl] = cV.n
            p1_idx[0] += 1

    P.wait("gpsimd", cG, consts_G)
    for ic in range(ND):
        P.op("gpsimd", "dma_start", out=WTa[:, D * ic: D * (ic + 1)],
             in_=dwq[128 * ic: 128 * (ic + 1), :], inc=cG, inck=16)
    wq_loaded = cG.n
    for ic in range(ND):
        P.op("gpsimd", "dma_start", out=v65[:, D * ic: D * (ic + 1)],
             in_=dwk[128 * ic: 128 * (ic + 1), :], inc=cG, inck=16)
    wk_loaded = cG.n

    norm_marks = [0, 0]
    trans_marks = [0, 0]
    evacT_m = [0]

    def emit_ln_tile(t):
        sl = t % 2
        if t >= 2:
            P.wait("sync", cS, norm_marks[sl])
        P.op("sync", "dma_start", out=xring[:, D * sl: D * (sl + 1)],
             in_=dx[128 * t: 128 * (t + 1), :], inc=cL, inck=16)
        x_t = xring[:, D * sl: D * (sl + 1)]
        P.wait("vector", cL)
        P.op("vector", "reduce_sum", lncol(t, 0), x_t, axis=AX.X)
        P.op("vector", "scalar_tensor_tensor", r_bc[:, 0:D], x_t, 1.0, x_t,
             ALU.mult, ALU.mult, accum_out=lncol(t, 1))
        P.drain("vector")
        P.op("vector", "tensor_scalar_mul", lncol(t, 2), lncol(t, 0), 1.0 / D)
        P.drain("vector")
        P.op("vector", "tensor_mul", lncol(t, 3), lncol(t, 2), lncol(t, 2))
        P.drain("vector")
        P.op("vector", "scalar_tensor_tensor", lncol(t, 4), lncol(t, 1),
             1.0 / D, lncol(t, 3), ALU.mult, ALU.subtract)
        P.drain("vector")
        P.op("vector", "tensor_scalar_add", lncol(t, 5), lncol(t, 4), EPS_LN)
        P.drain("vector")
        P.op("vector", "reciprocal", lncol(t, 6), lncol(t, 5))
        P.drain("vector")
        P.op("vector", "sem_inc", cV.sem, 1)
        cV.n += 1
        P.wait("scalar", cV)
        P.op("scalar", "activation", lncol(t, 7), lncol(t, 6), AF.Sqrt)
        P.drain("scalar")
        P.op("scalar", "sem_inc", cS.sem, 1)
        cS.n += 1
        P.wait("vector", cS)
        P.op("vector", "scalar_tensor_tensor", lncol(t, 8), lncol(t, 2), -1.0,
             lncol(t, 7), ALU.mult, ALU.mult)
        P.drain("vector")
        P.op("vector", "sem_inc", cV.sem, 1)
        cV.n += 1
        P.wait("scalar", cV)
        if t >= 2:
            P.wait("scalar", cT, trans_marks[sl])
        P.op("scalar", "activation", xnring[:, D * sl: D * (sl + 1)], x_t,
             AF.Identity, scale=lncol(t, 7), bias=lncol(t, 8), inc=cS)
        norm_marks[sl] = cS.n
        P.wait("tensor", cS)
        if t >= 1:
            P.wait("tensor", cV, evacT_m[0])
        for j in range(ND):
            P.op("tensor", "transpose", psT[:, 128 * j: 128 * (j + 1)],
                 xnring[:, D * sl + 128 * j: D * sl + 128 * (j + 1)],
                 identb[:], inc=(cT if j == ND - 1 else None))
        trans_marks[sl] = cT.n
        P.wait("vector", cT)
        tsrc = psT[:, 0:1024].rearrange("p (c f) -> p c f", f=128)
        tdst = xn1T[:, :].rearrange("p (c f) -> p c f", f=S)[:, :, 128 * t: 128 * (t + 1)]
        P.op("vector", "tensor_copy", tdst, tsrc, inc=cV)
        evacT_m[0] = cV.n
        emit_pass1(4)

    def qk_state(which, w_loaded_mark, wt):
        return {"which": which, "w": w_loaded_mark, "wt": wt, "wt_free": 0,
                "evac": {}, "ev_idx": {}, "qh": {0: 0, 1: 0}, "cm": [0],
                "prev": None, "idx": 0,
                "tgt": qsT if which == "q" else ksT,
                "b8": bq8 if which == "q" else bk8,
                "nw8": nqw8 if which == "q" else nkw8,
                "wmax": wmaxq if which == "q" else wmaxk,
                "bound": sm16a if which == "q" else sm16b}

    def emit_qh2(st, oc2, tc2):
        P.wait("tensor", cS, st["evac"][(oc2, tc2)])
        P.wait("tensor", cV, st["cm"][0])
        emm(psC[0:16, :],
            masks[:, 16 * oc2: 16 * oc2 + 16],
            q2buf[:, 1024 * tc2: 1024 * (tc2 + 1)],
            start=True, stop=True, inc=cT)
        st["qh"][tc2] = cT.n
        P.wait("vector", cT, st["qh"][tc2])
        dstq = qh2s[:, 1024 * tc2: 1024 * (tc2 + 1)]
        if oc2 == 0:
            P.op("vector", "tensor_copy", dstq, psC[0:16, :], inc=cV)
        else:
            P.op("vector", "scalar_tensor_tensor", dstq, dstq, 1.0,
                 psC[0:16, :], ALU.mult, ALU.add, inc=cV)
        P.drain("vector")
        st["cm"][0] = cV.n

    def emit_qk_group(st, oc, tc, wait_cv=None):
        pp = psA if st["idx"] % 2 == 0 else psB
        if wait_cv is not None:
            P.wait("tensor", cV, wait_cv)
        P.wait("tensor", cG, st["w"])
        if st["idx"] - 2 in st["ev_idx"]:
            P.wait("tensor", cS, st["ev_idx"][st["idx"] - 2])
        for ic in range(ND):
            emm(pp[:, :],
                st["wt"][:, D * ic + 128 * oc: D * ic + 128 * (oc + 1)],
                xn1T[:, S * ic + 1024 * tc: S * ic + 1024 * (tc + 1)],
                start=(ic == 0), stop=(ic == ND - 1),
                inc=(cT if ic == ND - 1 else None))
        if (oc, tc) == (ND - 1, 1):
            st["wt_free"] = cT.n
        if st["prev"] is not None:
            emit_qh2(st, *st["prev"])
        P.wait("scalar", cT)
        P.op("scalar", "activation",
             st["tgt"][:, S * oc + 1024 * tc: S * oc + 1024 * (tc + 1)],
             pp[:, :], AF.Identity, bias=st["b8"][:, oc: oc + 1])
        P.op("scalar", "activation",
             q2buf[:, 1024 * tc: 1024 * (tc + 1)],
             pp[:, :], AF.Square, bias=st["b8"][:, oc: oc + 1], inc=cS)
        st["evac"][(oc, tc)] = st["ev_idx"][st["idx"]] = cS.n
        st["prev"] = (oc, tc)
        st["idx"] += 1

    def emit_qk_tail(st):
        tgt, nw8 = st["tgt"], st["nw8"]
        emit_qh2(st, *st["prev"])
        qh2_done = st["cm"][0]
        P.wait("tensor", cV, qh2_done)
        P.wait("tensor", cS)
        for j in range(4):
            pq = psA if j < 2 else psB
            P.op("tensor", "matmul", pq[0:1, 512 * (j % 2): 512 * (j % 2) + 512],
                 onesfc[:], qh2s[:, 512 * j: 512 * (j + 1)],
                 start=True, stop=True, inc=(cT if j == 3 else None))
        P.wait("vector", cT)
        P.op("vector", "tensor_scalar_mul", qhn[0:1, 0:1024], psA[0:1, :], 1.0 / D)
        P.op("vector", "tensor_scalar_mul", qhn[0:1, 1024:2048], psB[0:1, :], 1.0 / D)
        P.drain("vector")
        P.op("vector", "tensor_scalar_add", r_bc[0:1, :], qhn[0:1, :], EPS_RMS)
        P.drain("vector")
        P.op("vector", "reciprocal", qhn[0:1, :], r_bc[0:1, :])
        P.drain("vector")
        P.op("vector", "sem_inc", cV.sem, 1)
        cV.n += 1
        P.wait("scalar", cV)
        P.op("scalar", "activation", qhn[0:1, :], qhn[0:1, :], AF.Sqrt,
             scale=(0.015625 if st["which"] == "q" else 1.0))
        P.drain("scalar")
        P.op("scalar", "sem_inc", cS.sem, 1)
        cS.n += 1
        P.wait("tensor", cS)
        for j in range(4):
            pq = psA if j < 2 else psB
            P.op("tensor", "matmul", pq[:, 512 * (j % 2): 512 * (j % 2) + 512],
                 onesf[:, 0:128], qhn[0:1, 512 * j: 512 * (j + 1)],
                 start=True, stop=True, inc=(cT if j == 3 else None))
        P.wait("vector", cT)
        P.op("vector", "tensor_copy", r_bc[:, 0:1024], psA[:, :])
        P.op("vector", "tensor_copy", r_bc[:, 1024:2048], psB[:, :], inc=cV)
        st["rbc"] = cV.n
        for oc in range(ND):
            P.op("vector", "scalar_tensor_tensor",
                 tgt[:, S * oc: S * (oc + 1)],
                 tgt[:, S * oc: S * (oc + 1)],
                 nw8[:, oc: oc + 1], r_bc[:, :], ALU.mult, ALU.mult,
                 inc=(cV if oc == ND - 1 else None))
        scale_done = cV.n
        P.wait("scalar", cV, qh2_done)
        P.wait("scalar", cT)
        P.op("scalar", "activation", qhn[:, :], qh2s[:, :], AF.Sqrt, inc=cS)
        P.wait("vector", cS)
        P.op("vector", "tensor_mul", qhn[:, :], qhn[:, :], r_bc[0:16, :])
        P.drain("vector")
        P.op("vector", "reduce_max", st["bound"][:], qhn[:, :], axis=AX.X)
        P.drain("vector")
        P.op("vector", "tensor_mul", st["bound"][:], st["bound"][:],
             st["wmax"][:])
        P.drain("vector")
        P.op("vector", "sem_inc", cV.sem, 1)
        cV.n += 1
        return scale_done, st["wt_free"]

    for t in range(8):
        emit_ln_tile(t)
    evac_t7 = evacT_m[0]
    stq = qk_state("q", wq_loaded, WTa)
    for g in range(ND):
        emit_qk_group(stq, g, 0, wait_cv=(evac_t7 if g == 0 else None))
        emit_ln_tile(8 + g)
    for g in range(ND):
        emit_qk_group(stq, g, 1, wait_cv=(evacT_m[0] if g == 0 else None))
    q_scale_done, wta_free = emit_qk_tail(stq)
    emit_pass1(16)
    P.wait("gpsimd", cT, wta_free)
    for ic in range(ND):
        P.op("gpsimd", "dma_start", out=WTa[:, D * ic: D * (ic + 1)],
             in_=dwv[128 * ic: 128 * (ic + 1), :], inc=cG, inck=16)
    wv_loaded = cG.n
    stk = qk_state("k", wk_loaded, v65)
    kv_wait = [cV.n]
    for tc in range(2):
        for g in range(ND):
            emit_qk_group(stk, g, tc,
                          wait_cv=(kv_wait[0] if (tc, g) == (0, 0) else None))
    k_scale_done, v65_free = emit_qk_tail(stk)
    emit_pass1(16)

    P.wait("vector", cV, memset_V)
    P.wait("vector", cT, v65_free)
    v65r = v65[:, :].rearrange("p (t h j) -> p t h j", h=H, j=65)
    P.op("vector", "memset", v65r[:, :, :, 64:65], 1.0)
    P.drain("vector")
    P.op("vector", "sem_inc", cV.sem, 1)
    cV.n += 1
    vones_done = cV.n
    v_evac_marks = [0, 0]
    P.wait("tensor", cV, stk["rbc"])
    for t in range(NT):
        ps = psA if t % 2 == 0 else psB
        P.wait("tensor", cG, wv_loaded)
        if t >= 2:
            P.wait("tensor", cS, v_evac_marks[t % 2])
        for ic in range(ND):
            emm(ps[:, :],
                xn1T[:, S * ic + 128 * t: S * ic + 128 * (t + 1)],
                WTa[:, D * ic: D * (ic + 1)],
                start=(ic == 0), stop=False)
        emm(ps[:, :], onesb[:, 0:128], bvrow[:],
            start=False, stop=True, inc=cT)
        P.wait("scalar", cT)
        if t == 0:
            P.wait("scalar", cV, vones_done)
        dst = v65[:, 1040 * t: 1040 * (t + 1)].rearrange(
            "p (h j) -> p h j", j=65)[:, :, 0:64]
        src = ps[:, :].rearrange("p (h j) -> p h j", j=64)
        P.op("scalar", "activation", dst, src, AF.Identity, inc=cS)
        v_evac_marks[t % 2] = cS.n
        emit_pass1(2)
    emit_pass1(96)

    P.op("vector", "tensor_mul", sm16c[:], sm16a[:], sm16b[:])
    P.drain("vector")
    P.op("vector", "tensor_scalar_mul", sm16c[:], sm16c[:], -1.0)
    P.drain("vector")
    P.op("vector", "sem_inc", cV.sem, 1)
    cV.n += 1
    P.wait("tensor", cV)
    P.op("tensor", "transpose", psS[0:1, 0:16], sm16c[:],
         identf[0:16, 0:16], inc=cT)
    P.wait("vector", cT)
    P.op("vector", "tensor_copy", nmrow[:], psS[0:1, 0:16], inc=cV)
    P.wait("tensor", cV)
    P.op("tensor", "matmul", psS[:, 16:32], onesf[:, 0:128], nmrow[0:1, :],
         start=True, stop=True, inc=cT)
    P.wait("vector", cT)
    P.op("vector", "tensor_copy", negmb[:], psS[:, 16:32], inc=cV)


    tap("t_xn1T", xn1T[:, :], cV)
    tap("t_qsT", qsT[:, :], cV)
    tap("t_ksT", ksT[:, :], cV)
    tap("t_v65", v65[:, :], cS)
    tap("t_negmb", negmb[:, :], cV)
    tap("t_qh2s", qh2s[:, :], cV)
    barrier()
    free("wstage", "s_bvrow", "s_nkwhd", "s_nqwhd", "qhn", "lns", "qh2s", "r_bc",
         "q2buf", "xnring", "xring", "xn1T", "WTa")
    pzfree("psS", "psT")
    psD = pzalloc("psD", [128, 1024], F32)
    oT_evac_done = [0, 0]

    for wi, (tag, c0, c1, nelem, mcl_b, sw_b) in enumerate((
            ("g", 0, 32, D * INNER, mclg_b, swg_b),
            ("u", 32, 64, D * INNER, mclu_b, swu_b),
            ("d", 64, 96, INNER * D, mcld_b, swd_b))):
        P.op("vector", "reduce_sum", colsum2[:, wi: wi + 1],
             wacc[:, c0: c1], axis=AX.X)
        P.drain("vector")
        P.op("vector", "sem_inc", cV.sem, 1)
        cV.n += 1
        P.wait("tensor", cV)
        P.op("tensor", "transpose", psD[0:1, 32:160],
             colsum2[:, wi: wi + 1], identf[:], inc=cT)
        P.wait("vector", cT)
        P.op("vector", "tensor_copy", sw1[:], psD[0:1, 32:160], inc=cV)
        P.drain("vector")
        P.op("vector", "reduce_sum", sw2[0:1, wi: wi + 1], sw1[:], axis=AX.X)
        P.drain("vector")
        P.op("vector", "tensor_scalar_mul", sw2[0:1, wi + 4: wi + 5],
             sw2[0:1, wi: wi + 1], 1.0 / nelem)
        P.drain("vector")
        P.op("vector", "tensor_scalar_max", sw2[0:1, wi: wi + 1],
             sw2[0:1, wi + 4: wi + 5], 1e-5)
        P.drain("vector")
        P.op("vector", "sem_inc", cV.sem, 1)
        cV.n += 1
        P.wait("tensor", cV)
        P.op("tensor", "matmul", psD[:, 160:161], onesf[:, 0:128],
             sw2[0:1, wi: wi + 1], start=True, stop=True, inc=cT)
        P.wait("vector", cT)
        P.op("vector", "tensor_copy", mcl_b[:], psD[:, 160:161])
        P.drain("vector")
        P.op("vector", "reciprocal", sw_b[:], mcl_b[:])
        P.drain("vector")
        P.op("vector", "sem_inc", cV.sem, 1)
        cV.n += 1
    sw_done = cV.n
    P.op("vector", "tensor_mul", mclud_b[:], mclu_b[:], mcld_b[:])
    P.drain("vector")
    P.op("vector", "tensor_scalar_mul", mclud_b[:], mclud_b[:], 1.0 / 127.0)
    P.drain("vector")
    P.op("vector", "sem_inc", cV.sem, 1)
    cV.n += 1
    oT_evac_done[1] = cV.n

    p2list = []
    for tag, dr, dst, rows, cols, cw in (("g", dwg, dscr_g, D, INNER, 2048),
                                         ("u", dwu, dscr_u, D, INNER, 2048),
                                         ("d", dwd, dscr_d, INNER, D, 1024)):
        sw_b = {"g": swg_b, "u": swu_b, "d": swd_b}[tag]
        for i2 in range(rows // 128):
            for j2 in range(cols // cw):
                p2list.append((dr[128 * i2: 128 * (i2 + 1), cw * j2: cw * (j2 + 1)],
                               dst[:, cols * i2 + cw * j2: cols * i2 + cw * (j2 + 1)],
                               sw_b, cw))
    assert len(p2list) == 64
    p2_state = {"i": 0, "ld": [0, 0], "st": 0}

    def emit_p2(n):
        for _ in range(n):
            i2 = p2_state["i"]
            if i2 >= len(p2list):
                return
            srcp, dstp, sw_b, cw = p2list[i2]
            sl = i2 % 2
            if i2 >= 2:
                P.wait("sync", cV, p2_state["ld"][sl])
            P.op("sync", "dma_start",
                 out=wstg[:, 2048 * sl: 2048 * sl + cw], in_=srcp,
                 inc=cL, inck=16)
            P.wait("vector", cL)
            if i2 == 0:
                P.wait("vector", cV, sw_done)
            if i2 >= 1:
                P.wait("vector", cG, p2_state["st"])
            ws = wstg[:, 2048 * sl: 2048 * sl + cw]
            P.op("vector", "tensor_scalar", pbufg[:, 0:cw], ws, sw_b[:, 0:1],
                 0.5, ALU.mult, ALU.is_ge)
            P.op("vector", "tensor_scalar", mbufg[:, 0:cw], ws, sw_b[:, 0:1],
                 -0.5, ALU.mult, ALU.is_le)
            P.op("vector", "tensor_sub", pw8[:, 0:cw], pbufg[:, 0:cw],
                 mbufg[:, 0:cw], inc=cV)
            p2_state["ld"][sl] = cV.n
            P.wait("gpsimd", cV)
            P.op("gpsimd", "dma_start", out=dstp, in_=pw8[:, 0:cw],
                 inc=cG, inck=16)
            p2_state["st"] = cG.n
            p2_state["i"] += 1


    oT = alloc("oT", [128, ND * S], BF16, "left")
    pTr = alloc("pTr", [128, 2 * 1024], BF16, "right")
    wstg = alloc("wstg", [128, 2 * 2048], F32, "right")
    pbufg = alloc("pbufg", [128, 2048], BF16, "right")
    mbufg = alloc("mbufg", [128, 2048], BF16, "right")
    pw8 = alloc("pw8", [128, 2048], F8, "right")
    rdrow = alloc("rdrow", [1, 1024], F32, "right")
    rdrowb = alloc("rdrowb", [1, 1024], BF16, "right")
    rdb = alloc("rdb", [128, 1024], F32, "right")
    otmp = alloc("otmp", [64, 1024], BF16, "right")

    exp_free = [0, 0]
    score_free = [0, 0]
    pairs = [(h, qc) for h in range(H) for qc in range(2)]
    tail_prev = None

    def tail_recip(prev):
        h, qc, slot, av_mark = prev
        P.wait("vector", cT, av_mark)
        P.op("vector", "reciprocal", rdrow[:], oT_ps[slot][64:65, :])
        P.drain("vector")
        P.op("vector", "tensor_copy", rdrowb[:], rdrow[:])
        P.drain("vector")
        P.op("vector", "sem_inc", cV.sem, 1)
        cV.n += 1
        return cV.n

    def tail_bcast(prev, rd_ready):
        P.wait("tensor", cV, rd_ready)
        for j in range(2):
            P.op("tensor", "matmul", psB[:, 512 * j: 512 * (j + 1)],
                 onesb[:, 0:128], rdrowb[0:1, 512 * j: 512 * (j + 1)],
                 start=True, stop=True, inc=(cT if j == 1 else None))

    def tail_evac(prev):
        h, qc, slot, av_mark = prev
        hp = h % 2
        c = h // 2
        P.wait("vector", cT)
        P.op("vector", "tensor_copy", rdb[:, :], psB[:, :], inc=cV)
        rdb_done = cV.n
        ocols = slice(S * c + 1024 * qc, S * c + 1024 * (qc + 1))
        if hp == 0:
            P.op("vector", "tensor_mul", oT[0:64, ocols],
                 oT_ps[slot][0:64, :], rdb[0:64, :], inc=cV)
        else:
            P.op("vector", "tensor_mul", otmp[:, :],
                 oT_ps[slot][0:64, :], rdb[0:64, :], inc=cV)
            P.wait("gpsimd", cV)
            P.op("gpsimd", "dma_start", out=oT[64:128, ocols],
                 in_=otmp[:, :], inc=cG, inck=16)
        oT_evac_done[slot] = cV.n
        return rdb_done, cV.n

    oT_ps = [psC, psD]
    sc_mark = [0, 0]
    for i, (h, qc) in enumerate(pairs):
        hp = h % 2
        c = h // 2
        slot = i % 2
        rd_ready = tail_recip(tail_prev) if tail_prev is not None else None
        rdb_done = None

        def emit_av(kkm):
            sl2 = kkm % 2
            P.wait("tensor", cS, score_free[sl2])
            if kkm == 0:
                P.wait("tensor", cV, oT_evac_done[slot])
            emm(oT_ps[slot][0:65, :],
                v65[:, 1040 * kkm + 65 * h: 1040 * kkm + 65 * (h + 1)],
                pTr[:, 1024 * sl2: 1024 * sl2 + 1024],
                start=(kkm == 0), stop=(kkm == NT - 1), inc=cT)
            exp_free[sl2] = cT.n

        for kk in range(NT):
            sl = kk % 2
            ps = psA if sl == 0 else psB
            P.wait("tensor", cS, score_free[sl])
            if kk == 7 and rdb_done is not None:
                P.wait("tensor", cV, rdb_done)
            emm(ps[:, :],
                ksT[64 * hp:64 * hp + 64,
                    S * c + 128 * kk: S * c + 128 * (kk + 1)],
                qsT[64 * hp:64 * hp + 64,
                    S * c + 1024 * qc: S * c + 1024 * (qc + 1)],
                start=True, stop=True, inc=cT)
            sc_mark[sl] = cT.n
            P.wait("scalar", cT, sc_mark[sl])
            P.wait("scalar", cT, exp_free[sl])
            P.op("scalar", "activation",
                 pTr[:, 1024 * sl: 1024 * sl + 1024],
                 ps[:, :], AF.Exp, bias=negmb[:, h: h + 1], inc=cS)
            score_free[sl] = cS.n
            if kk >= 1:
                emit_av(kk - 1)
            if kk == 5 and tail_prev is not None:
                P.wait("tensor", cS, score_free[1])
                tail_bcast(tail_prev, rd_ready)
                rdb_done, _ = tail_evac(tail_prev)
        emit_av(NT - 1)
        av_mark = cT.n
        tail_prev = (h, qc, slot, av_mark)
        emit_p2(2)
    emit_p2(64)
    rd_ready = tail_recip(tail_prev)
    tail_bcast(tail_prev, rd_ready)
    tail_evac(tail_prev)
    P.wait("gpsimd", cG)
    tap("t_oT", oT[:, :], cV)
    barrier()
    pzfree("psD")
    psS = pzalloc("psS2", [128, 512], F32)
    psT = pzalloc("psT2", [128, 1024], BF16)
    free("otmp", "rdb", "rdrowb", "rdrow", "pw8", "mbufg", "pbufg", "wstg", "pTr")
    free("v65", "ksT", "qsT")

    x_mid = alloc("x_mid", [128, NT * D], F32, "right")
    tmbuf = alloc("tmbuf", [128, D], F32, "right")
    l2 = alloc("l2", [128, 20 * 16], F32, "right")

    def l2c(j):
        return l2[:, 16 * j: 16 * (j + 1)]

    WTo = alloc("WTo", [128, ND * D], BF16, "right")
    xr2 = alloc("xr2", [128, 2 * D], F32, "right")
    for ic in range(ND):
        P.op("gpsimd", "dma_start", out=WTo[:, D * ic: D * (ic + 1)],
             in_=dwo[128 * ic: 128 * (ic + 1), :], inc=cG, inck=16)
    wo_loaded = cG.n

    wo_evac = [0, 0]
    for t in range(NT):
        sl = t % 2
        if t >= 2:
            P.wait("sync", cV, wo_evac[sl])
        P.op("sync", "dma_start", out=xr2[:, D * sl: D * (sl + 1)],
             in_=dx[128 * t: 128 * (t + 1), :], inc=cL, inck=16)
        ps = psA if sl == 0 else psB
        P.wait("tensor", cG, wo_loaded)
        if t >= 2:
            P.wait("tensor", cV, wo_evac[sl])
        for ic in range(ND):
            emm(ps[:, :],
                oT[:, S * ic + 128 * t: S * ic + 128 * (t + 1)],
                WTo[:, D * ic: D * (ic + 1)],
                start=(ic == 0), stop=False)
        emm(ps[:, :], onesb[:, 0:128], borow[:],
            start=False, stop=True, inc=cT)
        P.wait("vector", cT)
        P.wait("vector", cL)
        P.op("vector", "scalar_tensor_tensor", x_mid[:, D * t: D * (t + 1)],
             ps[:, :], 1.0, xr2[:, D * sl: D * (sl + 1)], ALU.mult, ALU.add,
             inc=cV)
        wo_evac[sl] = cV.n
        x_t = x_mid[:, D * t: D * (t + 1)]
        P.op("vector", "reduce_sum", l2c(0)[:, t: t + 1], x_t, axis=AX.X)
        P.op("vector", "scalar_tensor_tensor", tmbuf[:], x_t, 1.0, x_t,
             ALU.mult, ALU.mult, accum_out=l2c(1)[:, t: t + 1])
        P.op("vector", "reduce_max", l2c(2)[:, t: t + 1], x_t, axis=AX.X)
        P.op("vector", "tensor_reduce", l2c(3)[:, t: t + 1], x_t, axis=AX.X,
             op=ALU.min)

    tap("t_xmid", x_mid[:, :], cV)
    barrier()
    free("xr2", "WTo")
    free("oT")

    w8g = alloc("w8g", [128, ND * INNER], F8, "left")
    w8u = alloc("w8u", [128, ND * INNER], F8, "left")
    w8d = alloc("w8d", [128, NI * D], F8, "left")

    P.op("sync", "dma_start", out=w8g[:, :], in_=dscr_g, inc=cL, inck=16)
    P.op("gpsimd", "dma_start", out=w8u[:, :], in_=dscr_u, inc=cG, inck=16)
    P.op("sync", "dma_start", out=w8d[:, :], in_=dscr_d, inc=cL, inck=16)
    w8_loads_L = cL.n
    w8_loads_G = cG.n
    P.wait("vector", cL, w8_loads_L)
    P.wait("vector", cG, w8_loads_G)
    P.op("vector", "sem_inc", cV.sem, 1)
    cV.n += 1
    w8_done = cV.n

    tap("t_w8g", w8g[:, :], cV)
    tap("t_w8d", w8d[:, :], cV)
    barrier()

    n1buf = alloc("n1buf", [128, D], I32, "right")
    r1buf = alloc("r1buf", [128, D], F32, "right")
    nabuf = alloc("nabuf", [128, D], BF16, "right")
    a1T = alloc("a1T", [128, D], BF16, "right")
    gsbuf = alloc("gsbuf", [128, 1024], F32, "right")
    hbuf = alloc("hbuf", [128, INNER], F32, "right")
    a2T = alloc("a2T", [128, 1024], BF16, "right")
    hm4 = alloc("hm4", [128, 16], F32, "right")

    P.drain("vector")
    P.op("vector", "tensor_scalar_mul", l2c(4), l2c(0), 1.0 / D)
    P.drain("vector")
    P.op("vector", "tensor_mul", l2c(5), l2c(4), l2c(4))
    P.drain("vector")
    P.op("vector", "scalar_tensor_tensor", l2c(6), l2c(1), 1.0 / D, l2c(5),
         ALU.mult, ALU.subtract)
    P.drain("vector")
    P.op("vector", "tensor_scalar_add", l2c(7), l2c(6), EPS_LN)
    P.drain("vector")
    P.op("vector", "reciprocal", l2c(8), l2c(7))
    P.drain("vector")
    P.op("vector", "sem_inc", cV.sem, 1)
    cV.n += 1
    P.wait("scalar", cV)
    P.op("scalar", "activation", l2c(9), l2c(8), AF.Sqrt)
    P.drain("scalar")
    P.op("scalar", "sem_inc", cS.sem, 1)
    cS.n += 1
    P.wait("vector", cS)
    P.op("vector", "scalar_tensor_tensor", l2c(10), l2c(4), -1.0, l2c(9),
         ALU.mult, ALU.mult)
    P.drain("vector")
    P.op("vector", "tensor_sub", l2c(11), l2c(2), l2c(4))
    P.op("vector", "tensor_sub", l2c(12), l2c(4), l2c(3))
    P.drain("vector")
    P.op("vector", "tensor_max", l2c(13), l2c(11), l2c(12))
    P.drain("vector")
    P.op("vector", "tensor_mul", l2c(14), l2c(9), l2c(13))
    P.drain("vector")
    P.op("vector", "tensor_scalar_max", l2c(15), l2c(14), 1e-5)
    P.drain("vector")
    P.op("vector", "reciprocal", l2c(16), l2c(15))
    P.drain("vector")
    P.op("vector", "tensor_scalar_mul", l2c(17), l2c(16), 127.0)
    P.drain("vector")
    P.op("vector", "tensor_mul", l2c(18), l2c(9), l2c(17))
    P.drain("vector")
    P.op("vector", "tensor_mul", l2c(19), l2c(10), l2c(17))
    P.drain("vector")
    P.op("vector", "reciprocal", l2c(11), l2c(17))
    P.drain("vector")
    P.op("vector", "tensor_scalar", l2c(12), l2c(11), mclg_b[:, 0:1], -1.0,
         ALU.mult, ALU.mult)
    P.drain("vector")
    P.op("vector", "sem_inc", cV.sem, 1)
    cV.n += 1
    ln2_done = cV.n

    a1T_free = [0, 0]
    gate_free = 0
    up_free = 0
    down_free = 0
    out_marks = [0, 0]
    na1_trans_mark = 0
    na2_trans_mark = 0
    scalar_tm_mark = 0
    for t in range(NT):
        sl = t % 2
        P.wait("scalar", cV, ln2_done)
        if t >= 1:
            P.wait("scalar", cV, scalar_tm_mark)
        P.op("scalar", "activation", tmbuf[:], x_mid[:, D * t: D * (t + 1)],
             AF.Identity, scale=l2c(18)[:, t: t + 1],
             bias=l2c(19)[:, t: t + 1], inc=cS)
        P.wait("vector", cS)
        P.op("vector", "tensor_scalar", n1buf[:], tmbuf[:], MAGIC, MAGIC,
             ALU.add, ALU.subtract)
        P.op("vector", "scalar_tensor_tensor", r1buf[:], tmbuf[:], 0.0,
             n1buf[:], ALU.add, ALU.subtract, inc=cV)
        scalar_tm_mark = cV.n
        P.op("vector", "tensor_scalar", r1buf[:], r1buf[:], MAGIC64, MAGIC64,
             ALU.add, ALU.subtract)
        if t >= 1:
            P.wait("vector", cT, na1_trans_mark)
        P.op("vector", "scalar_tensor_tensor", nabuf[:], r1buf[:], -1.0,
             n1buf[:], ALU.mult, ALU.subtract, inc=cV)
        if DEBUG_TAPS and t == 0:
            P.wait("gpsimd", cV)
            P.op("gpsimd", "dma_start", out=dtaps["t_a1"], in_=nabuf[:],
                 inc=cG, inck=16)
            P.op("gpsimd", "dma_start", out=dtaps["t_tm"], in_=tmbuf[:],
                 inc=cG, inck=16)
            P.op("gpsimd", "dma_start", out=dtaps["t_n1"], in_=n1buf[:],
                 inc=cG, inck=16)
            dbg_mark = cG.n
            P.wait("vector", cG, dbg_mark)
            P.wait("scalar", cG, dbg_mark)
            P.wait("tensor", cG, dbg_mark)
        P.wait("tensor", cV)
        if t >= 1:
            P.wait("tensor", cT, a1T_free[0])
            P.wait("tensor", cV, na2_trans_mark)
        for j in range(ND):
            P.op("tensor", "transpose", psT[:, 128 * j: 128 * (j + 1)],
                 nabuf[:, 128 * j: 128 * (j + 1)], identb[:],
                 inc=(cT if j == ND - 1 else None))
        na1_trans_mark = cT.n
        P.wait("vector", cT)
        P.op("vector", "tensor_copy", a1T[:, :], psT[:, :], inc=cV)
        a1t_ready = cV.n
        for ob in range(4):
            P.wait("tensor", cV, a1t_ready)
            if t == 0 and ob == 0:
                P.wait("tensor", cV, w8_done)
            P.wait("tensor", cS, gate_free)
            for ic in range(ND):
                emm(psA[:, :],
                    a1T[:, 128 * ic: 128 * (ic + 1)],
                    w8g[:, INNER * ic + 1024 * ob: INNER * ic + 1024 * (ob + 1)],
                    start=(ic == 0), stop=(ic == ND - 1),
                    inc=(cT if ic == ND - 1 else None))
            gate_mm = cT.n
            P.wait("tensor", cV, up_free)
            for ic in range(ND):
                emm(psB[:, :],
                    a1T[:, 128 * ic: 128 * (ic + 1)],
                    w8u[:, INNER * ic + 1024 * ob: INNER * ic + 1024 * (ob + 1)],
                    start=(ic == 0), stop=(ic == ND - 1),
                    inc=(cT if ic == ND - 1 else None))
            up_mm = cT.n
            P.wait("scalar", cT, gate_mm)
            P.op("scalar", "activation", gsbuf[:], psA[:, :], AF.Silu,
                 scale=l2c(12)[:, t: t + 1], inc=cS)
            gate_free = cS.n
            P.wait("vector", cS)
            P.wait("vector", cT, up_mm)
            P.op("vector", "tensor_mul", hbuf[:, 1024 * ob: 1024 * (ob + 1)],
                 gsbuf[:], psB[:, :])
            P.op("vector", "reduce_max", hm4[:, 4 * sl + ob: 4 * sl + ob + 1],
                 hbuf[:, 1024 * ob: 1024 * (ob + 1)], axis=AX.X,
                 apply_absolute_value=True, inc=cV)
            up_free = cV.n
        P.drain("vector")
        P.op("vector", "reduce_max", l2c(13)[:, t: t + 1],
             hm4[:, 4 * sl: 4 * sl + 4], axis=AX.X)
        P.drain("vector")
        P.op("vector", "tensor_scalar_max", l2c(14)[:, t: t + 1],
             l2c(13)[:, t: t + 1], 1e-5)
        P.drain("vector")
        P.op("vector", "reciprocal", l2c(15)[:, t: t + 1],
             l2c(14)[:, t: t + 1])
        P.drain("vector")
        P.op("vector", "tensor_scalar_mul", l2c(16)[:, t: t + 1],
             l2c(15)[:, t: t + 1], 127.0)
        P.drain("vector")
        P.op("vector", "tensor_mul", l2c(5)[:, t: t + 1],
             l2c(11)[:, t: t + 1], l2c(14)[:, t: t + 1])
        P.drain("vector")
        P.op("vector", "tensor_single_scalar", l2c(6)[:, t: t + 1],
             l2c(5)[:, t: t + 1], mclud_b[:, 0:1], ALU.mult)
        P.drain("vector")
        P.op("vector", "sem_inc", cV.sem, 1)
        cV.n += 1
        s2_done = cV.n
        for qb in range(4):
            P.wait("scalar", cV, s2_done)
            P.wait("scalar", cV, scalar_tm_mark)
            P.op("scalar", "activation", tmbuf[:],
                 hbuf[:, 1024 * qb: 1024 * (qb + 1)], AF.Identity,
                 scale=l2c(16)[:, t: t + 1], bias=magic[:, 0:1], inc=cS)
            P.wait("vector", cS)
            P.op("vector", "tensor_scalar_sub", n1buf[:], tmbuf[:], MAGIC)
            P.op("vector", "scalar_tensor_tensor", r1buf[:], tmbuf[:], -MAGIC,
                 n1buf[:], ALU.add, ALU.subtract, inc=cV)
            scalar_tm_mark = cV.n
            P.op("vector", "tensor_scalar", r1buf[:], r1buf[:], MAGIC64,
                 MAGIC64, ALU.add, ALU.subtract)
            P.wait("vector", cT, na1_trans_mark)
            P.op("vector", "scalar_tensor_tensor", nabuf[:], r1buf[:], -1.0,
                 n1buf[:], ALU.mult, ALU.subtract, inc=cV)
            na2_ready = cV.n
            P.wait("tensor", cV, na2_ready)
            P.wait("tensor", cV, na2_trans_mark)
            for j in range(ND):
                P.op("tensor", "transpose", psT[:, 128 * j: 128 * (j + 1)],
                     nabuf[:, 128 * j: 128 * (j + 1)],
                     identb[:], inc=(cT if j == ND - 1 else None))
            P.wait("vector", cT)
            na2_trans_tmark = cT.n
            qbase = 0
            P.op("vector", "tensor_copy",
                 a2T[:, qbase: qbase + 1024], psT[:, :], inc=cV)
            na2_trans_mark = cV.n
            if DEBUG_TAPS and t == 0:
                P.wait("gpsimd", cV, na2_trans_mark)
                P.op("gpsimd", "dma_start",
                     out=dtaps["t_a2T"][:, 1024 * qb: 1024 * (qb + 1)],
                     in_=a2T[:, 0:1024], inc=cG, inck=16)
                P.wait("vector", cG)
            na1_trans_mark = na2_trans_tmark
            P.wait("tensor", cV, na2_trans_mark)
            if qb == 0:
                P.wait("tensor", cV, down_free)
            for j in range(ND):
                ic = 8 * qb + j
                emm(psC[:, :],
                    a2T[:, qbase + 128 * j: qbase + 128 * (j + 1)],
                    w8d[:, D * ic: D * (ic + 1)],
                    start=(ic == 0), stop=(ic == NI - 1),
                    inc=(cT if ic == NI - 1 else None))
        if t == 0:
            tap("t_na1", nabuf[:, :], cV)
            tap("t_hbuf", hbuf[:, :], cV)
            tap("t_l2", l2[:, :], cV)
        a1T_free[0] = cT.n
        P.wait("vector", cT)
        P.op("vector", "scalar_tensor_tensor",
             x_mid[:, D * t: D * (t + 1)], psC[:, :],
             l2c(6)[:, t: t + 1], x_mid[:, D * t: D * (t + 1)],
             ALU.mult, ALU.add, inc=cV)
        down_free = cV.n
        P.wait("gpsimd", cV, down_free)
        P.op("gpsimd", "dma_start", out=dout[128 * t: 128 * (t + 1), :],
             in_=x_mid[:, D * t: D * (t + 1)], inc=cO, inck=16)

    P.wait("gpsimd", cO, cO.n)

    with nc.Block() as blk:
        @blk.sync
        def _(e):
            P.emit("sync", e)

        @blk.gpsimd
        def _(e):
            P.emit("gpsimd", e)

        @blk.tensor
        def _(e):
            P.emit("tensor", e)

        @blk.vector
        def _(e):
            P.emit("vector", e)

        @blk.scalar
        def _(e):
            P.emit("scalar", e)

    for name in list(live.keys())[::-1]:
        try:
            live.pop(name).__exit__(None, None, None)
        except Exception:
            pass
    for cm in pcm[::-1]:
        try:
            cm.__exit__(None, None, None)
        except Exception:
            pass
    for cm in semlist[::-1]:
        try:
            cm.__exit__(None, None, None)
        except Exception:
            pass
    return nc


_CACHED_NC = None


def kernel(L0_attn_x, L0_attn_Wq, L0_attn_bq, L0_attn_Wk, L0_attn_bk,
           L0_attn_Wv, L0_attn_bv, L0_attn_Wo, L0_attn_bo,
           L0_attn_norm_q_w, L0_attn_norm_k_w,
           L0_ffn_w_gate, L0_ffn_w_up, L0_ffn_w_down):
    global _CACHED_NC
    if _CACHED_NC is None:
        _CACHED_NC = build_program()
    nc = _CACHED_NC

    f32 = np.float32
    x = np.ascontiguousarray(np.asarray(L0_attn_x, dtype=f32))
    assert x.shape == (8, S, D)

    def c(a):
        return np.ascontiguousarray(np.asarray(a, dtype=f32))

    common = {
        "wqt": c(np.asarray(L0_attn_Wq, dtype=f32).T),
        "wkt": c(np.asarray(L0_attn_Wk, dtype=f32).T),
        "wvt": c(np.asarray(L0_attn_Wv, dtype=f32).T),
        "wot": c(np.asarray(L0_attn_Wo, dtype=f32).T),
        "bq8": c(np.asarray(L0_attn_bq, dtype=f32).reshape(ND, 128).T),
        "bk8": c(np.asarray(L0_attn_bk, dtype=f32).reshape(ND, 128).T),
        "bvrow": c(np.asarray(L0_attn_bv, dtype=f32).reshape(1, D)),
        "borow": c(np.asarray(L0_attn_bo, dtype=f32).reshape(1, D)),
        "nqw8": c(np.asarray(L0_attn_norm_q_w, dtype=f32).reshape(ND, 128).T),
        "nkw8": c(np.asarray(L0_attn_norm_k_w, dtype=f32).reshape(ND, 128).T),
        "nqwhd": c(np.asarray(L0_attn_norm_q_w, dtype=f32).reshape(H, HD)),
        "nkwhd": c(np.asarray(L0_attn_norm_k_w, dtype=f32).reshape(H, HD)),
        "wgT": c(np.asarray(L0_ffn_w_gate, dtype=f32).T),
        "wuT": c(np.asarray(L0_ffn_w_up, dtype=f32).T),
        "wdT": c(np.asarray(L0_ffn_w_down, dtype=f32).T),
        "identf": np.eye(128, dtype=f32),
    }
    in_maps = []
    for core in range(8):
        m = {"x": np.ascontiguousarray(x[core])}
        m.update(common)
        in_maps.append(m)

    res = run_bass_kernel_spmd(nc, in_maps, core_ids=list(range(8)))
    out = np.stack([np.asarray(r["out"], dtype=np.float32)
                    for r in res.results], axis=0)
    return out


# version 50
# speedup vs baseline: 1.1048x; 1.0265x over previous
import numpy as np
import ml_dtypes
import concourse.bass as bass
import concourse.mybir as mybir
from concourse.bass_utils import run_bass_kernel_spmd

BF16 = mybir.dt.bfloat16
F32 = mybir.dt.float32
I32 = mybir.dt.int32
F8 = mybir.dt.float8e4
AF = mybir.ActivationFunctionType
ALU = mybir.AluOpType
AX = mybir.AxisListType

MAGIC = 12582912.0
MAGIC64 = 6755399441055744.0
S, D, H, HD, INNER = 2048, 1024, 16, 64, 4096
NT = S // 128
ND = D // 128
NI = INNER // 128
EPS_LN = 1e-5
EPS_RMS = 1e-6

ENGINES = ("sync", "gpsimd", "tensor", "vector", "scalar")


class Ch:
    def __init__(self, sem):
        self.sem = sem
        self.n = 0


class Prog:
    def __init__(self):
        self.ops = {e: [] for e in ENGINES}

    def op(self, eng, meth, *args, inc=None, inck=1, **kw):
        if inc is not None:
            inc.n += inck
            sem, val = inc.sem, inck
        else:
            sem, val = None, 0
        self.ops[eng].append(("op", meth, args, kw, sem, val))

    def wait(self, eng, ch, val=None):
        v = ch.n if val is None else val
        if v > 0:
            self.ops[eng].append(("wait", ch.sem, v))

    def drain(self, eng):
        self.ops[eng].append(("drain",))

    def emit(self, eng_name, eng):
        for entry in self.ops[eng_name]:
            if entry[0] == "op":
                _, meth, args, kw, sem, val = entry
                r = getattr(eng, meth)(*args, **kw)
                if sem is not None:
                    r.then_inc(sem, val)
            elif entry[0] == "wait":
                eng.wait_ge(entry[1], entry[2])
            else:
                eng.drain()


DEBUG_TAPS = False


def build_program():
    nc = bass.Bass("TRN2", target_bir_lowering=False,
                   detect_race_conditions=False)

    dx = nc.dram_tensor("x", [S, D], F32, kind="ExternalInput").ap()
    dwq = nc.dram_tensor("wqt", [D, D], F32, kind="ExternalInput").ap()
    dwk = nc.dram_tensor("wkt", [D, D], F32, kind="ExternalInput").ap()
    dwv = nc.dram_tensor("wvt", [D, D], F32, kind="ExternalInput").ap()
    dwo = nc.dram_tensor("wot", [D, D], F32, kind="ExternalInput").ap()
    dbq = nc.dram_tensor("bq8", [128, ND], F32, kind="ExternalInput").ap()
    dbk = nc.dram_tensor("bk8", [128, ND], F32, kind="ExternalInput").ap()
    dbv = nc.dram_tensor("bvrow", [1, D], F32, kind="ExternalInput").ap()
    dbo = nc.dram_tensor("borow", [1, D], F32, kind="ExternalInput").ap()
    dnq8 = nc.dram_tensor("nqw8", [128, ND], F32, kind="ExternalInput").ap()
    dnk8 = nc.dram_tensor("nkw8", [128, ND], F32, kind="ExternalInput").ap()
    dnqh = nc.dram_tensor("nqwhd", [H, HD], F32, kind="ExternalInput").ap()
    dnkh = nc.dram_tensor("nkwhd", [H, HD], F32, kind="ExternalInput").ap()
    dwg = nc.dram_tensor("wgT", [D, INNER], F32, kind="ExternalInput").ap()
    dwu = nc.dram_tensor("wuT", [D, INNER], F32, kind="ExternalInput").ap()
    dwd = nc.dram_tensor("wdT", [INNER, D], F32, kind="ExternalInput").ap()
    didf = nc.dram_tensor("identf", [128, 128], F32, kind="ExternalInput").ap()
    dout = nc.dram_tensor("out", [S, D], F32, kind="ExternalOutput").ap()
    dscr_g = nc.dram_tensor("scr_g", [128, ND * INNER], F8, kind="Internal").ap()
    dscr_u = nc.dram_tensor("scr_u", [128, ND * INNER], F8, kind="Internal").ap()
    dscr_d = nc.dram_tensor("scr_d", [128, NI * D], F8, kind="Internal").ap()
    dtaps = {}
    if DEBUG_TAPS:
        for nm, shape, dt in (
                ("t_xn1T", [128, ND * S], F32), ("t_qsT", [128, ND * S], F32),
                ("t_ksT", [128, ND * S], F32), ("t_v65", [128, NT * H * 65], F32),
                ("t_negmb", [128, H], F32), ("t_oT", [128, ND * S], F32),
                ("t_xmid", [128, NT * D], F32), ("t_w8g", [128, ND * INNER], F32),
                ("t_w8d", [128, NI * D], F32),
                ("t_scl", [128, 16], F32), ("t_na1", [128, D], F32),
                ("t_hbuf", [128, INNER], F32), ("t_a2T", [128, INNER], F32),
                ("t_l2", [128, 320], F32), ("t_qh2s", [16, S], F32),
                ("t_a1", [128, D], F32), ("t_tm", [128, D], F32),
                ("t_n1", [128, D], I32)):
            dtaps[nm] = nc.dram_tensor(nm, shape, dt, kind="ExternalOutput").ap()

    P = Prog()
    semlist = []

    def mksem(name):
        cm = nc.semaphore(name)
        semlist.append(cm)
        return Ch(cm.__enter__())

    cL = mksem("cL")
    cG = mksem("cG")
    cT = mksem("cT")
    cV = mksem("cV")
    cS = mksem("cS")
    cO = mksem("cO")
    cB = mksem("cB")

    def barrier():
        base = cB.n
        cB.n += len(ENGINES)
        for e in ENGINES:
            P.ops[e].append(("op", "sem_inc", (cB.sem, 1), {}, None, 0))
            P.ops[e].append(("wait", cB.sem, base + len(ENGINES)))

    def tap(nm, src_ap, ch=None):
        if not DEBUG_TAPS:
            return
        if ch is not None:
            P.wait("gpsimd", ch)
        P.op("gpsimd", "dma_start", out=dtaps[nm], in_=src_ap, inc=cG,
             inck=16)

    def emm(out, lhsT, rhs, start, stop, inc=None):
        N = rhs.shape[-1]
        if N <= 512:
            P.op("tensor", "matmul", out, lhsT, rhs, start=start, stop=stop,
                 inc=inc)
            return
        for a in range(0, N, 512):
            b = min(a + 512, N)
            P.op("tensor", "matmul", out[:, a:b], lhsT, rhs[:, a:b],
                 start=start, stop=stop, inc=(inc if b == N else None))

    live = {}

    def alloc(name, shape, dtype, side):
        cm = nc.sbuf_tensor(name, shape, dtype, side=side)
        h = cm.__enter__()
        live[name] = cm
        return h

    def free(*names):
        for name in names:
            live.pop(name).__exit__(None, None, None)

    identf = alloc("c_identf", [128, 128], F32, "left")
    identb = alloc("c_identb", [128, 128], BF16, "left")
    bq8 = alloc("c_bq8", [128, ND], F32, "left")
    bk8 = alloc("c_bk8", [128, ND], F32, "left")
    nqw8 = alloc("c_nqw8", [128, ND], F32, "left")
    nkw8 = alloc("c_nkw8", [128, ND], F32, "left")
    borow = alloc("c_borow", [1, D], BF16, "left")
    magic = alloc("c_magic", [128, 1], F32, "left")
    onesb = alloc("c_onesb", [1, 128], BF16, "left")
    onesf = alloc("c_onesf", [1, 128], F32, "left")
    masks = alloc("c_masks", [128, 16 * ND], BF16, "left")
    onesfc = alloc("c_onesfc", [16, 1], F32, "left")
    negmb = alloc("c_negmb", [128, H], F32, "left")
    wmaxq = alloc("c_wmaxq", [H, 1], F32, "left")
    wmaxk = alloc("c_wmaxk", [H, 1], F32, "left")
    nmrow = alloc("c_nmrow", [1, H], F32, "left")
    sm16a = alloc("c_sm16a", [16, 1], F32, "left")
    sm16b = alloc("c_sm16b", [16, 1], F32, "left")
    sm16c = alloc("c_sm16c", [16, 1], F32, "left")
    wacc = alloc("c_wacc", [128, 96], F32, "left")
    swg_b = alloc("c_swg_b", [128, 1], F32, "left")
    mclg_b = alloc("c_mclg_b", [128, 1], F32, "left")
    swu_b = alloc("c_swu_b", [128, 1], F32, "left")
    mclu_b = alloc("c_mclu_b", [128, 1], F32, "left")
    swd_b = alloc("c_swd_b", [128, 1], F32, "left")
    mcld_b = alloc("c_mcld_b", [128, 1], F32, "left")
    mclud_b = alloc("c_mclud_b", [128, 1], F32, "left")
    sw1 = alloc("c_sw1", [1, 128], F32, "left")
    sw2 = alloc("c_sw2", [1, 8], F32, "left")
    colsum2 = alloc("c_colsum2", [128, 4], F32, "left")

    pcm = []

    def palloc(name, shape, dtype):
        cm = nc.psum_tensor(name, shape, dtype)
        pcm.append(cm)
        return cm.__enter__()

    plive = {}

    def pzalloc(name, shape, dtype):
        cm = nc.psum_tensor(name, shape, dtype)
        plive[name] = cm
        return cm.__enter__()

    def pzfree(*names):
        for name in names:
            plive.pop(name).__exit__(None, None, None)

    psA = palloc("psA", [128, 1024], F32)
    psB = palloc("psB", [128, 1024], F32)
    psC = palloc("psC", [128, 1024], F32)
    psT = pzalloc("psT", [128, 1024], BF16)
    psS = pzalloc("psS", [128, 512], F32)

    P.op("sync", "dma_start", out=identf[:], in_=didf, inc=cL, inck=16)
    P.op("sync", "dma_start", out=bq8[:], in_=dbq, inc=cL, inck=16)
    P.op("sync", "dma_start", out=bk8[:], in_=dbk, inc=cL, inck=16)
    P.op("sync", "dma_start", out=nqw8[:], in_=dnq8, inc=cL, inck=16)
    P.op("sync", "dma_start", out=nkw8[:], in_=dnk8, inc=cL, inck=16)
    P.op("gpsimd", "dma_start", out=identb[:], in_=didf, inc=cG, inck=16)
    P.op("gpsimd", "dma_start", out=borow[:], in_=dbo, inc=cG, inck=16)

    P.op("vector", "memset", magic[:], MAGIC)
    P.op("vector", "memset", onesb[:], 1.0)
    P.op("vector", "memset", onesf[:], 1.0)
    P.op("vector", "memset", masks[:], 0.0)
    P.drain("vector")
    P.op("vector", "memset", onesfc[:], 1.0)
    P.drain("vector")
    for c in range(ND):
        P.op("vector", "memset", masks[0:64, 16 * c + 2 * c: 16 * c + 2 * c + 1], 1.0)
        P.op("vector", "memset", masks[64:128, 16 * c + 2 * c + 1: 16 * c + 2 * c + 2], 1.0)
    P.drain("vector")
    P.op("vector", "sem_inc", cV.sem, 1)
    cV.n += 1
    memset_V = cV.n

    qsT = alloc("qsT", [128, ND * S], BF16, "right")
    ksT = alloc("ksT", [128, ND * S], BF16, "right")
    v65 = alloc("v65", [128, NT * H * 65], BF16, "right")
    WTa = alloc("WTa", [128, ND * D], BF16, "right")
    xn1T = alloc("xn1T", [128, ND * S], BF16, "right")
    xring = alloc("xring", [128, 2 * D], F32, "right")
    xnring = alloc("xnring", [128, 2 * D], BF16, "right")
    q2buf = alloc("q2buf", [128, S], BF16, "right")
    r_bc = alloc("r_bc", [128, S], F32, "right")
    qh2s = alloc("qh2s", [16, S], F32, "right")
    lns = alloc("lns", [128, 16 * NT], F32, "right")
    qhn = alloc("qhn", [H, S], F32, "right")
    nqwhd = alloc("s_nqwhd", [H, HD], F32, "right")
    nkwhd = alloc("s_nkwhd", [H, HD], F32, "right")
    bvrow = alloc("s_bvrow", [1, D], BF16, "right")
    wstage = alloc("wstage", [128, 3 * 1024], F32, "right")

    def lncol(t, j):
        return lns[:, 16 * t + j: 16 * t + j + 1]

    P.op("sync", "dma_start", out=nqwhd[:], in_=dnqh, inc=cL, inck=16)
    P.op("sync", "dma_start", out=nkwhd[:], in_=dnkh, inc=cL, inck=16)
    P.op("gpsimd", "dma_start", out=bvrow[:], in_=dbv, inc=cG, inck=16)
    consts_L = cL.n
    consts_G = cG.n

    P.wait("vector", cL, consts_L)
    P.op("vector", "reduce_max", wmaxq[:], nqwhd[:], axis=AX.X,
         apply_absolute_value=True)
    P.op("vector", "reduce_max", wmaxk[:], nkwhd[:], axis=AX.X,
         apply_absolute_value=True)
    P.drain("vector")
    P.op("vector", "sem_inc", cV.sem, 1)
    cV.n += 1

    pass1_srcs = []
    for tag, dr, rows, cols in (("g", dwg, D, INNER), ("u", dwu, D, INNER),
                                ("d", dwd, INNER, D)):
        for i in range(rows // 128):
            for j in range(cols // 1024):
                pass1_srcs.append(dr[128 * i:128 * (i + 1),
                                     1024 * j:1024 * (j + 1)])
    assert len(pass1_srcs) == 96
    p1_idx = [0]
    p1_marks = [0, 0, 0]

    def emit_pass1(nchunks):
        for _ in range(nchunks):
            i = p1_idx[0]
            if i >= 96:
                return
            sl = i % 3
            if i >= 3:
                P.wait("sync", cV, p1_marks[sl])
            P.op("sync", "dma_start",
                 out=wstage[:, 1024 * sl: 1024 * (sl + 1)],
                 in_=pass1_srcs[i], inc=cL, inck=16)
            P.wait("vector", cL)
            P.op("vector", "reduce_sum", wacc[:, i: i + 1],
                 wstage[:, 1024 * sl: 1024 * (sl + 1)], axis=AX.X,
                 apply_absolute_value=True, inc=cV)
            p1_marks[sl] = cV.n
            p1_idx[0] += 1

    P.wait("gpsimd", cG, consts_G)
    for ic in range(ND):
        P.op("gpsimd", "dma_start", out=WTa[:, D * ic: D * (ic + 1)],
             in_=dwq[128 * ic: 128 * (ic + 1), :], inc=cG, inck=16)
    wq_loaded = cG.n
    for ic in range(ND):
        P.op("gpsimd", "dma_start", out=v65[:, D * ic: D * (ic + 1)],
             in_=dwk[128 * ic: 128 * (ic + 1), :], inc=cG, inck=16)
    wk_loaded = cG.n

    norm_marks = [0, 0]
    trans_marks = [0, 0]
    evacT_m = [0]

    def emit_ln_tile(t):
        sl = t % 2
        if t >= 2:
            P.wait("sync", cS, norm_marks[sl])
        P.op("sync", "dma_start", out=xring[:, D * sl: D * (sl + 1)],
             in_=dx[128 * t: 128 * (t + 1), :], inc=cL, inck=16)
        x_t = xring[:, D * sl: D * (sl + 1)]
        P.wait("vector", cL)
        P.op("vector", "reduce_sum", lncol(t, 0), x_t, axis=AX.X)
        P.op("vector", "scalar_tensor_tensor", r_bc[:, 0:D], x_t, 1.0, x_t,
             ALU.mult, ALU.mult, accum_out=lncol(t, 1))
        P.drain("vector")
        P.op("vector", "tensor_scalar_mul", lncol(t, 2), lncol(t, 0), 1.0 / D)
        P.drain("vector")
        P.op("vector", "tensor_mul", lncol(t, 3), lncol(t, 2), lncol(t, 2))
        P.drain("vector")
        P.op("vector", "scalar_tensor_tensor", lncol(t, 4), lncol(t, 1),
             1.0 / D, lncol(t, 3), ALU.mult, ALU.subtract)
        P.drain("vector")
        P.op("vector", "tensor_scalar_add", lncol(t, 5), lncol(t, 4), EPS_LN)
        P.drain("vector")
        P.op("vector", "reciprocal", lncol(t, 6), lncol(t, 5))
        P.drain("vector")
        P.op("vector", "sem_inc", cV.sem, 1)
        cV.n += 1
        P.wait("scalar", cV)
        P.op("scalar", "activation", lncol(t, 7), lncol(t, 6), AF.Sqrt)
        P.drain("scalar")
        P.op("scalar", "sem_inc", cS.sem, 1)
        cS.n += 1
        P.wait("vector", cS)
        P.op("vector", "scalar_tensor_tensor", lncol(t, 8), lncol(t, 2), -1.0,
             lncol(t, 7), ALU.mult, ALU.mult)
        P.drain("vector")
        P.op("vector", "sem_inc", cV.sem, 1)
        cV.n += 1
        P.wait("scalar", cV)
        if t >= 2:
            P.wait("scalar", cT, trans_marks[sl])
        P.op("scalar", "activation", xnring[:, D * sl: D * (sl + 1)], x_t,
             AF.Identity, scale=lncol(t, 7), bias=lncol(t, 8), inc=cS)
        norm_marks[sl] = cS.n
        P.wait("tensor", cS)
        if t >= 1:
            P.wait("tensor", cV, evacT_m[0])
        for j in range(ND):
            P.op("tensor", "transpose", psT[:, 128 * j: 128 * (j + 1)],
                 xnring[:, D * sl + 128 * j: D * sl + 128 * (j + 1)],
                 identb[:], inc=(cT if j == ND - 1 else None))
        trans_marks[sl] = cT.n
        P.wait("vector", cT)
        tsrc = psT[:, 0:1024].rearrange("p (c f) -> p c f", f=128)
        tdst = xn1T[:, :].rearrange("p (c f) -> p c f", f=S)[:, :, 128 * t: 128 * (t + 1)]
        P.op("vector", "tensor_copy", tdst, tsrc, inc=cV)
        evacT_m[0] = cV.n
        emit_pass1(4)

    def qk_state(which, w_loaded_mark, wt):
        return {"which": which, "w": w_loaded_mark, "wt": wt, "wt_free": 0,
                "evac": {}, "ev_idx": {}, "qh": {0: 0, 1: 0}, "cm": [0],
                "prev": None, "idx": 0,
                "tgt": qsT if which == "q" else ksT,
                "b8": bq8 if which == "q" else bk8,
                "nw8": nqw8 if which == "q" else nkw8,
                "wmax": wmaxq if which == "q" else wmaxk,
                "bound": sm16a if which == "q" else sm16b}

    def emit_qh2(st, oc2, tc2):
        P.wait("tensor", cS, st["evac"][(oc2, tc2)])
        P.wait("tensor", cV, st["cm"][0])
        emm(psC[0:16, :],
            masks[:, 16 * oc2: 16 * oc2 + 16],
            q2buf[:, 1024 * tc2: 1024 * (tc2 + 1)],
            start=True, stop=True, inc=cT)
        st["qh"][tc2] = cT.n
        P.wait("vector", cT, st["qh"][tc2])
        dstq = qh2s[:, 1024 * tc2: 1024 * (tc2 + 1)]
        if oc2 == 0:
            P.op("vector", "tensor_copy", dstq, psC[0:16, :], inc=cV)
        else:
            P.op("vector", "scalar_tensor_tensor", dstq, dstq, 1.0,
                 psC[0:16, :], ALU.mult, ALU.add, inc=cV)
        P.drain("vector")
        st["cm"][0] = cV.n

    def emit_qk_group(st, oc, tc, wait_cv=None):
        pp = psA if st["idx"] % 2 == 0 else psB
        if wait_cv is not None:
            P.wait("tensor", cV, wait_cv)
        P.wait("tensor", cG, st["w"])
        if st["idx"] - 2 in st["ev_idx"]:
            P.wait("tensor", cS, st["ev_idx"][st["idx"] - 2])
        for ic in range(ND):
            emm(pp[:, :],
                st["wt"][:, D * ic + 128 * oc: D * ic + 128 * (oc + 1)],
                xn1T[:, S * ic + 1024 * tc: S * ic + 1024 * (tc + 1)],
                start=(ic == 0), stop=(ic == ND - 1),
                inc=(cT if ic == ND - 1 else None))
        if (oc, tc) == (ND - 1, 1):
            st["wt_free"] = cT.n
        if st["prev"] is not None:
            emit_qh2(st, *st["prev"])
        P.wait("scalar", cT)
        P.op("scalar", "activation",
             st["tgt"][:, S * oc + 1024 * tc: S * oc + 1024 * (tc + 1)],
             pp[:, :], AF.Identity, bias=st["b8"][:, oc: oc + 1])
        P.op("scalar", "activation",
             q2buf[:, 1024 * tc: 1024 * (tc + 1)],
             pp[:, :], AF.Square, bias=st["b8"][:, oc: oc + 1], inc=cS)
        st["evac"][(oc, tc)] = st["ev_idx"][st["idx"]] = cS.n
        st["prev"] = (oc, tc)
        st["idx"] += 1

    def emit_qk_tailA(st, use_psC=False):
        emit_qh2(st, *st["prev"])
        qh2_done = st["qh2d"] = st["cm"][0]
        P.wait("tensor", cV, qh2_done)
        P.wait("tensor", cS)
        if use_psC:
            for r2 in range(2):
                if r2 == 1:
                    P.wait("tensor", cV)
                for j in range(2):
                    P.op("tensor", "matmul", psC[0:1, 512 * j: 512 * (j + 1)],
                         onesfc[:], qh2s[:, 1024 * r2 + 512 * j:
                                         1024 * r2 + 512 * (j + 1)],
                         start=True, stop=True, inc=(cT if j == 1 else None))
                P.wait("vector", cT)
                P.op("vector", "tensor_scalar_mul",
                     qhn[0:1, 1024 * r2: 1024 * (r2 + 1)], psC[0:1, :],
                     1.0 / D, inc=cV)
        else:
            for j in range(4):
                pq = psA if j < 2 else psB
                P.op("tensor", "matmul", pq[0:1, 512 * (j % 2): 512 * (j % 2) + 512],
                     onesfc[:], qh2s[:, 512 * j: 512 * (j + 1)],
                     start=True, stop=True, inc=(cT if j == 3 else None))
            P.wait("vector", cT)
            P.op("vector", "tensor_scalar_mul", qhn[0:1, 0:1024], psA[0:1, :], 1.0 / D)
            P.op("vector", "tensor_scalar_mul", qhn[0:1, 1024:2048], psB[0:1, :], 1.0 / D)
        P.drain("vector")
        P.op("vector", "tensor_scalar_add", r_bc[0:1, :], qhn[0:1, :], EPS_RMS)
        P.drain("vector")
        P.op("vector", "reciprocal", qhn[0:1, :], r_bc[0:1, :])
        P.drain("vector")
        P.op("vector", "sem_inc", cV.sem, 1)
        cV.n += 1
        P.wait("scalar", cV)
        P.op("scalar", "activation", qhn[0:1, :], qhn[0:1, :], AF.Sqrt,
             scale=(0.015625 if st["which"] == "q" else 1.0))
        P.drain("scalar")
        P.op("scalar", "sem_inc", cS.sem, 1)
        cS.n += 1
        P.wait("tensor", cS)
        if use_psC:
            for r2 in range(2):
                if r2 == 1:
                    P.wait("tensor", cV)
                for j in range(2):
                    P.op("tensor", "matmul", psC[:, 512 * j: 512 * (j + 1)],
                         onesf[:, 0:128],
                         qhn[0:1, 1024 * r2 + 512 * j: 1024 * r2 + 512 * (j + 1)],
                         start=True, stop=True, inc=(cT if j == 1 else None))
                P.wait("vector", cT)
                P.op("vector", "tensor_copy",
                     r_bc[:, 1024 * r2: 1024 * (r2 + 1)], psC[:, :], inc=cV)
        else:
            for j in range(4):
                pq = psA if j < 2 else psB
                P.op("tensor", "matmul", pq[:, 512 * (j % 2): 512 * (j % 2) + 512],
                     onesf[:, 0:128], qhn[0:1, 512 * j: 512 * (j + 1)],
                     start=True, stop=True, inc=(cT if j == 3 else None))
            P.wait("vector", cT)
            P.op("vector", "tensor_copy", r_bc[:, 0:1024], psA[:, :])
            P.op("vector", "tensor_copy", r_bc[:, 1024:2048], psB[:, :], inc=cV)
        st["rbc"] = cV.n
        P.wait("scalar", cV, st["qh2d"])
        P.wait("scalar", cT)
        P.op("scalar", "activation", qhn[:, :], qh2s[:, :], AF.Sqrt, inc=cS)
        P.wait("vector", cS)
        P.op("vector", "tensor_mul", qhn[:, :], qhn[:, :], r_bc[0:16, :])
        P.drain("vector")
        P.op("vector", "reduce_max", st["bound"][:], qhn[:, :], axis=AX.X)
        P.drain("vector")
        P.op("vector", "tensor_mul", st["bound"][:], st["bound"][:],
             st["wmax"][:])
        P.drain("vector")
        P.op("vector", "sem_inc", cV.sem, 1)
        cV.n += 1
        return st["wt_free"]

    def emit_qk_tailB(st):
        tgt, nw8 = st["tgt"], st["nw8"]
        for oc in range(ND):
            P.op("vector", "scalar_tensor_tensor",
                 tgt[:, S * oc: S * (oc + 1)],
                 tgt[:, S * oc: S * (oc + 1)],
                 nw8[:, oc: oc + 1], r_bc[:, :], ALU.mult, ALU.mult,
                 inc=(cV if oc == ND - 1 else None))
        return cV.n

    for t in range(8):
        emit_ln_tile(t)
    evac_t7 = evacT_m[0]
    stq = qk_state("q", wq_loaded, WTa)
    for g in range(ND):
        emit_qk_group(stq, g, 0, wait_cv=(evac_t7 if g == 0 else None))
        emit_ln_tile(8 + g)
    for g in range(ND):
        emit_qk_group(stq, g, 1, wait_cv=(evacT_m[0] if g == 0 else None))
    wta_free = emit_qk_tailA(stq)
    emit_pass1(16)
    P.wait("gpsimd", cT, wta_free)
    for ic in range(ND):
        P.op("gpsimd", "dma_start", out=WTa[:, D * ic: D * (ic + 1)],
             in_=dwv[128 * ic: 128 * (ic + 1), :], inc=cG, inck=16)
    wv_loaded = cG.n
    stk = qk_state("k", wk_loaded, v65)
    kv_wait = [stq["rbc"]]
    for tc in range(2):
        for g in range(ND):
            emit_qk_group(stk, g, tc,
                          wait_cv=(kv_wait[0] if (tc, g) == (0, 0) else None))
    q_scale_done = emit_qk_tailB(stq)
    v65_free = emit_qk_tailA(stk, use_psC=True)
    emit_pass1(16)

    P.wait("vector", cV, memset_V)
    P.wait("vector", cT, v65_free)
    v65r = v65[:, :].rearrange("p (t h j) -> p t h j", h=H, j=65)
    P.op("vector", "memset", v65r[:, :, :, 64:65], 1.0)
    P.drain("vector")
    P.op("vector", "sem_inc", cV.sem, 1)
    cV.n += 1
    vones_done = cV.n
    v_evac_marks = [0, 0]
    P.wait("tensor", cS, stk["ev_idx"][15])
    for t in range(NT):
        ps = psA if t % 2 == 0 else psB
        P.wait("tensor", cG, wv_loaded)
        if t >= 2:
            P.wait("tensor", cS, v_evac_marks[t % 2])
        for ic in range(ND):
            emm(ps[:, :],
                xn1T[:, S * ic + 128 * t: S * ic + 128 * (t + 1)],
                WTa[:, D * ic: D * (ic + 1)],
                start=(ic == 0), stop=False)
        emm(ps[:, :], onesb[:, 0:128], bvrow[:],
            start=False, stop=True, inc=cT)
        P.wait("scalar", cT)
        if t == 0:
            P.wait("scalar", cV, vones_done)
        dst = v65[:, 1040 * t: 1040 * (t + 1)].rearrange(
            "p (h j) -> p h j", j=65)[:, :, 0:64]
        src = ps[:, :].rearrange("p (h j) -> p h j", j=64)
        P.op("scalar", "activation", dst, src, AF.Identity, inc=cS)
        v_evac_marks[t % 2] = cS.n
        emit_pass1(2)
    emit_pass1(96)
    k_scale_done = emit_qk_tailB(stk)

    P.op("vector", "tensor_mul", sm16c[:], sm16a[:], sm16b[:])
    P.drain("vector")
    P.op("vector", "tensor_scalar_mul", sm16c[:], sm16c[:], -1.0)
    P.drain("vector")
    P.op("vector", "sem_inc", cV.sem, 1)
    cV.n += 1
    P.wait("tensor", cV)
    P.op("tensor", "transpose", psS[0:1, 0:16], sm16c[:],
         identf[0:16, 0:16], inc=cT)
    P.wait("vector", cT)
    P.op("vector", "tensor_copy", nmrow[:], psS[0:1, 0:16], inc=cV)
    P.wait("tensor", cV)
    P.op("tensor", "matmul", psS[:, 16:32], onesf[:, 0:128], nmrow[0:1, :],
         start=True, stop=True, inc=cT)
    P.wait("vector", cT)
    P.op("vector", "tensor_copy", negmb[:], psS[:, 16:32], inc=cV)


    tap("t_xn1T", xn1T[:, :], cV)
    tap("t_qsT", qsT[:, :], cV)
    tap("t_ksT", ksT[:, :], cV)
    tap("t_v65", v65[:, :], cS)
    tap("t_negmb", negmb[:, :], cV)
    tap("t_qh2s", qh2s[:, :], cV)
    barrier()
    free("wstage", "s_bvrow", "s_nkwhd", "s_nqwhd", "qhn", "lns", "qh2s", "r_bc",
         "q2buf", "xnring", "xring", "xn1T", "WTa")
    pzfree("psS", "psT")
    psD = pzalloc("psD", [128, 1024], F32)
    oT_evac_done = [0, 0]

    for wi, (tag, c0, c1, nelem, mcl_b, sw_b) in enumerate((
            ("g", 0, 32, D * INNER, mclg_b, swg_b),
            ("u", 32, 64, D * INNER, mclu_b, swu_b),
            ("d", 64, 96, INNER * D, mcld_b, swd_b))):
        P.op("vector", "reduce_sum", colsum2[:, wi: wi + 1],
             wacc[:, c0: c1], axis=AX.X)
        P.drain("vector")
        P.op("vector", "sem_inc", cV.sem, 1)
        cV.n += 1
        P.wait("tensor", cV)
        P.op("tensor", "transpose", psD[0:1, 32:160],
             colsum2[:, wi: wi + 1], identf[:], inc=cT)
        P.wait("vector", cT)
        P.op("vector", "tensor_copy", sw1[:], psD[0:1, 32:160], inc=cV)
        P.drain("vector")
        P.op("vector", "reduce_sum", sw2[0:1, wi: wi + 1], sw1[:], axis=AX.X)
        P.drain("vector")
        P.op("vector", "tensor_scalar_mul", sw2[0:1, wi + 4: wi + 5],
             sw2[0:1, wi: wi + 1], 1.0 / nelem)
        P.drain("vector")
        P.op("vector", "tensor_scalar_max", sw2[0:1, wi: wi + 1],
             sw2[0:1, wi + 4: wi + 5], 1e-5)
        P.drain("vector")
        P.op("vector", "sem_inc", cV.sem, 1)
        cV.n += 1
        P.wait("tensor", cV)
        P.op("tensor", "matmul", psD[:, 160:161], onesf[:, 0:128],
             sw2[0:1, wi: wi + 1], start=True, stop=True, inc=cT)
        P.wait("vector", cT)
        P.op("vector", "tensor_copy", mcl_b[:], psD[:, 160:161])
        P.drain("vector")
        P.op("vector", "reciprocal", sw_b[:], mcl_b[:])
        P.drain("vector")
        P.op("vector", "sem_inc", cV.sem, 1)
        cV.n += 1
    sw_done = cV.n
    P.op("vector", "tensor_mul", mclud_b[:], mclu_b[:], mcld_b[:])
    P.drain("vector")
    P.op("vector", "tensor_scalar_mul", mclud_b[:], mclud_b[:], 1.0 / 127.0)
    P.drain("vector")
    P.op("vector", "sem_inc", cV.sem, 1)
    cV.n += 1
    oT_evac_done[1] = cV.n

    p2list = []
    for tag, dr, dst, rows, cols, cw in (("g", dwg, dscr_g, D, INNER, 2048),
                                         ("u", dwu, dscr_u, D, INNER, 2048),
                                         ("d", dwd, dscr_d, INNER, D, 1024)):
        sw_b = {"g": swg_b, "u": swu_b, "d": swd_b}[tag]
        for i2 in range(rows // 128):
            for j2 in range(cols // cw):
                p2list.append((dr[128 * i2: 128 * (i2 + 1), cw * j2: cw * (j2 + 1)],
                               dst[:, cols * i2 + cw * j2: cols * i2 + cw * (j2 + 1)],
                               sw_b, cw))
    assert len(p2list) == 64
    p2_state = {"i": 0, "ld": [0, 0], "st": 0}

    def emit_p2(n):
        for _ in range(n):
            i2 = p2_state["i"]
            if i2 >= len(p2list):
                return
            srcp, dstp, sw_b, cw = p2list[i2]
            sl = i2 % 2
            if i2 >= 2:
                P.wait("sync", cV, p2_state["ld"][sl])
            P.op("sync", "dma_start",
                 out=wstg[:, 2048 * sl: 2048 * sl + cw], in_=srcp,
                 inc=cL, inck=16)
            P.wait("vector", cL)
            if i2 == 0:
                P.wait("vector", cV, sw_done)
            if i2 >= 1:
                P.wait("vector", cG, p2_state["st"])
            ws = wstg[:, 2048 * sl: 2048 * sl + cw]
            P.op("vector", "tensor_scalar", pbufg[:, 0:cw], ws, sw_b[:, 0:1],
                 0.5, ALU.mult, ALU.is_ge)
            P.op("vector", "tensor_scalar", mbufg[:, 0:cw], ws, sw_b[:, 0:1],
                 -0.5, ALU.mult, ALU.is_le)
            P.op("vector", "tensor_sub", pw8[:, 0:cw], pbufg[:, 0:cw],
                 mbufg[:, 0:cw], inc=cV)
            p2_state["ld"][sl] = cV.n
            P.wait("gpsimd", cV)
            P.op("gpsimd", "dma_start", out=dstp, in_=pw8[:, 0:cw],
                 inc=cG, inck=16)
            p2_state["st"] = cG.n
            p2_state["i"] += 1


    oT = alloc("oT", [128, ND * S], BF16, "left")
    pTr = alloc("pTr", [128, 2 * 1024], BF16, "right")
    wstg = alloc("wstg", [128, 2 * 2048], F32, "right")
    pbufg = alloc("pbufg", [128, 2048], BF16, "right")
    mbufg = alloc("mbufg", [128, 2048], BF16, "right")
    pw8 = alloc("pw8", [128, 2048], F8, "right")
    rdrow = alloc("rdrow", [1, 1024], F32, "right")
    rdrowb = alloc("rdrowb", [1, 1024], BF16, "right")
    rdb = alloc("rdb", [128, 1024], F32, "right")
    otmp = alloc("otmp", [64, 1024], BF16, "right")

    exp_free = [0, 0]
    score_free = [0, 0]
    pairs = [(h, qc) for h in range(H) for qc in range(2)]
    tail_prev = None

    def tail_recip(prev):
        h, qc, slot, av_mark = prev
        P.wait("vector", cT, av_mark)
        P.op("vector", "reciprocal", rdrow[:], oT_ps[slot][64:65, :])
        P.drain("vector")
        P.op("vector", "tensor_copy", rdrowb[:], rdrow[:])
        P.drain("vector")
        P.op("vector", "sem_inc", cV.sem, 1)
        cV.n += 1
        return cV.n

    def tail_bcast(prev, rd_ready):
        P.wait("tensor", cV, rd_ready)
        for j in range(2):
            P.op("tensor", "matmul", psB[:, 512 * j: 512 * (j + 1)],
                 onesb[:, 0:128], rdrowb[0:1, 512 * j: 512 * (j + 1)],
                 start=True, stop=True, inc=(cT if j == 1 else None))

    def tail_evac(prev):
        h, qc, slot, av_mark = prev
        hp = h % 2
        c = h // 2
        P.wait("vector", cT)
        P.op("vector", "tensor_copy", rdb[:, :], psB[:, :], inc=cV)
        rdb_done = cV.n
        ocols = slice(S * c + 1024 * qc, S * c + 1024 * (qc + 1))
        if hp == 0:
            P.op("vector", "tensor_mul", oT[0:64, ocols],
                 oT_ps[slot][0:64, :], rdb[0:64, :], inc=cV)
        else:
            P.op("vector", "tensor_mul", otmp[:, :],
                 oT_ps[slot][0:64, :], rdb[0:64, :], inc=cV)
            P.wait("gpsimd", cV)
            P.op("gpsimd", "dma_start", out=oT[64:128, ocols],
                 in_=otmp[:, :], inc=cG, inck=16)
        oT_evac_done[slot] = cV.n
        return rdb_done, cV.n

    oT_ps = [psC, psD]
    sc_mark = [0, 0]
    for i, (h, qc) in enumerate(pairs):
        hp = h % 2
        c = h // 2
        slot = i % 2
        rd_ready = tail_recip(tail_prev) if tail_prev is not None else None
        rdb_done = None

        def emit_av(kkm):
            sl2 = kkm % 2
            P.wait("tensor", cS, score_free[sl2])
            if kkm == 0:
                P.wait("tensor", cV, oT_evac_done[slot])
            emm(oT_ps[slot][0:65, :],
                v65[:, 1040 * kkm + 65 * h: 1040 * kkm + 65 * (h + 1)],
                pTr[:, 1024 * sl2: 1024 * sl2 + 1024],
                start=(kkm == 0), stop=(kkm == NT - 1), inc=cT)
            exp_free[sl2] = cT.n

        for kk in range(NT):
            sl = kk % 2
            ps = psA if sl == 0 else psB
            P.wait("tensor", cS, score_free[sl])
            if kk == 7 and rdb_done is not None:
                P.wait("tensor", cV, rdb_done)
            emm(ps[:, :],
                ksT[64 * hp:64 * hp + 64,
                    S * c + 128 * kk: S * c + 128 * (kk + 1)],
                qsT[64 * hp:64 * hp + 64,
                    S * c + 1024 * qc: S * c + 1024 * (qc + 1)],
                start=True, stop=True, inc=cT)
            sc_mark[sl] = cT.n
            P.wait("scalar", cT, sc_mark[sl])
            P.wait("scalar", cT, exp_free[sl])
            P.op("scalar", "activation",
                 pTr[:, 1024 * sl: 1024 * sl + 1024],
                 ps[:, :], AF.Exp, bias=negmb[:, h: h + 1], inc=cS)
            score_free[sl] = cS.n
            if kk >= 1:
                emit_av(kk - 1)
            if kk == 5 and tail_prev is not None:
                P.wait("tensor", cS, score_free[1])
                tail_bcast(tail_prev, rd_ready)
                rdb_done, _ = tail_evac(tail_prev)
        emit_av(NT - 1)
        av_mark = cT.n
        tail_prev = (h, qc, slot, av_mark)
        emit_p2(3)
    emit_p2(64)
    rd_ready = tail_recip(tail_prev)
    tail_bcast(tail_prev, rd_ready)
    tail_evac(tail_prev)
    P.wait("gpsimd", cG)
    tap("t_oT", oT[:, :], cV)
    barrier()
    pzfree("psD")
    psS = pzalloc("psS2", [128, 512], F32)
    psT = pzalloc("psT2", [128, 1024], BF16)
    free("otmp", "rdb", "rdrowb", "rdrow", "pw8", "mbufg", "pbufg", "wstg", "pTr")
    free("v65", "ksT", "qsT")

    x_mid = alloc("x_mid", [128, NT * D], F32, "right")
    tmbuf = alloc("tmbuf", [128, D], F32, "right")
    l2 = alloc("l2", [128, 20 * 16], F32, "right")

    def l2c(j):
        return l2[:, 16 * j: 16 * (j + 1)]

    WTo = alloc("WTo", [128, ND * D], BF16, "right")
    xr2 = alloc("xr2", [128, 2 * D], F32, "right")
    for ic in range(ND):
        P.op("gpsimd", "dma_start", out=WTo[:, D * ic: D * (ic + 1)],
             in_=dwo[128 * ic: 128 * (ic + 1), :], inc=cG, inck=16)
    wo_loaded = cG.n

    wo_evac = [0, 0]
    for t in range(NT):
        sl = t % 2
        if t >= 2:
            P.wait("sync", cV, wo_evac[sl])
        P.op("sync", "dma_start", out=xr2[:, D * sl: D * (sl + 1)],
             in_=dx[128 * t: 128 * (t + 1), :], inc=cL, inck=16)
        ps = psA if sl == 0 else psB
        P.wait("tensor", cG, wo_loaded)
        if t >= 2:
            P.wait("tensor", cV, wo_evac[sl])
        for ic in range(ND):
            emm(ps[:, :],
                oT[:, S * ic + 128 * t: S * ic + 128 * (t + 1)],
                WTo[:, D * ic: D * (ic + 1)],
                start=(ic == 0), stop=False)
        emm(ps[:, :], onesb[:, 0:128], borow[:],
            start=False, stop=True, inc=cT)
        P.wait("vector", cT)
        P.wait("vector", cL)
        P.op("vector", "scalar_tensor_tensor", x_mid[:, D * t: D * (t + 1)],
             ps[:, :], 1.0, xr2[:, D * sl: D * (sl + 1)], ALU.mult, ALU.add,
             inc=cV)
        wo_evac[sl] = cV.n
        x_t = x_mid[:, D * t: D * (t + 1)]
        P.op("vector", "reduce_sum", l2c(0)[:, t: t + 1], x_t, axis=AX.X)
        P.op("vector", "scalar_tensor_tensor", tmbuf[:], x_t, 1.0, x_t,
             ALU.mult, ALU.mult, accum_out=l2c(1)[:, t: t + 1])
        P.op("vector", "reduce_max", l2c(2)[:, t: t + 1], x_t, axis=AX.X)
        P.op("vector", "tensor_reduce", l2c(3)[:, t: t + 1], x_t, axis=AX.X,
             op=ALU.min)

    tap("t_xmid", x_mid[:, :], cV)
    barrier()
    free("xr2", "WTo")
    free("oT")

    w8g = alloc("w8g", [128, ND * INNER], F8, "left")
    w8u = alloc("w8u", [128, ND * INNER], F8, "left")
    w8d = alloc("w8d", [128, NI * D], F8, "left")

    P.op("sync", "dma_start", out=w8g[:, :], in_=dscr_g, inc=cL, inck=16)
    P.op("gpsimd", "dma_start", out=w8u[:, :], in_=dscr_u, inc=cG, inck=16)
    P.op("sync", "dma_start", out=w8d[:, :], in_=dscr_d, inc=cL, inck=16)
    w8_loads_L = cL.n
    w8_loads_G = cG.n
    P.wait("vector", cL, w8_loads_L)
    P.wait("vector", cG, w8_loads_G)
    P.op("vector", "sem_inc", cV.sem, 1)
    cV.n += 1
    w8_done = cV.n

    tap("t_w8g", w8g[:, :], cV)
    tap("t_w8d", w8d[:, :], cV)
    barrier()

    n1buf = alloc("n1buf", [128, D], I32, "right")
    r1buf = alloc("r1buf", [128, D], F32, "right")
    nabuf = alloc("nabuf", [128, D], BF16, "right")
    a1T = alloc("a1T", [128, D], BF16, "right")
    gsbuf = alloc("gsbuf", [128, 1024], F32, "right")
    hbuf = alloc("hbuf", [128, INNER], F32, "right")
    a2T = alloc("a2T", [128, 1024], BF16, "right")
    hm4 = alloc("hm4", [128, 16], F32, "right")

    P.drain("vector")
    P.op("vector", "tensor_scalar_mul", l2c(4), l2c(0), 1.0 / D)
    P.drain("vector")
    P.op("vector", "tensor_mul", l2c(5), l2c(4), l2c(4))
    P.drain("vector")
    P.op("vector", "scalar_tensor_tensor", l2c(6), l2c(1), 1.0 / D, l2c(5),
         ALU.mult, ALU.subtract)
    P.drain("vector")
    P.op("vector", "tensor_scalar_add", l2c(7), l2c(6), EPS_LN)
    P.drain("vector")
    P.op("vector", "reciprocal", l2c(8), l2c(7))
    P.drain("vector")
    P.op("vector", "sem_inc", cV.sem, 1)
    cV.n += 1
    P.wait("scalar", cV)
    P.op("scalar", "activation", l2c(9), l2c(8), AF.Sqrt)
    P.drain("scalar")
    P.op("scalar", "sem_inc", cS.sem, 1)
    cS.n += 1
    P.wait("vector", cS)
    P.op("vector", "scalar_tensor_tensor", l2c(10), l2c(4), -1.0, l2c(9),
         ALU.mult, ALU.mult)
    P.drain("vector")
    P.op("vector", "tensor_sub", l2c(11), l2c(2), l2c(4))
    P.op("vector", "tensor_sub", l2c(12), l2c(4), l2c(3))
    P.drain("vector")
    P.op("vector", "tensor_max", l2c(13), l2c(11), l2c(12))
    P.drain("vector")
    P.op("vector", "tensor_mul", l2c(14), l2c(9), l2c(13))
    P.drain("vector")
    P.op("vector", "tensor_scalar_max", l2c(15), l2c(14), 1e-5)
    P.drain("vector")
    P.op("vector", "reciprocal", l2c(16), l2c(15))
    P.drain("vector")
    P.op("vector", "tensor_scalar_mul", l2c(17), l2c(16), 127.0)
    P.drain("vector")
    P.op("vector", "tensor_mul", l2c(18), l2c(9), l2c(17))
    P.drain("vector")
    P.op("vector", "tensor_mul", l2c(19), l2c(10), l2c(17))
    P.drain("vector")
    P.op("vector", "reciprocal", l2c(11), l2c(17))
    P.drain("vector")
    P.op("vector", "tensor_scalar", l2c(12), l2c(11), mclg_b[:, 0:1], -1.0,
         ALU.mult, ALU.mult)
    P.drain("vector")
    P.op("vector", "sem_inc", cV.sem, 1)
    cV.n += 1
    ln2_done = cV.n

    down_marks = [0]
    pend_out = [None]

    def emit_out_evac():
        tt = pend_out[0]
        if tt is None:
            return 0
        P.wait("vector", cT, down_marks[0])
        P.op("vector", "scalar_tensor_tensor",
             x_mid[:, D * tt: D * (tt + 1)], psC[:, :],
             l2c(6)[:, tt: tt + 1], x_mid[:, D * tt: D * (tt + 1)],
             ALU.mult, ALU.add, inc=cV)
        P.wait("gpsimd", cV)
        P.op("gpsimd", "dma_start", out=dout[128 * tt: 128 * (tt + 1), :],
             in_=x_mid[:, D * tt: D * (tt + 1)], inc=cO, inck=16)
        pend_out[0] = None
        return cV.n

    a1T_free = [0, 0]
    gate_free = 0
    up_free = 0
    down_free = 0
    out_marks = [0, 0]
    na1_trans_mark = 0
    na2_trans_mark = 0
    scalar_tm_mark = 0
    for t in range(NT):
        sl = t % 2
        P.wait("scalar", cV, ln2_done)
        if t >= 1:
            P.wait("scalar", cV, scalar_tm_mark)
        P.op("scalar", "activation", tmbuf[:], x_mid[:, D * t: D * (t + 1)],
             AF.Identity, scale=l2c(18)[:, t: t + 1],
             bias=l2c(19)[:, t: t + 1], inc=cS)
        P.wait("vector", cS)
        P.op("vector", "tensor_scalar", n1buf[:], tmbuf[:], MAGIC, MAGIC,
             ALU.add, ALU.subtract)
        P.op("vector", "scalar_tensor_tensor", r1buf[:], tmbuf[:], 0.0,
             n1buf[:], ALU.add, ALU.subtract, inc=cV)
        scalar_tm_mark = cV.n
        P.op("vector", "tensor_scalar", r1buf[:], r1buf[:], MAGIC64, MAGIC64,
             ALU.add, ALU.subtract)
        if t >= 1:
            P.wait("vector", cT, na1_trans_mark)
        P.op("vector", "scalar_tensor_tensor", nabuf[:], r1buf[:], -1.0,
             n1buf[:], ALU.mult, ALU.subtract, inc=cV)
        if DEBUG_TAPS and t == 0:
            P.wait("gpsimd", cV)
            P.op("gpsimd", "dma_start", out=dtaps["t_a1"], in_=nabuf[:],
                 inc=cG, inck=16)
            P.op("gpsimd", "dma_start", out=dtaps["t_tm"], in_=tmbuf[:],
                 inc=cG, inck=16)
            P.op("gpsimd", "dma_start", out=dtaps["t_n1"], in_=n1buf[:],
                 inc=cG, inck=16)
            dbg_mark = cG.n
            P.wait("vector", cG, dbg_mark)
            P.wait("scalar", cG, dbg_mark)
            P.wait("tensor", cG, dbg_mark)
        down_free = emit_out_evac()
        P.wait("tensor", cV)
        if t >= 1:
            P.wait("tensor", cT, a1T_free[0])
            P.wait("tensor", cV, na2_trans_mark)
        for j in range(ND):
            P.op("tensor", "transpose", psT[:, 128 * j: 128 * (j + 1)],
                 nabuf[:, 128 * j: 128 * (j + 1)], identb[:],
                 inc=(cT if j == ND - 1 else None))
        na1_trans_mark = cT.n
        P.wait("vector", cT)
        P.op("vector", "tensor_copy", a1T[:, :], psT[:, :], inc=cV)
        a1t_ready = cV.n
        for ob in range(4):
            P.wait("tensor", cV, a1t_ready)
            if t == 0 and ob == 0:
                P.wait("tensor", cV, w8_done)
            P.wait("tensor", cS, gate_free)
            for ic in range(ND):
                emm(psA[:, :],
                    a1T[:, 128 * ic: 128 * (ic + 1)],
                    w8g[:, INNER * ic + 1024 * ob: INNER * ic + 1024 * (ob + 1)],
                    start=(ic == 0), stop=(ic == ND - 1),
                    inc=(cT if ic == ND - 1 else None))
            gate_mm = cT.n
            P.wait("tensor", cV, up_free)
            for ic in range(ND):
                emm(psB[:, :],
                    a1T[:, 128 * ic: 128 * (ic + 1)],
                    w8u[:, INNER * ic + 1024 * ob: INNER * ic + 1024 * (ob + 1)],
                    start=(ic == 0), stop=(ic == ND - 1),
                    inc=(cT if ic == ND - 1 else None))
            up_mm = cT.n
            P.wait("scalar", cT, gate_mm)
            P.op("scalar", "activation", gsbuf[:], psA[:, :], AF.Silu,
                 scale=l2c(12)[:, t: t + 1], inc=cS)
            gate_free = cS.n
            P.wait("vector", cS)
            P.wait("vector", cT, up_mm)
            P.op("vector", "tensor_mul", hbuf[:, 1024 * ob: 1024 * (ob + 1)],
                 gsbuf[:], psB[:, :])
            P.op("vector", "reduce_max", hm4[:, 4 * sl + ob: 4 * sl + ob + 1],
                 hbuf[:, 1024 * ob: 1024 * (ob + 1)], axis=AX.X,
                 apply_absolute_value=True, inc=cV)
            up_free = cV.n
        P.drain("vector")
        P.op("vector", "reduce_max", l2c(13)[:, t: t + 1],
             hm4[:, 4 * sl: 4 * sl + 4], axis=AX.X)
        P.drain("vector")
        P.op("vector", "tensor_scalar_max", l2c(14)[:, t: t + 1],
             l2c(13)[:, t: t + 1], 1e-5)
        P.drain("vector")
        P.op("vector", "reciprocal", l2c(15)[:, t: t + 1],
             l2c(14)[:, t: t + 1])
        P.drain("vector")
        P.op("vector", "tensor_scalar_mul", l2c(16)[:, t: t + 1],
             l2c(15)[:, t: t + 1], 127.0)
        P.drain("vector")
        P.op("vector", "sem_inc", cV.sem, 1)
        cV.n += 1
        s2_done = cV.n
        for qb in range(4):
            P.wait("scalar", cV, s2_done)
            P.wait("scalar", cV, scalar_tm_mark)
            P.op("scalar", "activation", tmbuf[:],
                 hbuf[:, 1024 * qb: 1024 * (qb + 1)], AF.Identity,
                 scale=l2c(16)[:, t: t + 1], bias=magic[:, 0:1], inc=cS)
            P.wait("vector", cS)
            P.op("vector", "tensor_scalar_sub", n1buf[:], tmbuf[:], MAGIC)
            P.op("vector", "scalar_tensor_tensor", r1buf[:], tmbuf[:], -MAGIC,
                 n1buf[:], ALU.add, ALU.subtract, inc=cV)
            scalar_tm_mark = cV.n
            P.op("vector", "tensor_scalar", r1buf[:], r1buf[:], MAGIC64,
                 MAGIC64, ALU.add, ALU.subtract)
            P.wait("vector", cT, na1_trans_mark)
            P.op("vector", "scalar_tensor_tensor", nabuf[:], r1buf[:], -1.0,
                 n1buf[:], ALU.mult, ALU.subtract, inc=cV)
            na2_ready = cV.n
            P.wait("tensor", cV, na2_ready)
            P.wait("tensor", cV, na2_trans_mark)
            for j in range(ND):
                P.op("tensor", "transpose", psT[:, 128 * j: 128 * (j + 1)],
                     nabuf[:, 128 * j: 128 * (j + 1)],
                     identb[:], inc=(cT if j == ND - 1 else None))
            P.wait("vector", cT)
            na2_trans_tmark = cT.n
            qbase = 0
            P.op("vector", "tensor_copy",
                 a2T[:, qbase: qbase + 1024], psT[:, :], inc=cV)
            na2_trans_mark = cV.n
            if DEBUG_TAPS and t == 0:
                P.wait("gpsimd", cV, na2_trans_mark)
                P.op("gpsimd", "dma_start",
                     out=dtaps["t_a2T"][:, 1024 * qb: 1024 * (qb + 1)],
                     in_=a2T[:, 0:1024], inc=cG, inck=16)
                P.wait("vector", cG)
            na1_trans_mark = na2_trans_tmark
            P.wait("tensor", cV, na2_trans_mark)
            if qb == 0:
                P.wait("tensor", cV, down_free)
            for j in range(ND):
                ic = 8 * qb + j
                emm(psC[:, :],
                    a2T[:, qbase + 128 * j: qbase + 128 * (j + 1)],
                    w8d[:, D * ic: D * (ic + 1)],
                    start=(ic == 0), stop=(ic == NI - 1),
                    inc=(cT if ic == NI - 1 else None))
        P.op("vector", "tensor_mul", l2c(5)[:, t: t + 1],
             l2c(11)[:, t: t + 1], l2c(14)[:, t: t + 1])
        P.drain("vector")
        P.op("vector", "tensor_single_scalar", l2c(6)[:, t: t + 1],
             l2c(5)[:, t: t + 1], mclud_b[:, 0:1], ALU.mult)
        P.drain("vector")
        if t == 0:
            tap("t_na1", nabuf[:, :], cV)
            tap("t_hbuf", hbuf[:, :], cV)
            tap("t_l2", l2[:, :], cV)
        a1T_free[0] = cT.n
        down_marks[0] = cT.n
        pend_out[0] = t
    emit_out_evac()

    P.wait("gpsimd", cO, cO.n)

    with nc.Block() as blk:
        @blk.sync
        def _(e):
            P.emit("sync", e)

        @blk.gpsimd
        def _(e):
            P.emit("gpsimd", e)

        @blk.tensor
        def _(e):
            P.emit("tensor", e)

        @blk.vector
        def _(e):
            P.emit("vector", e)

        @blk.scalar
        def _(e):
            P.emit("scalar", e)

    for name in list(live.keys())[::-1]:
        try:
            live.pop(name).__exit__(None, None, None)
        except Exception:
            pass
    for cm in pcm[::-1]:
        try:
            cm.__exit__(None, None, None)
        except Exception:
            pass
    for cm in semlist[::-1]:
        try:
            cm.__exit__(None, None, None)
        except Exception:
            pass
    return nc


_CACHED_NC = None


def kernel(L0_attn_x, L0_attn_Wq, L0_attn_bq, L0_attn_Wk, L0_attn_bk,
           L0_attn_Wv, L0_attn_bv, L0_attn_Wo, L0_attn_bo,
           L0_attn_norm_q_w, L0_attn_norm_k_w,
           L0_ffn_w_gate, L0_ffn_w_up, L0_ffn_w_down):
    global _CACHED_NC
    if _CACHED_NC is None:
        _CACHED_NC = build_program()
    nc = _CACHED_NC

    f32 = np.float32
    x = np.ascontiguousarray(np.asarray(L0_attn_x, dtype=f32))
    assert x.shape == (8, S, D)

    def c(a):
        return np.ascontiguousarray(np.asarray(a, dtype=f32))

    common = {
        "wqt": c(np.asarray(L0_attn_Wq, dtype=f32).T),
        "wkt": c(np.asarray(L0_attn_Wk, dtype=f32).T),
        "wvt": c(np.asarray(L0_attn_Wv, dtype=f32).T),
        "wot": c(np.asarray(L0_attn_Wo, dtype=f32).T),
        "bq8": c(np.asarray(L0_attn_bq, dtype=f32).reshape(ND, 128).T),
        "bk8": c(np.asarray(L0_attn_bk, dtype=f32).reshape(ND, 128).T),
        "bvrow": c(np.asarray(L0_attn_bv, dtype=f32).reshape(1, D)),
        "borow": c(np.asarray(L0_attn_bo, dtype=f32).reshape(1, D)),
        "nqw8": c(np.asarray(L0_attn_norm_q_w, dtype=f32).reshape(ND, 128).T),
        "nkw8": c(np.asarray(L0_attn_norm_k_w, dtype=f32).reshape(ND, 128).T),
        "nqwhd": c(np.asarray(L0_attn_norm_q_w, dtype=f32).reshape(H, HD)),
        "nkwhd": c(np.asarray(L0_attn_norm_k_w, dtype=f32).reshape(H, HD)),
        "wgT": c(np.asarray(L0_ffn_w_gate, dtype=f32).T),
        "wuT": c(np.asarray(L0_ffn_w_up, dtype=f32).T),
        "wdT": c(np.asarray(L0_ffn_w_down, dtype=f32).T),
        "identf": np.eye(128, dtype=f32),
    }
    in_maps = []
    for core in range(8):
        m = {"x": np.ascontiguousarray(x[core])}
        m.update(common)
        in_maps.append(m)

    res = run_bass_kernel_spmd(nc, in_maps, core_ids=list(range(8)))
    out = np.stack([np.asarray(r["out"], dtype=np.float32)
                    for r in res.results], axis=0)
    return out
